# version 10
# speedup vs baseline: 1.6444x; 1.3984x over previous
import numpy as np
import ml_dtypes
from contextlib import ExitStack

import concourse.bass as bass
import concourse.mybir as mybir
from concourse.bass_utils import run_bass_kernel_spmd

F32 = mybir.dt.float32
BF16 = mybir.dt.bfloat16
AF = mybir.ActivationFunctionType
ALU = mybir.AluOpType
AX = mybir.AxisListType
bf16 = ml_dtypes.bfloat16

def _psum_r(nc, name, shape, dtype):
    return nc.psum_tensor(name, shape, dtype, unsafe_allow_realloc=True)


B, S, D = 4, 4096, 1024
HALF = S // 2
NTOW = HALF // 128
NTF = S // 128
NDC = D // 128
NCORES = 8
NEG = -1e9


class Tile:
    __slots__ = ("h", "last_w", "readers", "name", "subs")
    def __init__(self, h, name=""):
        self.h = h
        self.name = name
        self.last_w = None
        self.readers = []
        self.subs = None
    def sub(self, key):
        if self.subs is None:
            self.subs = {}
        t = self.subs.get(key)
        if t is None:
            t = Tile(self.h, f"{self.name}.{key}")
            self.subs[key] = t
        return t
    def __getitem__(self, sl):
        return self.h[sl]
    @property
    def ap(self):
        return self.h[:]

class Kq:
    __slots__ = ("name", "attr", "step", "sem", "n", "is_dma")
    def __init__(self, name, attr, step):
        self.name, self.attr, self.step = name, attr, step
        self.sem = None
        self.n = 0
        self.is_dma = step == 16

class KB:

    def __init__(self, nc, ctx):
        self.nc = nc
        self.ctx = ctx
        self.block = ctx.enter_context(nc.Block())
        self.q = {}
        for name, attr, step in [
            ("pe", "tensor", 1), ("act", "scalar", 1), ("dve", "vector", 1),
            ("pool", "gpsimd", 1), ("sd", "sync", 16), ("gd", "gpsimd", 16),
            ("cc0", "gpsimd", 1), ("cc1", "gpsimd", 1),
            ("cc2", "gpsimd", 1), ("cc3", "gpsimd", 1),
            ("cb0", "gpsimd", 1), ("cb1", "gpsimd", 1),
            ("cb2", "gpsimd", 1), ("cb3", "gpsimd", 1),
        ]:
            kq = Kq(name, attr, step)
            kq.sem = ctx.enter_context(nc.semaphore("sem_" + name))
            self.q[name] = kq
        self.satisfied = {n: {} for n in self.q}
        self.maxwait = {n: 0 for n in self.q}
        self._names = {}
        self.pending = {}

    _cur = None

    def uname(self, base):
        i = self._names.get(base, 0)
        self._names[base] = i + 1
        return f"{base}_{i}" if i else base

    def sb(self, name, shape, dtype=F32, ctx=None):
        h = (ctx or self._cur or self.ctx).enter_context(
            self.nc.sbuf_tensor(self.uname(name), list(shape), dtype,
                                unsafe_allow_realloc=True))
        return Tile(h, name)

    def psum(self, name, shape, dtype=F32):
        h = self.ctx.enter_context(
            self.nc.psum_tensor(self.uname(name), list(shape), dtype,
                                unsafe_allow_realloc=True))
        return Tile(h, name)

    def _wait(self, eng, cons_q, tag):
        if tag is None:
            return
        pq, pn = tag
        if pn is None:
            raise RuntimeError("wait on unresolved pending tag")
        if pq.name == cons_q.name and pq.name == "pe":
            return
        sat = self.satisfied[cons_q.name]
        if sat.get(pq.name, 0) >= pn:
            return
        eng.wait_ge(pq.sem, pn * pq.step)
        sat[pq.name] = pn
        self.maxwait[pq.name] = max(self.maxwait[pq.name], pn)

    @staticmethod
    def _freeze(fn):
        import types
        if fn.__closure__:
            cells = tuple(types.CellType(c.cell_contents)
                          for c in fn.__closure__)
            return types.FunctionType(fn.__code__, fn.__globals__,
                                      fn.__name__, fn.__defaults__, cells)
        return fn

    def op(self, qname, emit, reads=(), writes=(), inc=True):
        emit = self._freeze(emit)
        q = self.q[qname]
        tags = []
        if q.is_dma and q.n > 0:
            tags.append((q, q.n))
        for t in reads:
            tags.append(t.last_w)
        for t in writes:
            tags.append(t.last_w)
            tags.extend(t.readers)
        holder = {}

        def body(eng):
            for tag in tags:
                self._wait(eng, q, tag)
            instr = emit(eng)
            if inc:
                instr.then_inc(q.sem, q.step)
            holder["i"] = instr

        self.pending.setdefault(q.attr, []).append(body)
        if inc:
            q.n += 1
            tag = (q, q.n)
        else:
            tag = (q, q.n + 1)
        for t in reads:
            t.readers.append(tag)
        for t in writes:
            t.last_w = tag
            t.readers = []
        return tag

    def flush(self):
        for attr, fns in self.pending.items():
            if not fns:
                continue
            def body(eng, fns=tuple(fns)):
                for fn in fns:
                    fn(eng)
            getattr(self.block, attr)(body)
            fns.clear()

    def dma(self, qname, out, in_, reads=(), writes=()):
        return self.op(qname, lambda e: e.dma_start(
            out=out, in_=in_), reads=reads, writes=writes)

    def raw(self, qname, fn):
        self.flush()
        q = self.q[qname]
        getattr(self.block, q.attr)(fn)

    def barrier_wait(self, qname, tags):
        q = self.q[qname]
        def body(eng):
            for tag in tags:
                self._wait(eng, q, tag)
        getattr(self.block, q.attr)(body)

    def phase_barrier(self):
        self.flush()
        counts = {n: q.n for n, q in self.q.items()}
        for cname, cq in self.q.items():
            def body(eng, cname=cname):
                for pname, pn in counts.items():
                    if pname == cname or pn == 0:
                        continue
                    self._wait(eng, self.q[cname], (self.q[pname], pn))
            getattr(self.block, cq.attr)(body)

    def finish(self, final_store_queue="gd"):
        self.flush()
        q = self.q[final_store_queue]
        n = q.n
        def body(eng):
            eng.wait_ge(q.sem, n * q.step)
        getattr(self.block, q.attr)(body)
        for name, kq in self.q.items():
            assert self.maxwait[name] <= kq.n, (
                f"wait beyond final count on {name}: "
                f"{self.maxwait[name]} > {kq.n}")



def alibi_slopes(n_heads=16):
    import math
    start = 2.0 ** (-2.0 ** (-(math.log2(n_heads) - 3)))
    return np.array([start * start ** i for i in range(n_heads)], np.float32)

def own_rows(hf):
    return (np.arange(S).reshape(NTF, 128)[hf::2]).reshape(-1)

def make_consts():
    c = {}
    c["ident"] = np.eye(128, dtype=bf16)
    c["identf32"] = np.eye(128, dtype=np.float32)
    kp = np.arange(128)[:, None]
    qp = np.arange(128)[None, :]
    c["tri_causal"] = np.where(kp > qp, NEG, 0.0).astype(np.float32)
    c["full_neg"] = np.full((128, 128), NEG, np.float32)
    c["zeros128"] = np.zeros((128, 128), np.float32)
    sl = alibi_slopes()
    ab = np.zeros((128, 16 * 32), np.float32)
    for h in range(16):
        for s in range(32):
            ab[:, h * 32 + s] = sl[h] * (128 * s + np.arange(128))
    c["alibik"] = ab
    for hf in range(2):
        rows = own_rows(hf).astype(np.float32)
        npq = np.zeros((1, 16 * HALF), np.float32)
        for h in range(16):
            npq[0, h * HALF:(h + 1) * HALF] = -sl[h] * rows
        c[f"negposq_{hf}"] = npq
    return c

CONSTS = make_consts()
SQRT_SCALE = float(1.0 / np.sqrt(np.sqrt(64.0)))



def emit_ln_tile(kb, xt, xn_out_ap, xn_out_tile, scratch, extra_rms=None,
                 skip_plain=False):
    s1, s2, m, v, r = (scratch[k] for k in ("s1", "s2", "m", "v", "r"))
    dummy = scratch["dummy"]
    kb.op("act", lambda e: e.activation(dummy.ap, xt.ap, AF.Identity,
                                        accum_out=s1.ap),
          reads=[xt], writes=[dummy, s1])
    kb.op("act", lambda e: e.activation(dummy.ap, xt.ap, AF.Square,
                                        accum_out=s2.ap),
          reads=[xt], writes=[dummy, s2])
    kb.op("dve", lambda e: e.tensor_scalar(m.ap, s1.ap, 1.0 / D, None,
                                           op0=ALU.mult),
          reads=[s1], writes=[m])
    kb.op("dve", lambda e: e.tensor_scalar(v.ap, s2.ap, 1.0 / D, None,
                                           op0=ALU.mult),
          reads=[s2], writes=[v])
    kb.op("dve", lambda e: e.scalar_tensor_tensor(v.ap, m.ap, m.ap, v.ap,
                                                  op0=ALU.mult,
                                                  op1=ALU.subtract),
          reads=[m, v], writes=[v])
    kb.op("dve", lambda e: e.tensor_scalar(v.ap, v.ap, -1.0, 1e-5,
                                           op0=ALU.mult, op1=ALU.add),
          reads=[v], writes=[v])
    kb.op("act", lambda e: e.activation(r.ap, v.ap, AF.Ln),
          reads=[v], writes=[r])
    kb.op("act", lambda e: e.activation(r.ap, r.ap, AF.Exp, scale=-0.5),
          reads=[r], writes=[r])
    if not skip_plain:
        kb.op("dve", lambda e: e.tensor_scalar(xn_out_ap, xt.ap, m.ap, r.ap,
                                               op0=ALU.subtract,
                                               op1=ALU.mult),
              reads=[xt, m, r], writes=[xn_out_tile])
    if extra_rms is not None:
        rms_ap, rms_tile, rms32_ap, rms32_tile = extra_rms
        ms2, r2 = scratch["ms2"], scratch["r2"]
        kb.op("dve", lambda e: e.tensor_scalar(ms2.ap, v.ap, -1e-5, None,
                                               op0=ALU.add),
              reads=[v], writes=[ms2])
        kb.op("dve", lambda e: e.reciprocal(r2.ap, v.ap),
              reads=[v], writes=[r2])
        kb.op("dve", lambda e: e.tensor_scalar(ms2.ap, ms2.ap, r2.ap, 1e-6,
                                               op0=ALU.mult, op1=ALU.add),
              reads=[ms2, r2], writes=[ms2])
        kb.op("act", lambda e: e.activation(ms2.ap, ms2.ap, AF.Ln),
              reads=[ms2], writes=[ms2])
        kb.op("act", lambda e: e.activation(ms2.ap, ms2.ap, AF.Exp,
                                            scale=-0.5),
              reads=[ms2], writes=[ms2])
        kb.op("dve", lambda e: e.tensor_scalar(r2.ap, r.ap, ms2.ap, None,
                                               op0=ALU.mult),
              reads=[r, ms2], writes=[r2])
        kb.op("dve", lambda e: e.tensor_scalar(rms_ap, xt.ap, m.ap, r2.ap,
                                               op0=ALU.subtract,
                                               op1=ALU.mult),
              reads=[xt, m, r2], writes=[rms_tile])
        if rms32_ap is not None:
            kb.op("dve", lambda e: e.tensor_scalar(rms32_ap, xt.ap, m.ap,
                                                   r2.ap, op0=ALU.subtract,
                                                   op1=ALU.mult),
                  reads=[xt, m, r2], writes=[rms32_tile])


def make_ln_scratch(kb, tag, dummy=None, extra=False, ctx=None):
    sc = {k: kb.sb(f"lns_{tag}_{k}", (128, 1), ctx=ctx) for k in
          ("s1", "s2", "m", "v", "r")}
    sc["dummy"] = dummy if dummy is not None else kb.sb(
        f"lns_{tag}_dummy", (128, 1024), ctx=ctx)
    if extra:
        sc["ms2"] = kb.sb(f"lns_{tag}_ms2", (128, 1), ctx=ctx)
        sc["r2"] = kb.sb(f"lns_{tag}_r2", (128, 1), ctx=ctx)
    return sc


def emit_transpose_tile(kb, src_ap, src_tile, dst_ap, dst_tile, ident, pst):
    kb.op("pe", lambda e: e.transpose(pst.h[0:128, 0:128], src_ap, ident.ap),
          reads=[src_tile, ident], writes=[pst])
    kb.op("dve", lambda e: e.tensor_copy(dst_ap, pst.h[0:128, 0:128]),
          reads=[pst], writes=[dst_tile])



def build_l0a(debug=False):
    nc = bass.Bass("TRN2", target_bir_lowering=False)
    xown = nc.dram_tensor("xown", [HALF, D], F32, kind="ExternalInput").ap()
    xfull = nc.dram_tensor("xfull", [S, D], F32, kind="ExternalInput").ap()
    wqkvT = nc.dram_tensor("wqkvT", [D, 3 * D], F32, kind="ExternalInput").ap()
    wprojT = nc.dram_tensor("wprojT", [D, D], F32, kind="ExternalInput").ap()
    identD = nc.dram_tensor("ident", [128, 128], BF16, kind="ExternalInput").ap()
    maskAD = nc.dram_tensor("maskA", [128, 128], F32, kind="ExternalInput").ap()
    maskBD = nc.dram_tensor("maskB", [128, 128], F32, kind="ExternalInput").ap()
    alibikD = nc.dram_tensor("alibik", [128, 512], F32, kind="ExternalInput").ap()
    negposqD = nc.dram_tensor("negposq", [1, 16 * HALF], F32,
                              kind="ExternalInput").ap()
    xoutD = nc.dram_tensor("xout", [HALF, D], F32, kind="ExternalOutput").ap()
    dbg = {}
    if debug:
        dbg["pt0"] = nc.dram_tensor("dbg_pt0", [128, 1536], F32,
                                    kind="ExternalOutput").ap()
        dbg["pt1"] = nc.dram_tensor("dbg_pt1", [128, 1536], F32,
                                    kind="ExternalOutput").ap()
        dbg["po0"] = nc.dram_tensor("dbg_po0", [128, 512], F32,
                                    kind="ExternalOutput").ap()
        dbg["oa0"] = nc.dram_tensor("dbg_oa0", [128, HALF], F32,
                                    kind="ExternalOutput").ap()
        dbg["qa0"] = nc.dram_tensor("dbg_qa0", [65, HALF], F32,
                                    kind="ExternalOutput").ap()
        dbg["ks0"] = nc.dram_tensor("dbg_ks0", [65, S], F32,
                                    kind="ExternalOutput").ap()
        dbg["kmaxn0"] = nc.dram_tensor("dbg_kmaxn0", [1, 1], F32,
                                       kind="ExternalOutput").ap()
        dbg["kall0"] = nc.dram_tensor("dbg_kall0", [128, S], F32,
                                      kind="ExternalOutput").ap()
        dbg["vall0"] = nc.dram_tensor("dbg_vall0", [128, 512], F32,
                                      kind="ExternalOutput").ap()
        dbg["xnow0"] = nc.dram_tensor("dbg_xnow0", [128, HALF], F32,
                                      kind="ExternalOutput").ap()

    with ExitStack() as ctx:
        kb = KB(nc, ctx)
        emit_l0a_body(kb, ctx, xown, xfull, wqkvT, wprojT, identD, maskAD,
                      maskBD, alibikD, negposqD, xoutD, dbg)
        kb.finish("gd")
    return nc


def emit_l0a_body(kb, ctx, xown, xfull, wqkvT, wprojT, identD, maskAD,
                  maskBD, alibikD, negposqD, xoutD, dbg=None, mode="l0a",
                  xmap=None):
    nc = kb.nc

    def ldxf(xt, g):
        if xmap is None:
            kb.dma("sd", xt.ap, xfull[g * 128:(g + 1) * 128, :],
                   writes=[xt])
        else:
            off = (g % 2) * HALF + (g // 2) * 128
            kb.dma("gd", xt.ap, xfull[off:off + 128, :], writes=[xt])
    ident = kb.sb("sb_ident", (128, 128), BF16)
    maskA = kb.sb("sb_maskA", (128, 128))
    maskB = kb.sb("sb_maskB", (128, 128))
    alibik = kb.sb("sb_alibik", (128, 512))
    ones64 = kb.sb("ones64", (64, 1), BF16)
    ones1x64 = kb.sb("ones1x64", (1, 64), BF16)
    m50 = kb.sb("m50", (128, 1))
    zz = kb.sb("zzb", (128, 1))
    dummy = kb.sb("lns_dummy", (128, 1024))
    kb.dma("sd", ident.ap, identD, writes=[ident])
    kb.dma("sd", maskA.ap, maskAD, writes=[maskA])
    kb.dma("sd", maskB.ap, maskBD, writes=[maskB])
    kb.dma("sd", alibik.ap, alibikD, writes=[alibik])
    kb.op("dve", lambda e: e.memset(ones64.ap, 1.0), writes=[ones64])
    kb.op("dve", lambda e: e.memset(ones1x64.ap, 1.0), writes=[ones1x64])
    kb.op("dve", lambda e: e.memset(m50.ap, -50.0), writes=[m50])
    kb.op("dve", lambda e: e.memset(zz.ap, 0.0), writes=[zz])

    OA = [kb.sb(f"oa{c}", (128, HALF), BF16) for c in range(NDC)]
    QF = [kb.sb(f"qf{c}", (128, HALF), BF16) for c in range(NDC)]

    with ExitStack() as ph:
        E = ph.enter_context
        pst = [Tile(E(_psum_r(nc, kb.uname("pst"), [128, 1024], BF16)))
               for _ in range(2)]
        psQ = [Tile(E(_psum_r(nc, kb.uname("psQ"), [128, 512], F32)))
               for _ in range(2)]
        xo = [kb.sb(f"xo{i}", (128, D), ctx=ph) for i in range(2)]
        xnt = [kb.sb(f"xnt{i}", (128, D), BF16, ctx=ph) for i in range(2)]
        xnow_fm = [kb.sb(f"xnowfm{c}", (128, HALF), BF16, ctx=ph)
                   for c in range(NDC)]
        wqs = kb.sb("wqs", (128, NDC * 1024), BF16, ctx=ph)
        lns = make_ln_scratch(kb, "own", dummy, ctx=ph)
        for c in range(NDC):
            kb.dma("gd", wqs.h[0:128, c * 1024:(c + 1) * 1024],
                   wqkvT[c * 128:(c + 1) * 128, 0:D], writes=[wqs])
        for j in range(NTOW):
            xt = xo[j % 2]
            kb.dma("sd", xt.ap, xown[j * 128:(j + 1) * 128, :], writes=[xt])
            xn = xnt[j % 2]
            emit_ln_tile(kb, xt, xn.ap, xn, lns)
            for c in range(NDC):
                emit_transpose_tile(
                    kb, xn.h[0:128, c * 128:(c + 1) * 128], xn,
                    xnow_fm[c].h[0:128, j * 128:(j + 1) * 128], xnow_fm[c],
                    ident, pst[(j * NDC + c) % 2])
        if dbg:
            kb.dma("gd", dbg["xnow0"], xnow_fm[0].ap, reads=[xnow_fm[0]])
        for tq in range(4):
            for dq in range(NDC):
                pq = psQ[dq % 2]
                for c in range(NDC):
                    kb.op("pe", lambda e, c=c, dq=dq, tq=tq, pq=pq: e.matmul(
                        pq.h[0:128, 0:512],
                        wqs.h[0:128, c * 1024 + dq * 128:
                              c * 1024 + dq * 128 + 128],
                        xnow_fm[c].h[0:128, tq * 512:(tq + 1) * 512],
                        start=(c == 0), stop=(c == NDC - 1)),
                        reads=[wqs, xnow_fm[c]], writes=[pq],
                        inc=(c == NDC - 1))
                kb.op("act", lambda e, pq=pq, dq=dq, tq=tq: e.activation(
                    QF[dq].h[0:128, tq * 512:(tq + 1) * 512],
                    pq.h[0:128, 0:512], AF.Identity, scale=SQRT_SCALE),
                    reads=[pq], writes=[QF[dq]])
        kb.phase_barrier()

    for hg in range(2):
        with ExitStack() as ph:
            E = ph.enter_context
            K_all = [kb.sb(f"kall{i}", (128, S), BF16, ctx=ph)
                     for i in range(4)]
            V_all = [kb.sb(f"vall{g}", (128, 512), BF16, ctx=ph)
                     for g in range(NTF)]
            with ExitStack() as ph2:
                E2 = ph2.enter_context
                pst = [Tile(E2(_psum_r(nc, kb.uname("pst"), [128, 1024],
                                              BF16))) for _ in range(2)]
                psK = [Tile(E2(_psum_r(nc, kb.uname("psK"), [128, 512],
                                              F32))) for _ in range(2)]
                psV = [Tile(E2(_psum_r(nc, kb.uname("psV"), [128, 512],
                                              F32))) for _ in range(2)]
                xf = [kb.sb(f"xf{i}", (128, D), ctx=ph2) for i in range(2)]
                xnt = [kb.sb(f"xnt{i}", (128, D), BF16, ctx=ph2)
                       for i in range(2)]
                xnf = [[kb.sb(f"xnf_{c}", (128, 512), BF16, ctx=ph2)
                        for c in range(NDC)]]
                wk = kb.sb("wk", (128, NDC * 512), BF16, ctx=ph2)
                wv = kb.sb("wv", (128, NDC * 512), BF16, ctx=ph2)
                lns = make_ln_scratch(kb, "full", dummy, ctx=ph2)
                for c in range(NDC):
                    kb.dma("gd", wk.h[0:128, c * 512:(c + 1) * 512],
                           wqkvT[c * 128:(c + 1) * 128,
                                 D + hg * 512:D + hg * 512 + 512],
                           writes=[wk])
                    kb.dma("gd", wv.h[0:128, c * 512:(c + 1) * 512],
                           wqkvT[c * 128:(c + 1) * 128,
                                 2 * D + hg * 512:2 * D + hg * 512 + 512],
                           writes=[wv])
                for kg in range(8):
                    fmb = xnf[0]
                    for t in range(4):
                        g = kg * 4 + t
                        xt = xf[g % 2]
                        ldxf(xt, g)
                        xn = xnt[g % 2]
                        emit_ln_tile(kb, xt, xn.ap, xn, lns)
                        for c in range(NDC):
                            emit_transpose_tile(
                                kb, xn.h[0:128, c * 128:(c + 1) * 128], xn,
                                fmb[c].h[0:128, t * 128:(t + 1) * 128],
                                fmb[c], ident, pst[(g * NDC + c) % 2])
                    for oc in range(4):
                        pk = psK[oc % 2]
                        for c in range(NDC):
                            kb.op("pe", lambda e, c=c, oc=oc, pk=pk, fmb=fmb:
                                  e.matmul(pk.h[0:128, 0:512],
                                           wk.h[0:128, c * 512 + oc * 128:
                                                c * 512 + oc * 128 + 128],
                                           fmb[c].h[0:128, 0:512],
                                           start=(c == 0),
                                           stop=(c == NDC - 1)),
                                  reads=[wk, fmb[c]], writes=[pk],
                                  inc=(c == NDC - 1))
                        kb.op("act", lambda e, pk=pk, oc=oc, kg=kg:
                              e.activation(
                                  K_all[oc].h[0:128, kg * 512:(kg + 1) * 512],
                                  pk.h[0:128, 0:512],
                                  AF.Identity, scale=SQRT_SCALE),
                              reads=[pk], writes=[K_all[oc]])
                    for t in range(4):
                        g = kg * 4 + t
                        pv = psV[t % 2]
                        for c in range(NDC):
                            kb.op("pe", lambda e, c=c, t=t, pv=pv, fmb=fmb:
                                  e.matmul(pv.h[0:128, 0:512],
                                           fmb[c].h[0:128,
                                                    t * 128:(t + 1) * 128],
                                           wv.h[0:128, c * 512:(c + 1) * 512],
                                           start=(c == 0),
                                           stop=(c == NDC - 1)),
                                  reads=[wv, fmb[c]], writes=[pv],
                                  inc=(c == NDC - 1))
                        kb.op("dve", lambda e, pv=pv, g=g: e.tensor_copy(
                            V_all[g].ap, pv.h[0:128, 0:512]),
                            reads=[pv], writes=[V_all[g]])
                if dbg and hg == 0:
                    kb.dma("gd", dbg["kall0"], K_all[0].ap, reads=[K_all[0]])
                    kb.dma("gd", dbg["vall0"], V_all[0].ap, reads=[V_all[0]])
                kb.phase_barrier()

            with ExitStack() as ph3:
                E3 = ph3.enter_context
                psS = [Tile(E3(_psum_r(nc, kb.uname("psS"),
                                              [128, 3 * 512], F32)))
                       for _ in range(2)]
                psO = [Tile(E3(_psum_r(nc, kb.uname("psO"), [128, 512],
                                              F32))) for _ in range(2)]
                QS = [kb.sb(f"qs{i}", (65, HALF), BF16, ctx=ph3)
                      for i in range(2)]
                KS = [kb.sb(f"ksx{i}", (65, S), BF16, ctx=ph3)
                      for i in range(2)]
                VS = [kb.sb(f"vsx{i}", (128, NTF * 65), BF16, ctx=ph3)
                      for i in range(2)]
                kmaxn = kb.sb("kmaxn", (1, 1), ctx=ph3)
                ksq = kb.sb("ksq", (64, S), BF16, ctx=ph3)
                qsq = kb.sb("qsq", (64, 512), BF16, ctx=ph3)
                mrow = kb.sb("mrow", (1, 512), ctx=ph3)
                PT = [kb.sb(f"pt{i}", (128, 3 * 512), BF16, ctx=ph3)
                      for i in range(2)]
                linv = kb.sb("linv", (1, 512), ctx=ph3)
                linb = kb.sb("linb", (1, 512), BF16, ctx=ph3)
                osb = kb.sb("osb", (64, 512), BF16, ctx=ph3)
                kred = kb.sb("kred", (1, 1), ctx=ph3)
                npq = [kb.sb(f"npq{i}", (1, HALF), ctx=ph3)
                       for i in range(2)]
                for hh in range(8):
                    h = hg * 8 + hh
                    ks = KS[hh % 2]
                    vs = VS[hh % 2]
                    qa = QS[hh % 2]
                    np_h = npq[hh % 2]
                    kb.dma("sd", np_h.ap,
                           negposqD[0:1, h * HALF:(h + 1) * HALF],
                           writes=[np_h])
                    kb.op("dve", lambda e, ks=ks, hh=hh: e.tensor_copy(
                        ks.h[0:64, 0:S],
                        K_all[hh // 2].h[(hh % 2) * 64:(hh % 2) * 64 + 64,
                                         0:S]),
                        reads=[K_all[hh // 2]], writes=[ks])
                    kb.op("dve", lambda e, ks=ks: e.memset(
                        ks.h[64:65, 0:S], 1.0), writes=[ks])
                    kb.op("dve", lambda e, vs=vs: e.memset(vs.ap, 1.0),
                          writes=[vs])
                    for g in range(NTF):
                        kb.op("dve", lambda e, vs=vs, g=g, hh=hh:
                              e.tensor_copy(
                                  vs.h[0:128, g * 65:g * 65 + 64],
                                  V_all[g].h[0:128, hh * 64:hh * 64 + 64]),
                              reads=[V_all[g]], writes=[vs])
                    c, r0 = h // 2, (h % 2) * 64
                    kb.op("dve", lambda e, qa=qa, c=c, r0=r0:
                          e.tensor_copy(
                              qa.h[0:64, 0:HALF],
                              QF[c].h[r0:r0 + 64, 0:HALF]),
                          reads=[QF[c]], writes=[qa])
                    kb.op("dve", lambda e, ks=ks: e.tensor_tensor(
                        ksq.ap, ks.h[0:64, 0:S], ks.h[0:64, 0:S],
                        op=ALU.mult), reads=[ks], writes=[ksq])
                    for kc in range(8):
                        psn = psS[0].sub(0)
                        kb.op("pe", lambda e, kc=kc, psn=psn: e.matmul(
                            psn.h[0:1, 0:512], ones64.ap,
                            ksq.h[0:64, kc * 512:(kc + 1) * 512],
                            start=True, stop=True),
                            reads=[ones64, ksq], writes=[psn])
                        if kc == 0:
                            kb.op("dve", lambda e, psn=psn: e.tensor_reduce(
                                kmaxn.ap, psn.h[0:1, 0:512], axis=AX.X,
                                op=ALU.max), reads=[psn], writes=[kmaxn])
                        else:
                            kb.op("dve", lambda e, psn=psn: e.tensor_reduce(
                                kred.ap, psn.h[0:1, 0:512], axis=AX.X,
                                op=ALU.max), reads=[psn], writes=[kred])
                            kb.op("dve", lambda e: e.tensor_tensor(
                                kmaxn.ap, kmaxn.ap, kred.ap, op=ALU.max),
                                reads=[kred, kmaxn], writes=[kmaxn])
                    kb.op("dve", lambda e: e.tensor_scalar(
                        kmaxn.ap, kmaxn.ap, -0.5, None, op0=ALU.mult),
                        reads=[kmaxn], writes=[kmaxn])
                    for qc in range(4):
                        kb.op("dve", lambda e, qa=qa, qc=qc: e.tensor_tensor(
                            qsq.ap, qa.h[0:64, qc * 512:(qc + 1) * 512],
                            qa.h[0:64, qc * 512:(qc + 1) * 512],
                            op=ALU.mult), reads=[qa], writes=[qsq])
                        psq = psS[0].sub(0)
                        kb.op("pe", lambda e, psq=psq: e.matmul(
                            psq.h[0:1, 0:512], ones64.ap, qsq.ap,
                            start=True, stop=True),
                            reads=[ones64, qsq], writes=[psq])
                        kb.op("dve", lambda e, psq=psq, h=h, qc=qc:
                              e.scalar_tensor_tensor(
                                  mrow.ap, psq.h[0:1, 0:512], 0.5,
                                  np_h.h[0:1, qc * 512:qc * 512 + 512],
                                  op0=ALU.mult, op1=ALU.subtract),
                              reads=[psq, np_h], writes=[mrow])
                        kb.op("dve", lambda e, qa=qa, qc=qc: e.tensor_scalar(
                            qa.h[64:65, qc * 512:(qc + 1) * 512], mrow.ap,
                            -1.0, kmaxn.ap, op0=ALU.mult, op1=ALU.add),
                            reads=[mrow, kmaxn], writes=[qa])
                        nslot = 8 * qc + 8 if mode == "l0a" else NTF
                        for s in range(nslot):
                            sbt = psS[(s // 3) % 2]
                            soff = (s % 3) * 512
                            sb3 = sbt.sub(soff)
                            kb.op("pe", lambda e, s=s, qc=qc, sb3=sb3,
                                  soff=soff, ks=ks, qa=qa: e.matmul(
                                      sb3.h[0:128, soff:soff + 512],
                                      ks.h[0:65, s * 128:(s + 1) * 128],
                                      qa.h[0:65, qc * 512:(qc + 1) * 512],
                                      start=True, stop=True),
                                  reads=[ks, qa], writes=[sb3])
                            ds = s - 8 * qc
                            if 0 <= ds <= 7:
                                r = ds // 2
                                mt = maskA if ds % 2 == 0 else maskB
                                kb.op("dve", lambda e, sb3=sb3, soff=soff,
                                      r=r, mt=mt: e.tensor_tensor(
                                          sb3.h[0:128, soff + r * 128:
                                                soff + r * 128 + 128],
                                          sb3.h[0:128, soff + r * 128:
                                                soff + r * 128 + 128],
                                          mt.ap, op=ALU.add),
                                      reads=[sb3, mt], writes=[sb3])
                            ptt = PT[(s // 3) % 2]
                            ptoff = (s % 3) * 512
                            pt = ptt.sub(ptoff)
                            if mode == "l0a":
                                deadr = max(0, ds // 2) if ds >= 0 else 0
                                if deadr > 0:
                                    kb.op("dve", lambda e, pt=pt,
                                          ptoff=ptoff, deadr=deadr:
                                          e.memset(
                                              pt.h[0:128, ptoff:
                                                   ptoff + deadr * 128],
                                              0.0), writes=[pt])
                                kb.op("act", lambda e, sb3=sb3, soff=soff,
                                      pt=pt, ptoff=ptoff, deadr=deadr, h=h,
                                      s=s: e.activation(
                                          pt.h[0:128, ptoff + deadr * 128:
                                               ptoff + 512],
                                          sb3.h[0:128, soff + deadr * 128:
                                                soff + 512],
                                          AF.Exp,
                                          bias=alibik.h[0:128, h * 32 + s:
                                                        h * 32 + s + 1]),
                                      reads=[sb3, alibik], writes=[pt])
                            else:
                                if ds < 0:
                                    parts = [(0, 512, 0.0)]
                                elif ds <= 7:
                                    rb = ds // 2
                                    parts = []
                                    if rb > 0:
                                        parts.append((0, rb * 128, -50.0))
                                    parts.append((rb * 128, 512, 0.0))
                                else:
                                    parts = [(0, 512, -50.0)]
                                for (lo, hi, bia) in parts:
                                    bt = m50 if bia == -50.0 else zz
                                    kb.op("act", lambda e, sb3=sb3,
                                          soff=soff, pt=pt, ptoff=ptoff,
                                          lo=lo, hi=hi, bt=bt:
                                          e.activation(
                                              pt.h[0:128, ptoff + lo:
                                                   ptoff + hi],
                                              sb3.h[0:128, soff + lo:
                                                    soff + hi],
                                              AF.Exp, bias=bt.ap),
                                          reads=[sb3, bt], writes=[pt])
                            po = psO[(qc * 8 + hh) % 2]
                            kb.op("pe", lambda e, s=s, pt=pt, ptoff=ptoff,
                                  po=po, vs=vs, nslot=nslot: e.matmul(
                                      po.h[0:65, 0:512],
                                      vs.h[0:128, s * 65:s * 65 + 65],
                                      pt.h[0:128, ptoff:ptoff + 512],
                                      start=(s == 0), stop=(s == nslot - 1)),
                                  reads=[vs, pt], writes=[po],
                                  inc=(s == nslot - 1))
                        po = psO[(qc * 8 + hh) % 2]
                        kb.op("dve", lambda e, po=po: e.reciprocal(
                            linv.ap, po.h[64:65, 0:512]),
                            reads=[po], writes=[linv])
                        kb.op("dve", lambda e: e.tensor_copy(
                            linb.ap, linv.ap), reads=[linv], writes=[linb])
                        psb = psS[0].sub(1024)
                        kb.op("pe", lambda e, psb=psb: e.matmul(
                            psb.h[0:64, 1024:1536], ones1x64.ap, linb.ap,
                            start=True, stop=True),
                            reads=[ones1x64, linb], writes=[psb])
                        kb.op("act", lambda e, po=po: e.activation(
                            osb.ap, po.h[0:64, 0:512], AF.Identity),
                            reads=[po], writes=[osb])
                        c2, r2 = h // 2, (h % 2) * 64
                        kb.op("dve", lambda e, psb=psb, c2=c2, r2=r2, qc=qc:
                              e.tensor_tensor(
                                  OA[c2].h[r2:r2 + 64,
                                           qc * 512:(qc + 1) * 512],
                                  osb.ap, psb.h[0:64, 1024:1536],
                                  op=ALU.mult),
                              reads=[osb, psb], writes=[OA[c2]])
                        if dbg and h == 0 and qc == 3:
                            kb.dma("gd", dbg["qa0"], qa.ap, reads=[qa])
                            kb.dma("gd", dbg["ks0"], ks.ap, reads=[ks])
                            kb.dma("gd", dbg["kmaxn0"], kmaxn.ap,
                                   reads=[kmaxn])
                        if dbg and h == 0 and qc == 0:
                            podump = kb.sb("podump", (128, 512), ctx=ph3)
                            kb.op("dve", lambda e, po=po: e.tensor_copy(
                                podump.ap, po.h[0:128, 0:512]),
                                reads=[po], writes=[podump])
                            kb.dma("gd", dbg["po0"], podump.ap,
                                   reads=[podump])
                            kb.dma("gd", dbg["pt0"], PT[0].ap,
                                   reads=[PT[0]])
                            kb.dma("gd", dbg["pt1"], PT[1].ap,
                                   reads=[PT[1]])
                kb.phase_barrier()
            kb.phase_barrier()

    if dbg:
        kb.dma("gd", dbg["oa0"], OA[0].ap, reads=[OA[0]])
    with ExitStack() as ph:
        E = ph.enter_context
        psY = [Tile(E(_psum_r(nc, kb.uname("psY"), [128, 512], F32)))
               for _ in range(4)]
        wpj = kb.sb("wpj", (128, NDC * 1024), BF16, ctx=ph)
        for c in range(NDC):
            kb.dma("gd", wpj.h[0:128, c * 1024:(c + 1) * 1024],
                   wprojT[c * 128:(c + 1) * 128, :], writes=[wpj])
        xr = [kb.sb(f"xr{i}", (128, D), ctx=ph) for i in range(2)]
        yout = [kb.sb(f"yout{i}", (128, D), ctx=ph) for i in range(2)]
        for j in range(NTOW):
            xt = xr[j % 2]
            kb.dma("sd", xt.ap, xown[j * 128:(j + 1) * 128, :], writes=[xt])
            yo = yout[j % 2]
            for o2 in range(2):
                py = psY[(j * 2 + o2) % 4]
                for c in range(NDC):
                    kb.op("pe", lambda e, c=c, o2=o2, j=j, py=py: e.matmul(
                        py.h[0:128, 0:512],
                        OA[c].h[0:128, j * 128:(j + 1) * 128],
                        wpj.h[0:128, c * 1024 + o2 * 512:
                              c * 1024 + o2 * 512 + 512],
                        start=(c == 0), stop=(c == NDC - 1)),
                        reads=[OA[c], wpj], writes=[py],
                        inc=(c == NDC - 1))
                kb.op("dve", lambda e, py=py, yo=yo, xt=xt, o2=o2:
                      e.tensor_tensor(
                          yo.h[0:128, o2 * 512:o2 * 512 + 512],
                          xt.h[0:128, o2 * 512:o2 * 512 + 512],
                          py.h[0:128, 0:512], op=ALU.add),
                      reads=[py, xt], writes=[yo])
            kb.dma("gd", xoutD[j * 128:(j + 1) * 128, :], yo.ap,
                   reads=[yo])
        kb.phase_barrier()


def build_l1b(debug=False):
    nc = bass.Bass("TRN2", target_bir_lowering=False)
    xown = nc.dram_tensor("xown", [HALF, D], F32, kind="ExternalInput").ap()
    xfull = nc.dram_tensor("xfull", [S, D], F32, kind="ExternalInput").ap()
    wqkvT = nc.dram_tensor("wqkvT", [D, 3 * D], F32, kind="ExternalInput").ap()
    wprojT = nc.dram_tensor("wprojT", [D, D], F32, kind="ExternalInput").ap()
    identD = nc.dram_tensor("ident", [128, 128], BF16, kind="ExternalInput").ap()
    maskAD = nc.dram_tensor("maskA", [128, 128], F32, kind="ExternalInput").ap()
    maskBD = nc.dram_tensor("maskB", [128, 128], F32, kind="ExternalInput").ap()
    alibikD = nc.dram_tensor("alibik", [128, 512], F32, kind="ExternalInput").ap()
    negposqD = nc.dram_tensor("negposq", [1, 16 * HALF], F32,
                              kind="ExternalInput").ap()
    xoutD = nc.dram_tensor("xout", [HALF, D], F32, kind="ExternalOutput").ap()
    with ExitStack() as ctx:
        kb = KB(nc, ctx)
        emit_l0a_body(kb, ctx, xown, xfull, wqkvT, wprojT, identD, maskAD,
                      maskBD, alibikD, negposqD, xoutD, None, mode="l1b")
        kb.finish("gd")
    return nc



def build_l0ffn():
    nc = bass.Bass("TRN2", target_bir_lowering=False)
    xown = nc.dram_tensor("xown", [HALF, D], F32, kind="ExternalInput").ap()
    gate_wT = nc.dram_tensor("gate_wT", [D, 8], F32, kind="ExternalInput").ap()
    rmsr = nc.dram_tensor("rmsr", [128, 8], F32, kind="ExternalInput").ap()
    w1D = nc.dram_tensor("w1", [8, D, 4096], F32, kind="ExternalInput").ap()
    w2D = nc.dram_tensor("w2", [8, 4096, D], F32, kind="ExternalInput").ap()
    b1rD = nc.dram_tensor("b1r", [128, 256], F32, kind="ExternalInput").ap()
    b2D = nc.dram_tensor("b2", [8, D], F32, kind="ExternalInput").ap()
    identD = nc.dram_tensor("ident", [128, 128], BF16, kind="ExternalInput").ap()
    identf32D = nc.dram_tensor("identf32", [128, 128], F32,
                               kind="ExternalInput").ap()
    xoutD = nc.dram_tensor("xout", [HALF, D], F32, kind="ExternalOutput").ap()
    with ExitStack() as ctx:
        kb = KB(nc, ctx)
        emit_l0ffn_body(kb, ctx, xown, gate_wT, rmsr, w1D, w2D, b1rD, b2D,
                        identD, identf32D, xoutD)
        kb.finish("gd")
    return nc


def emit_l0ffn_body(kb, ctx, xown, gate_wT, rmsr, w1D, w2D, b1rD, b2D,
                    identD, identf32D, xoutD):
    nc = kb.nc
    NF = 4096 // 128
    ident = kb.sb("sb_ident", (128, 128), BF16)
    rms = kb.sb("sb_rms", (128, 8))
    b1r = kb.sb("sb_b1r", (128, 256))
    gwr = kb.sb("sb_gwr", (128, 8 * NDC))
    wts = kb.sb("wts", (128, 8 * NTOW))
    xnw = [kb.sb(f"xnw{c}", (128, HALF), BF16) for c in range(NDC)]
    outb = kb.sb("outb", (128, NTOW * D), BF16)
    idf32 = kb.sb("sb_idf32", (128, 128))
    kb.dma("sd", ident.ap, identD, writes=[ident])
    kb.dma("sd", idf32.ap, identf32D, writes=[idf32])
    kb.dma("sd", rms.ap, rmsr, writes=[rms])
    kb.dma("sd", b1r.ap, b1rD, writes=[b1r])

    with ExitStack() as ph:
        E = ph.enter_context
        pst = [Tile(E(_psum_r(nc, kb.uname("pst"), [128, 1024], BF16)))
               for _ in range(2)]
        ps32 = [Tile(E(_psum_r(nc, kb.uname("ps32"), [128, 512], F32)))
                for _ in range(2)]
        psR = Tile(E(_psum_r(nc, kb.uname("psR"), [128, 512], F32)))
        xo = [kb.sb(f"xo{i}", (128, D), ctx=ph) for i in range(2)]
        xnb = [kb.sb(f"xnb{i}", (128, D), BF16, ctx=ph) for i in range(2)]
        xn32 = [kb.sb(f"xn32{i}", (128, D), ctx=ph) for i in range(2)]
        xf32 = [kb.sb(f"xf32{i}", (128, D), ctx=ph) for i in range(2)]
        lmax = kb.sb("lmax", (128, 1), ctx=ph)
        nlmax = kb.sb("nlmax", (128, 1), ctx=ph)
        expl = kb.sb("expl", (128, 8), ctx=ph)
        lsum = kb.sb("lsum", (128, 1), ctx=ph)
        rcp = kb.sb("rcp", (128, 1), ctx=ph)
        msk = kb.sb("msk", (128, 8), ctx=ph)
        lns = make_ln_scratch(kb, "ffn", None, extra=True, ctx=ph)
        for c in range(NDC):
            kb.dma("sd", gwr.h[0:128, 8 * c:8 * c + 8],
                   gate_wT[c * 128:(c + 1) * 128, :], writes=[gwr])
        for c in range(NDC):
            kb.op("dve", lambda e, c=c: e.tensor_scalar(
                gwr.h[0:128, 8 * c:8 * c + 8],
                gwr.h[0:128, 8 * c:8 * c + 8],
                rms.h[0:128, c:c + 1], None, op0=ALU.mult),
                reads=[gwr, rms], writes=[gwr])
        for j in range(NTOW):
            xt = xo[j % 2]
            kb.dma("sd", xt.ap, xown[j * 128:(j + 1) * 128, :], writes=[xt])
            xb_ = xnb[j % 2]
            x3 = xn32[j % 2]
            emit_ln_tile(kb, xt, None, None, lns,
                         extra_rms=(xb_.ap, xb_, x3.ap, x3),
                         skip_plain=True)
            xf = xf32[j % 2]
            for c in range(NDC):
                kb.op("pe", lambda e, xb_=xb_, c=c, j=j: e.transpose(
                    pst[(j * NDC + c) % 2].h[0:128, 0:128],
                    xb_.h[0:128, c * 128:(c + 1) * 128], ident.ap),
                    reads=[xb_, ident], writes=[pst[(j * NDC + c) % 2]])
                kb.op("dve", lambda e, c=c, j=j: e.tensor_scalar(
                    xnw[c].h[0:128, j * 128:(j + 1) * 128],
                    pst[(j * NDC + c) % 2].h[0:128, 0:128],
                    rms.h[0:128, c:c + 1], None, op0=ALU.mult),
                    reads=[pst[(j * NDC + c) % 2], rms], writes=[xnw[c]])
                kb.op("pe", lambda e, x3=x3, c=c, j=j: e.transpose(
                    ps32[c % 2].h[0:128, 0:128],
                    x3.h[0:128, c * 128:(c + 1) * 128], idf32.ap),
                    reads=[x3, idf32], writes=[ps32[c % 2]])
                kb.op("dve", lambda e, xf=xf, c=c: e.tensor_copy(
                    xf.h[0:128, c * 128:(c + 1) * 128],
                    ps32[c % 2].h[0:128, 0:128]),
                    reads=[ps32[c % 2]], writes=[xf])
            for c in range(NDC):
                kb.op("pe", lambda e, xf=xf, c=c: e.matmul(
                    psR.h[0:128, 0:8],
                    xf.h[0:128, c * 128:(c + 1) * 128],
                    gwr.h[0:128, 8 * c:8 * c + 8],
                    start=(c == 0), stop=(c == NDC - 1)),
                    reads=[xf, gwr], writes=[psR], inc=(c == NDC - 1))
            kb.op("dve", lambda e: e.tensor_reduce(
                lmax.ap, psR.h[0:128, 0:8], axis=AX.X, op=ALU.max),
                reads=[psR], writes=[lmax])
            kb.op("dve", lambda e: e.tensor_scalar(
                nlmax.ap, lmax.ap, -1.0, None, op0=ALU.mult),
                reads=[lmax], writes=[nlmax])
            kb.op("act", lambda e: e.activation(
                expl.ap, psR.h[0:128, 0:8], AF.Exp, bias=nlmax.ap,
                accum_out=lsum.ap), reads=[psR, nlmax],
                writes=[expl, lsum])
            kb.op("dve", lambda e: e.reciprocal(rcp.ap, lsum.ap),
                  reads=[lsum], writes=[rcp])
            kb.op("dve", lambda e: e.tensor_scalar(
                msk.ap, psR.h[0:128, 0:8], lmax.ap, None, op0=ALU.is_ge),
                reads=[psR, lmax], writes=[msk])
            kb.op("dve", lambda e, j=j: e.tensor_scalar(
                wts.h[0:128, 8 * j:8 * j + 8], msk.ap, rcp.ap, None,
                op0=ALU.mult), reads=[msk, rcp], writes=[wts])
        kb.phase_barrier()

    with ExitStack() as ph:
        E = ph.enter_context
        psU = [Tile(E(_psum_r(nc, kb.uname("psU"), [128, 512], F32)))
               for _ in range(2)]
        psD = [Tile(E(_psum_r(nc, kb.uname("psD"), [128, 512], F32)))
               for _ in range(4)]
        w1s = [kb.sb(f"w1s{c}", (128, 4096), BF16, ctx=ph)
               for c in range(NDC)]
        w2s = [kb.sb(f"w2s{i}", (128, 512), BF16, ctx=ph) for i in range(2)]
        hfm = kb.sb("hfm", (128, NF * 512), BF16, ctx=ph)
        b2s = kb.sb("b2s", (1, D), BF16, ctx=ph)
        ones1 = kb.sb("ones1r", (1, 128), BF16, ctx=ph)
        kb.op("dve", lambda e: e.memset(ones1.ap, 1.0), writes=[ones1])
        kb.op("dve", lambda e: e.memset(outb.ap, 0.0), writes=[outb])
        for ex in range(8):
            for c in range(NDC):
                kb.dma("gd", w1s[c].ap, w1D[ex, c * 128:(c + 1) * 128, :],
                       writes=[w1s[c]])
            kb.dma("gd", b2s.ap, b2D[ex:ex + 1, :], writes=[b2s])
            for tq in range(4):
                for fc in range(NF):
                    pu = psU[fc % 2]
                    for c in range(NDC):
                        kb.op("pe", lambda e, c=c, fc=fc, tq=tq, pu=pu:
                              e.matmul(pu.h[0:128, 0:512],
                                       w1s[c].h[0:128, fc * 128:
                                                (fc + 1) * 128],
                                       xnw[c].h[0:128, tq * 512:
                                                (tq + 1) * 512],
                                       start=(c == 0), stop=(c == NDC - 1)),
                              reads=[w1s[c], xnw[c]], writes=[pu],
                              inc=(c == NDC - 1))
                    kb.op("act", lambda e, pu=pu, fc=fc, ex=ex: e.activation(
                        hfm.h[0:128, fc * 512:(fc + 1) * 512],
                        pu.h[0:128, 0:512], AF.Silu,
                        bias=b1r.h[0:128, ex * 32 + fc:ex * 32 + fc + 1]),
                        reads=[pu, b1r], writes=[hfm])
                for dh in range(2):
                    for t4 in range(4):
                        pd = psD[t4]
                        kb.op("pe", lambda e, pd=pd, dh=dh: e.matmul(
                            pd.h[0:128, 0:512], ones1.ap,
                            b2s.h[0:1, dh * 512:dh * 512 + 512],
                            start=True, stop=False),
                            reads=[ones1, b2s], writes=[pd], inc=False)
                    for fc in range(NF):
                        w2t = w2s[fc % 2]
                        kb.dma("gd", w2t.ap,
                               w2D[ex, fc * 128:(fc + 1) * 128,
                                   dh * 512:dh * 512 + 512],
                               writes=[w2t])
                        for t4 in range(4):
                            pd = psD[t4]
                            kb.op("pe", lambda e, pd=pd, w2t=w2t, fc=fc,
                                  t4=t4: e.matmul(
                                      pd.h[0:128, 0:512],
                                      hfm.h[0:128, fc * 512 + t4 * 128:
                                            fc * 512 + t4 * 128 + 128],
                                      w2t.h[0:128, 0:512],
                                      start=False, stop=(fc == NF - 1)),
                                  reads=[hfm, w2t], writes=[pd],
                                  inc=(t4 == 3))
                    for t4 in range(4):
                        j = tq * 4 + t4
                        pd = psD[t4]
                        kb.op("dve", lambda e, pd=pd, j=j, dh=dh, ex=ex:
                              e.scalar_tensor_tensor(
                                  outb.h[0:128, j * 1024 + dh * 512:
                                         j * 1024 + dh * 512 + 512],
                                  pd.h[0:128, 0:512],
                                  wts.h[0:128, 8 * j + ex:8 * j + ex + 1],
                                  outb.h[0:128, j * 1024 + dh * 512:
                                         j * 1024 + dh * 512 + 512],
                                  op0=ALU.mult, op1=ALU.add),
                              reads=[pd, wts, outb], writes=[outb])
        kb.phase_barrier()

    with ExitStack() as ph:
        xr = [kb.sb(f"xr{i}", (128, D), ctx=ph) for i in range(2)]
        yout = [kb.sb(f"yo{i}", (128, D), ctx=ph) for i in range(2)]
        for j in range(NTOW):
            xt = xr[j % 2]
            kb.dma("sd", xt.ap, xown[j * 128:(j + 1) * 128, :], writes=[xt])
            yo = yout[j % 2]
            kb.op("dve", lambda e, xt=xt, yo=yo, j=j: e.tensor_tensor(
                yo.ap, xt.ap, outb.h[0:128, j * 1024:(j + 1) * 1024],
                op=ALU.add), reads=[xt, outb], writes=[yo])
            kb.dma("gd", xoutD[j * 128:(j + 1) * 128, :], yo.ap, reads=[yo])
        kb.phase_barrier()



def build_l0b():
    nc = bass.Bass("TRN2", target_bir_lowering=False)
    xfull = nc.dram_tensor("xfull", [S, D], F32, kind="ExternalInput").ap()
    waT = nc.dram_tensor("waT", [D, D], F32, kind="ExternalInput").ap()
    wbT = nc.dram_tensor("wbT", [D, D], F32, kind="ExternalInput").ap()
    wcT = nc.dram_tensor("wcT", [D, D], F32, kind="ExternalInput").ap()
    basD = nc.dram_tensor("bas", [128, 8 * 3], F32, kind="ExternalInput").ap()
    identD = nc.dram_tensor("ident", [128, 128], BF16, kind="ExternalInput").ap()
    xoutD = nc.dram_tensor("xout", [S, D], F32, kind="ExternalOutput").ap()
    with ExitStack() as ctx:
        kb = KB(nc, ctx)
        emit_l0b_body(kb, ctx, xfull, waT, wbT, wcT, basD, identD, xoutD)
        kb.finish("gd")
    return nc


def emit_l0b_body(kb, ctx, xfull, waT, wbT, wcT, basD, identD, xoutD,
                  xmap=None, fused=None):
    nc = kb.nc

    def ldxf(xt, g):
        if xmap is None:
            kb.dma("sd", xt.ap, xfull[g * 128:(g + 1) * 128, :],
                   writes=[xt])
        else:
            off = (g % 2) * HALF + (g // 2) * 128
            kb.dma("gd", xt.ap, xfull[off:off + 128, :], writes=[xt])
    ident = kb.sb("sb_ident", (128, 128), BF16)
    bas = kb.sb("sb_bas", (128, 24))
    kb.dma("sd", ident.ap, identD, writes=[ident])
    kb.dma("sd", bas.ap, basD, writes=[bas])
    xnf = [kb.sb(f"xnf{c}", (128, S), BF16) for c in range(NDC)]
    ytm = kb.sb("ytm", (128, NTF * D), BF16)

    with ExitStack() as ph:
        E = ph.enter_context
        pst = [Tile(E(_psum_r(nc, kb.uname("pst"), [128, 1024], BF16)))
               for _ in range(2)]
        xo = [kb.sb(f"xo{i}", (128, D), ctx=ph) for i in range(2)]
        xnt = [kb.sb(f"xnt{i}", (128, D), BF16, ctx=ph) for i in range(2)]
        lns = make_ln_scratch(kb, "l0b", None, ctx=ph)
        for g in range(NTF):
            xt = xo[g % 2]
            ldxf(xt, g)
            xn = xnt[g % 2]
            emit_ln_tile(kb, xt, xn.ap, xn, lns)
            for c in range(NDC):
                emit_transpose_tile(
                    kb, xn.h[0:128, c * 128:(c + 1) * 128], xn,
                    xnf[c].h[0:128, g * 128:(g + 1) * 128], xnf[c],
                    ident, pst[(g * NDC + c) % 2])
        kb.phase_barrier()

    with ExitStack() as ph:
        E = ph.enter_context
        psP = [Tile(E(_psum_r(nc, kb.uname("psP"), [128, 512], F32)))
               for _ in range(3)]
        pst = [Tile(E(_psum_r(nc, kb.uname("pst"), [128, 1024], BF16)))
               for _ in range(2)]
        was = kb.sb("was", (128, NDC * 128), BF16, ctx=ph)
        wbs = kb.sb("wbs", (128, NDC * 128), BF16, ctx=ph)
        wcs = kb.sb("wcs", (128, NDC * 128), BF16, ctx=ph)
        at = kb.sb("at", (128, S), BF16, ctx=ph)
        dt_ = kb.sb("dt", (128, S), BF16, ctx=ph)
        ct_ = kb.sb("ct", (128, S), BF16, ctx=ph)
        st_ = kb.sb("st", (128, S), BF16, ctx=ph)
        yc = kb.sb("yc", (128, S), BF16, ctx=ph)
        for co in range(NDC):
            for c in range(NDC):
                kb.dma("gd", was.h[0:128, c * 128:(c + 1) * 128],
                       waT[c * 128:(c + 1) * 128, co * 128:(co + 1) * 128],
                       writes=[was])
                kb.dma("gd", wbs.h[0:128, c * 128:(c + 1) * 128],
                       wbT[c * 128:(c + 1) * 128, co * 128:(co + 1) * 128],
                       writes=[wbs])
                kb.dma("gd", wcs.h[0:128, c * 128:(c + 1) * 128],
                       wcT[c * 128:(c + 1) * 128, co * 128:(co + 1) * 128],
                       writes=[wcs])
            for kg in range(8):
                for i, (ws, dst, fn, bcol) in enumerate([
                        (was, at, AF.Sigmoid, co),
                        (wbs, dt_, AF.Tanh, 8 + co),
                        (wcs, ct_, AF.Sigmoid, 16 + co)]):
                    pp = psP[i]
                    for c in range(NDC):
                        kb.op("pe", lambda e, ws=ws, c=c, kg=kg, pp=pp:
                              e.matmul(pp.h[0:128, 0:512],
                                       ws.h[0:128, c * 128:(c + 1) * 128],
                                       xnf[c].h[0:128, kg * 512:
                                                (kg + 1) * 512],
                                       start=(c == 0), stop=(c == NDC - 1)),
                              reads=[ws, xnf[c]], writes=[pp],
                              inc=(c == NDC - 1))
                    kb.op("act", lambda e, pp=pp, dst=dst, fn=fn, bcol=bcol,
                          kg=kg: e.activation(
                              dst.h[0:128, kg * 512:(kg + 1) * 512],
                              pp.h[0:128, 0:512], fn,
                              bias=bas.h[0:128, bcol:bcol + 1]),
                          reads=[pp, bas], writes=[dst])
                kb.op("dve", lambda e, kg=kg: e.tensor_scalar(
                    yc.h[0:128, kg * 512:(kg + 1) * 512],
                    at.h[0:128, kg * 512:(kg + 1) * 512], -1.0, 1.0,
                    op0=ALU.mult, op1=ALU.add),
                    reads=[at], writes=[yc])
                kb.op("dve", lambda e, kg=kg: e.tensor_tensor(
                    dt_.h[0:128, kg * 512:(kg + 1) * 512],
                    yc.h[0:128, kg * 512:(kg + 1) * 512],
                    dt_.h[0:128, kg * 512:(kg + 1) * 512], op=ALU.mult),
                    reads=[yc, dt_], writes=[dt_])
            kb.op("dve", lambda e: e.tensor_tensor_scan(
                st_.ap, at.ap, dt_.ap, 0.0, op0=ALU.mult, op1=ALU.add),
                reads=[at, dt_], writes=[st_])
            kb.op("dve", lambda e: e.tensor_tensor(
                yc.ap, ct_.ap, st_.ap, op=ALU.mult),
                reads=[ct_, st_], writes=[yc])
            for g in range(NTF):
                p = pst[g % 2]
                kb.op("pe", lambda e, g=g, p=p: e.transpose(
                    p.h[0:128, 0:128], yc.h[0:128, g * 128:(g + 1) * 128],
                    ident.ap), reads=[yc, ident], writes=[p])
                kb.op("dve", lambda e, g=g, p=p, co=co: e.tensor_copy(
                    ytm.h[0:128, g * 1024 + co * 128:
                          g * 1024 + co * 128 + 128],
                    p.h[0:128, 0:128]), reads=[p], writes=[ytm])
        kb.phase_barrier()

    if fused is None:
        with ExitStack() as ph:
            xr = [kb.sb(f"xr{i}", (128, D), ctx=ph) for i in range(2)]
            yout = [kb.sb(f"yo{i}", (128, D), ctx=ph) for i in range(2)]
            for g in range(NTF):
                xt = xr[g % 2]
                kb.dma("sd", xt.ap, xfull[g * 128:(g + 1) * 128, :],
                       writes=[xt])
                yo = yout[g % 2]
                kb.op("dve", lambda e, xt=xt, yo=yo, g=g: e.tensor_tensor(
                    yo.ap, xt.ap, ytm.h[0:128, g * 1024:(g + 1) * 1024],
                    op=ALU.add), reads=[xt, ytm], writes=[yo])
                kb.dma("gd", xoutD[g * 128:(g + 1) * 128, :], yo.ap,
                       reads=[yo])
            kb.phase_barrier()
    else:
        ybf, yrs, xcur = fused["ybf"], fused["yrs"], fused["xcur"]
        for g in range(NTF):
            roff = (g % 2) * HALF + (g // 2) * 128
            kb.dma("gd", ybf.h[roff:roff + 128, :],
                   ytm.h[0:128, g * 1024:(g + 1) * 1024],
                   reads=[ytm], writes=[ybf])
        kb.phase_barrier()
        kb.op(fused["ccq"], lambda e: e.collective_compute(
            "ReduceScatter", ALU.add,
            replica_groups=[[0, 1], [2, 3], [4, 5], [6, 7]],
            ins=[ybf.h[:]], outs=[yrs.h[:]]),
            reads=[ybf], writes=[yrs])
        if "cbq" in fused:
            bi, bo = fused["bar"]
            kb.op(fused["cbq"], lambda e: e.collective_compute(
                "AllReduce", ALU.add,
                replica_groups=[[0, 1], [2, 3], [4, 5], [6, 7]],
                ins=[bi.h[:]], outs=[bo.h[:]]),
                reads=[bi, yrs], writes=[bo])
        kb.phase_barrier()
        with ExitStack() as ph:
            xr = [kb.sb(f"xr{i}", (128, D), ctx=ph) for i in range(2)]
            yr = [kb.sb(f"yr{i}", (128, D), BF16, ctx=ph) for i in range(2)]
            yout = [kb.sb(f"yo{i}", (128, D), ctx=ph) for i in range(2)]
            for j in range(NTOW):
                xt = xr[j % 2]
                yt = yr[j % 2]
                kb.dma("sd", xt.ap, xcur[j * 128:(j + 1) * 128, :],
                       writes=[xt])
                kb.dma("gd", yt.ap, yrs.h[j * 128:(j + 1) * 128, :],
                       reads=[yrs], writes=[yt])
                yo = yout[j % 2]
                kb.op("dve", lambda e, xt=xt, yt=yt, yo=yo:
                      e.scalar_tensor_tensor(
                          yo.ap, yt.ap, 0.5, xt.ap,
                          op0=ALU.mult, op1=ALU.add),
                      reads=[xt, yt], writes=[yo])
                kb.dma("gd", xoutD[j * 128:(j + 1) * 128, :], yo.ap,
                       reads=[yo])
            kb.phase_barrier()



def build_l1ffn():
    nc = bass.Bass("TRN2", target_bir_lowering=False)
    xown = nc.dram_tensor("xown", [HALF, D], F32, kind="ExternalInput").ap()
    rwD = nc.dram_tensor("router_w", [D, 8], F32, kind="ExternalInput").ap()
    gwD = nc.dram_tensor("gate_w", [8, D, 512], F32, kind="ExternalInput").ap()
    uwD = nc.dram_tensor("up_w", [8, D, 512], F32, kind="ExternalInput").ap()
    dwD = nc.dram_tensor("down_w", [8, 512, D], F32, kind="ExternalInput").ap()
    sgD = nc.dram_tensor("s_gate_w", [D, 2048], F32, kind="ExternalInput").ap()
    suD = nc.dram_tensor("s_up_w", [D, 2048], F32, kind="ExternalInput").ap()
    sdD = nc.dram_tensor("s_down_w", [2048, D], F32, kind="ExternalInput").ap()
    identD = nc.dram_tensor("ident", [128, 128], BF16, kind="ExternalInput").ap()
    identf32D = nc.dram_tensor("identf32", [128, 128], F32,
                               kind="ExternalInput").ap()
    xoutD = nc.dram_tensor("xout", [HALF, D], F32, kind="ExternalOutput").ap()
    with ExitStack() as ctx:
        kb = KB(nc, ctx)
        emit_l1ffn_body(kb, ctx, xown, rwD, gwD, uwD, dwD, sgD, suD, sdD,
                        identD, identf32D, xoutD)
        kb.finish("gd")
    return nc


def emit_l1ffn_body(kb, ctx, xown, rwD, gwD, uwD, dwD, sgD, suD, sdD,
                    identD, identf32D, xoutD):
    nc = kb.nc
    ident = kb.sb("sb_ident", (128, 128), BF16)
    idf32 = kb.sb("sb_idf32", (128, 128))
    rwr = kb.sb("sb_rwr", (128, 8 * NDC))
    wts = kb.sb("wts", (128, 8 * NTOW))
    xnw = [kb.sb(f"xnw{c}", (128, HALF), BF16) for c in range(NDC)]
    outb = kb.sb("outb", (128, NTOW * D), BF16)
    kb.dma("sd", ident.ap, identD, writes=[ident])
    kb.dma("sd", idf32.ap, identf32D, writes=[idf32])

    with ExitStack() as ph:
        E = ph.enter_context
        pst = [Tile(E(_psum_r(nc, kb.uname("pst"), [128, 1024], BF16)))
               for _ in range(2)]
        ps32 = [Tile(E(_psum_r(nc, kb.uname("ps32"), [128, 512], F32)))
                for _ in range(2)]
        psR = Tile(E(_psum_r(nc, kb.uname("psR"), [128, 512], F32)))
        xo = [kb.sb(f"xo{i}", (128, D), ctx=ph) for i in range(2)]
        xnb = [kb.sb(f"xnb{i}", (128, D), BF16, ctx=ph) for i in range(2)]
        xn32 = [kb.sb(f"xn32{i}", (128, D), ctx=ph) for i in range(2)]
        xf32 = [kb.sb(f"xf32{i}", (128, D), ctx=ph) for i in range(2)]
        top8 = kb.sb("top8", (128, 8), ctx=ph)
        nlmax = kb.sb("nlmax", (128, 1), ctx=ph)
        expl = kb.sb("expl", (128, 8), ctx=ph)
        lsum = kb.sb("lsum", (128, 1), ctx=ph)
        msk = kb.sb("msk", (128, 8), ctx=ph)
        msel = kb.sb("msel", (128, 8), ctx=ph)
        s4 = kb.sb("s4", (128, 1), ctx=ph)
        den = kb.sb("den", (128, 1), ctx=ph)
        rcp = kb.sb("rcp", (128, 1), ctx=ph)
        lns = make_ln_scratch(kb, "l1f", None, ctx=ph)
        for c in range(NDC):
            kb.dma("sd", rwr.h[0:128, 8 * c:8 * c + 8],
                   rwD[c * 128:(c + 1) * 128, :], writes=[rwr])
        for j in range(NTOW):
            xt = xo[j % 2]
            kb.dma("sd", xt.ap, xown[j * 128:(j + 1) * 128, :], writes=[xt])
            xb_ = xnb[j % 2]
            x3 = xn32[j % 2]
            emit_ln_tile(kb, xt, xb_.ap, xb_, lns)
            kb.op("dve", lambda e, xt=xt, x3=x3: e.tensor_scalar(
                x3.ap, xt.ap, lns["m"].ap, lns["r"].ap,
                op0=ALU.subtract, op1=ALU.mult),
                reads=[xt, lns["m"], lns["r"]], writes=[x3])
            xf = xf32[j % 2]
            for c in range(NDC):
                kb.op("pe", lambda e, xb_=xb_, c=c, j=j: e.transpose(
                    pst[(j * NDC + c) % 2].h[0:128, 0:128],
                    xb_.h[0:128, c * 128:(c + 1) * 128], ident.ap),
                    reads=[xb_, ident], writes=[pst[(j * NDC + c) % 2]])
                kb.op("dve", lambda e, c=c, j=j: e.tensor_copy(
                    xnw[c].h[0:128, j * 128:(j + 1) * 128],
                    pst[(j * NDC + c) % 2].h[0:128, 0:128]),
                    reads=[pst[(j * NDC + c) % 2]], writes=[xnw[c]])
                kb.op("pe", lambda e, x3=x3, c=c: e.transpose(
                    ps32[c % 2].h[0:128, 0:128],
                    x3.h[0:128, c * 128:(c + 1) * 128], idf32.ap),
                    reads=[x3, idf32], writes=[ps32[c % 2]])
                kb.op("dve", lambda e, xf=xf, c=c: e.tensor_copy(
                    xf.h[0:128, c * 128:(c + 1) * 128],
                    ps32[c % 2].h[0:128, 0:128]),
                    reads=[ps32[c % 2]], writes=[xf])
            for c in range(NDC):
                kb.op("pe", lambda e, xf=xf, c=c: e.matmul(
                    psR.h[0:128, 0:8],
                    xf.h[0:128, c * 128:(c + 1) * 128],
                    rwr.h[0:128, 8 * c:8 * c + 8],
                    start=(c == 0), stop=(c == NDC - 1)),
                    reads=[xf, rwr], writes=[psR], inc=(c == NDC - 1))
            kb.op("dve", lambda e: e.max(top8.ap, psR.h[0:128, 0:8]),
                  reads=[psR], writes=[top8])
            kb.op("dve", lambda e: e.tensor_scalar(
                nlmax.ap, top8.h[0:128, 0:1], -1.0, None, op0=ALU.mult),
                reads=[top8], writes=[nlmax])
            kb.op("act", lambda e: e.activation(
                expl.ap, psR.h[0:128, 0:8], AF.Exp, bias=nlmax.ap,
                accum_out=lsum.ap), reads=[psR, nlmax],
                writes=[expl, lsum])
            kb.op("dve", lambda e: e.tensor_scalar(
                msk.ap, psR.h[0:128, 0:8], top8.h[0:128, 3:4], None,
                op0=ALU.is_ge), reads=[psR, top8], writes=[msk])
            kb.op("dve", lambda e: e.tensor_tensor(
                msel.ap, msk.ap, expl.ap, op=ALU.mult),
                reads=[msk, expl], writes=[msel])
            kb.op("dve", lambda e: e.tensor_reduce(
                s4.ap, msel.ap, axis=AX.X, op=ALU.add),
                reads=[msel], writes=[s4])
            kb.op("dve", lambda e: e.scalar_tensor_tensor(
                den.ap, lsum.ap, 1e-20, s4.ap, op0=ALU.mult, op1=ALU.add),
                reads=[lsum, s4], writes=[den])
            kb.op("dve", lambda e: e.reciprocal(rcp.ap, den.ap),
                  reads=[den], writes=[rcp])
            kb.op("dve", lambda e, j=j: e.tensor_scalar(
                wts.h[0:128, 8 * j:8 * j + 8], msel.ap, rcp.ap, None,
                op0=ALU.mult), reads=[msel, rcp], writes=[wts])
        kb.phase_barrier()

    with ExitStack() as ph:
        E = ph.enter_context
        psG = [Tile(E(_psum_r(nc, kb.uname("psG"), [128, 512], F32)))
               for _ in range(2)]
        psU = [Tile(E(_psum_r(nc, kb.uname("psU"), [128, 512], F32)))
               for _ in range(2)]
        psD = [Tile(E(_psum_r(nc, kb.uname("psD"), [128, 512], F32)))
               for _ in range(4)]
        gws = [kb.sb(f"gws{c}", (128, 512), BF16, ctx=ph)
               for c in range(NDC)]
        uws = [kb.sb(f"uws{c}", (128, 512), BF16, ctx=ph)
               for c in range(NDC)]
        dws = [kb.sb(f"dws{i}", (128, D), BF16, ctx=ph) for i in range(4)]
        gsb = kb.sb("gsb", (128, 512), BF16, ctx=ph)
        act4 = kb.sb("act4", (128, 4 * 512), BF16, ctx=ph)
        kb.op("dve", lambda e: e.memset(outb.ap, 0.0), writes=[outb])
        for ei in range(12):
            routed = ei < 8
            ic0 = 0 if routed else (ei - 8) * 4
            for c in range(NDC):
                src_g = (gwD[ei, c * 128:(c + 1) * 128, :] if routed else
                         sgD[c * 128:(c + 1) * 128,
                             (ei - 8) * 512:(ei - 7) * 512])
                src_u = (uwD[ei, c * 128:(c + 1) * 128, :] if routed else
                         suD[c * 128:(c + 1) * 128,
                             (ei - 8) * 512:(ei - 7) * 512])
                kb.dma("gd", gws[c].ap, src_g, writes=[gws[c]])
                kb.dma("gd", uws[c].ap, src_u, writes=[uws[c]])
            for i in range(4):
                src_d = (dwD[ei, i * 128:(i + 1) * 128, :] if routed else
                         sdD[(ei - 8) * 512 + i * 128:
                             (ei - 8) * 512 + (i + 1) * 128, :])
                kb.dma("gd", dws[i].ap, src_d, writes=[dws[i]])
            for tq in range(4):
                for ic in range(4):
                    pg = psG[ic % 2]
                    pu = psU[ic % 2]
                    for c in range(NDC):
                        kb.op("pe", lambda e, c=c, ic=ic, tq=tq, pg=pg:
                              e.matmul(pg.h[0:128, 0:512],
                                       gws[c].h[0:128, ic * 128:
                                                (ic + 1) * 128],
                                       xnw[c].h[0:128, tq * 512:
                                                (tq + 1) * 512],
                                       start=(c == 0), stop=(c == NDC - 1)),
                              reads=[gws[c], xnw[c]], writes=[pg],
                              inc=(c == NDC - 1))
                    for c in range(NDC):
                        kb.op("pe", lambda e, c=c, ic=ic, tq=tq, pu=pu:
                              e.matmul(pu.h[0:128, 0:512],
                                       uws[c].h[0:128, ic * 128:
                                                (ic + 1) * 128],
                                       xnw[c].h[0:128, tq * 512:
                                                (tq + 1) * 512],
                                       start=(c == 0), stop=(c == NDC - 1)),
                              reads=[uws[c], xnw[c]], writes=[pu],
                              inc=(c == NDC - 1))
                    kb.op("act", lambda e, pg=pg: e.activation(
                        gsb.ap, pg.h[0:128, 0:512], AF.Silu),
                        reads=[pg], writes=[gsb])
                    kb.op("dve", lambda e, pu=pu, ic=ic: e.tensor_tensor(
                        act4.h[0:128, ic * 512:(ic + 1) * 512],
                        gsb.ap, pu.h[0:128, 0:512], op=ALU.mult),
                        reads=[gsb, pu], writes=[act4])
                for dh in range(2):
                    for t4 in range(4):
                        pd = psD[t4]
                        for ic in range(4):
                            kb.op("pe", lambda e, pd=pd, ic=ic, t4=t4,
                                  dh=dh: e.matmul(
                                      pd.h[0:128, 0:512],
                                      act4.h[0:128, ic * 512 + t4 * 128:
                                             ic * 512 + t4 * 128 + 128],
                                      dws[ic].h[0:128, dh * 512:
                                                dh * 512 + 512],
                                      start=(ic == 0), stop=(ic == 3)),
                                  reads=[act4, dws[ic]], writes=[pd],
                                  inc=(ic == 3))
                        j = tq * 4 + t4
                        if routed:
                            kb.op("dve", lambda e, pd=pd, j=j, dh=dh, ei=ei:
                                  e.scalar_tensor_tensor(
                                      outb.h[0:128, j * 1024 + dh * 512:
                                             j * 1024 + dh * 512 + 512],
                                      pd.h[0:128, 0:512],
                                      wts.h[0:128, 8 * j + ei:
                                            8 * j + ei + 1],
                                      outb.h[0:128, j * 1024 + dh * 512:
                                             j * 1024 + dh * 512 + 512],
                                      op0=ALU.mult, op1=ALU.add),
                                  reads=[pd, wts, outb], writes=[outb])
                        else:
                            kb.op("dve", lambda e, pd=pd, j=j, dh=dh:
                                  e.tensor_tensor(
                                      outb.h[0:128, j * 1024 + dh * 512:
                                             j * 1024 + dh * 512 + 512],
                                      pd.h[0:128, 0:512],
                                      outb.h[0:128, j * 1024 + dh * 512:
                                             j * 1024 + dh * 512 + 512],
                                      op=ALU.add),
                                  reads=[pd, outb], writes=[outb])
        kb.phase_barrier()

    with ExitStack() as ph:
        xr = [kb.sb(f"xr{i}", (128, D), ctx=ph) for i in range(2)]
        yout = [kb.sb(f"yo{i}", (128, D), ctx=ph) for i in range(2)]
        for j in range(NTOW):
            xt = xr[j % 2]
            kb.dma("sd", xt.ap, xown[j * 128:(j + 1) * 128, :], writes=[xt])
            yo = yout[j % 2]
            kb.op("dve", lambda e, xt=xt, yo=yo, j=j: e.tensor_tensor(
                yo.ap, xt.ap, outb.h[0:128, j * 1024:(j + 1) * 1024],
                op=ALU.add), reads=[xt, outb], writes=[yo])
            kb.dma("gd", xoutD[j * 128:(j + 1) * 128, :], yo.ap, reads=[yo])
        kb.phase_barrier()



def l1a_masks(hf):
    kp = np.arange(128)[:, None]
    qp = np.arange(128)[None, :]
    tri = np.where(kp > qp, NEG, 0.0).astype(np.float32)
    edge = np.where(kp <= qp, NEG, 0.0).astype(np.float32)
    full = np.full((128, 128), NEG, np.float32)
    zeros = np.zeros((128, 128), np.float32)
    g64 = np.broadcast_to(np.where(kp >= 64, NEG, 0.0),
                          (128, 128)).astype(np.float32)
    g64tri = np.where((kp >= 64) | (kp > qp), NEG, 0.0).astype(np.float32)
    if hf == 0:
        return dict(mCm2=edge, mCm1=zeros, mC0=tri, mCp1=full,
                    mG0=full, mG1=g64tri, mG=g64)
    return dict(mCm2=full, mCm1=edge, mC0=zeros, mCp1=tri,
                mG0=full, mG1=g64, mG=g64)


def rope_tables():
    inv = 1.0 / (10000.0 ** (np.arange(0, 64, 2, dtype=np.float32) / 64.0))
    t = np.arange(S, dtype=np.float32)
    fr = t[:, None] * inv[None, :]
    return np.cos(fr).T.astype(np.float32), np.sin(fr).T.astype(np.float32)


def build_l1a():
    nc = bass.Bass("TRN2", target_bir_lowering=False)
    xown = nc.dram_tensor("xown", [HALF, D], F32, kind="ExternalInput").ap()
    xfull = nc.dram_tensor("xfull", [S, D], F32, kind="ExternalInput").ap()
    wqD = nc.dram_tensor("wq", [D, D], F32, kind="ExternalInput").ap()
    wkD = nc.dram_tensor("wk", [D, 256], F32, kind="ExternalInput").ap()
    wvD = nc.dram_tensor("wv", [D, 256], F32, kind="ExternalInput").ap()
    woD = nc.dram_tensor("wo", [D, D], F32, kind="ExternalInput").ap()
    identD = nc.dram_tensor("ident", [128, 128], BF16, kind="ExternalInput").ap()
    cqD = nc.dram_tensor("cosq", [32, HALF], F32, kind="ExternalInput").ap()
    sqD = nc.dram_tensor("sinq", [32, HALF], F32, kind="ExternalInput").ap()
    ckD = nc.dram_tensor("cosk", [32, S], F32, kind="ExternalInput").ap()
    skD = nc.dram_tensor("sink", [32, S], F32, kind="ExternalInput").ap()
    mks = {m: nc.dram_tensor(m, [128, 128], F32, kind="ExternalInput").ap()
           for m in ("mCm2", "mCm1", "mC0", "mCp1", "mG0", "mG1", "mG")}
    xoutD = nc.dram_tensor("xout", [HALF, D], F32, kind="ExternalOutput").ap()
    with ExitStack() as ctx:
        kb = KB(nc, ctx)
        emit_l1a_body(kb, ctx, xown, xfull, wqD, wkD, wvD, woD, identD,
                      cqD, sqD, ckD, skD, mks, xoutD)
        kb.finish("gd")
    return nc


def emit_l1a_body(kb, ctx, xown, xfull, wqD, wkD, wvD, woD, identD,
                  cqD, sqD, ckD, skD, mksD, xoutD, xmap=None):
    nc = kb.nc

    def ldxf(xt, g):
        if xmap is None:
            kb.dma("sd", xt.ap, xfull[g * 128:(g + 1) * 128, :],
                   writes=[xt])
        else:
            off = (g % 2) * HALF + (g // 2) * 128
            kb.dma("gd", xt.ap, xfull[off:off + 128, :], writes=[xt])
    RMS_EPS = float(np.finfo(np.float32).eps)
    ident = kb.sb("sb_ident", (128, 128), BF16)
    cosq = kb.sb("sb_cosq", (32, HALF), BF16)
    sinq = kb.sb("sb_sinq", (32, HALF), BF16)
    cosk = kb.sb("sb_cosk", (32, S), BF16)
    sink = kb.sb("sb_sink", (32, S), BF16)
    mks = {m: kb.sb("sb_" + m, (128, 128)) for m in mksD}
    ones64 = kb.sb("ones64", (64, 1), BF16)
    ones1x64 = kb.sb("ones1x64", (1, 64), BF16)
    kb.dma("sd", ident.ap, identD, writes=[ident])
    kb.dma("gd", cosq.ap, cqD, writes=[cosq])
    kb.dma("gd", sinq.ap, sqD, writes=[sinq])
    kb.dma("gd", cosk.ap, ckD, writes=[cosk])
    kb.dma("gd", sink.ap, skD, writes=[sink])
    for m in mks:
        kb.dma("sd", mks[m].ap, mksD[m], writes=[mks[m]])
    kb.op("dve", lambda e: e.memset(ones64.ap, 1.0), writes=[ones64])
    kb.op("dve", lambda e: e.memset(ones1x64.ap, 1.0), writes=[ones1x64])
    OA = [kb.sb(f"oa{c}", (128, HALF), BF16) for c in range(NDC)]
    QF = [kb.sb(f"qf{c}", (128, HALF), BF16) for c in range(NDC)]
    K_all = [kb.sb(f"kall{i}", (128, S), BF16) for i in range(2)]
    V_all = [kb.sb(f"vall{g}", (128, 256), BF16) for g in range(NTF)]

    with ExitStack() as ph:
        E = ph.enter_context
        pst = [Tile(E(_psum_r(nc, kb.uname("pst"), [128, 1024], BF16)))
               for _ in range(2)]
        psQ = [Tile(E(_psum_r(nc, kb.uname("psQ"), [128, 512], F32)))
               for _ in range(2)]
        xo = [kb.sb(f"xo{i}", (128, D), ctx=ph) for i in range(2)]
        xnt = [kb.sb(f"xnt{i}", (128, D), BF16, ctx=ph) for i in range(2)]
        xnow_fm = [kb.sb(f"xnowfm{c}", (128, HALF), BF16, ctx=ph)
                   for c in range(NDC)]
        wqsl = [kb.sb(f"wqsl{i}", (128, 1024), BF16, ctx=ph)
                for i in range(2)]
        lns = make_ln_scratch(kb, "l1a", None, ctx=ph)
        for j in range(NTOW):
            xt = xo[j % 2]
            kb.dma("sd", xt.ap, xown[j * 128:(j + 1) * 128, :], writes=[xt])
            xn = xnt[j % 2]
            emit_ln_tile(kb, xt, xn.ap, xn, lns)
            for c in range(NDC):
                emit_transpose_tile(
                    kb, xn.h[0:128, c * 128:(c + 1) * 128], xn,
                    xnow_fm[c].h[0:128, j * 128:(j + 1) * 128], xnow_fm[c],
                    ident, pst[(j * NDC + c) % 2])
        for dq in range(NDC):
            ws = wqsl[dq % 2]
            for c in range(NDC):
                kb.dma("gd", ws.h[0:128, c * 128:(c + 1) * 128],
                       wqD[c * 128:(c + 1) * 128, dq * 128:(dq + 1) * 128],
                       writes=[ws])
            for tq in range(4):
                pq = psQ[tq % 2]
                for c in range(NDC):
                    kb.op("pe", lambda e, c=c, tq=tq, pq=pq, ws=ws:
                          e.matmul(
                              pq.h[0:128, 0:512],
                              ws.h[0:128, c * 128:(c + 1) * 128],
                              xnow_fm[c].h[0:128, tq * 512:(tq + 1) * 512],
                              start=(c == 0), stop=(c == NDC - 1)),
                          reads=[ws, xnow_fm[c]], writes=[pq],
                          inc=(c == NDC - 1))
                kb.op("act", lambda e, pq=pq, dq=dq, tq=tq: e.activation(
                    QF[dq].h[0:128, tq * 512:(tq + 1) * 512],
                    pq.h[0:128, 0:512], AF.Identity),
                    reads=[pq], writes=[QF[dq]])
        kb.phase_barrier()

    with ExitStack() as ph:
        E = ph.enter_context
        pst = [Tile(E(_psum_r(nc, kb.uname("pst"), [128, 1024], BF16)))
               for _ in range(2)]
        psK = [Tile(E(_psum_r(nc, kb.uname("psK"), [128, 512], F32)))
               for _ in range(2)]
        psV = [Tile(E(_psum_r(nc, kb.uname("psV"), [128, 512], F32)))
               for _ in range(2)]
        xf = [kb.sb(f"xf{i}", (128, D), ctx=ph) for i in range(2)]
        xnt = [kb.sb(f"xnt{i}", (128, D), BF16, ctx=ph) for i in range(2)]
        xnf = [[kb.sb(f"xnf_{c}", (128, 512), BF16, ctx=ph)
                for c in range(NDC)]]
        wks = kb.sb("wks", (128, NDC * 256), BF16, ctx=ph)
        wvs = kb.sb("wvs", (128, NDC * 256), BF16, ctx=ph)
        lns = make_ln_scratch(kb, "l1af", None, ctx=ph)
        for c in range(NDC):
            kb.dma("gd", wks.h[0:128, c * 256:(c + 1) * 256],
                   wkD[c * 128:(c + 1) * 128, :], writes=[wks])
            kb.dma("gd", wvs.h[0:128, c * 256:(c + 1) * 256],
                   wvD[c * 128:(c + 1) * 128, :], writes=[wvs])
        for kg in range(8):
            fmb = xnf[0]
            for t in range(4):
                g = kg * 4 + t
                xt = xf[g % 2]
                ldxf(xt, g)
                xn = xnt[g % 2]
                emit_ln_tile(kb, xt, xn.ap, xn, lns)
                for c in range(NDC):
                    emit_transpose_tile(
                        kb, xn.h[0:128, c * 128:(c + 1) * 128], xn,
                        fmb[c].h[0:128, t * 128:(t + 1) * 128], fmb[c],
                        ident, pst[(g * NDC + c) % 2])
            for oc in range(2):
                pk = psK[oc % 2]
                for c in range(NDC):
                    kb.op("pe", lambda e, c=c, oc=oc, pk=pk, fmb=fmb:
                          e.matmul(pk.h[0:128, 0:512],
                                   wks.h[0:128, c * 256 + oc * 128:
                                         c * 256 + oc * 128 + 128],
                                   fmb[c].h[0:128, 0:512],
                                   start=(c == 0), stop=(c == NDC - 1)),
                          reads=[wks, fmb[c]], writes=[pk],
                          inc=(c == NDC - 1))
                kb.op("act", lambda e, pk=pk, oc=oc, kg=kg: e.activation(
                    K_all[oc].h[0:128, kg * 512:(kg + 1) * 512],
                    pk.h[0:128, 0:512], AF.Identity),
                    reads=[pk], writes=[K_all[oc]])
            for t in range(4):
                g = kg * 4 + t
                pv = psV[t % 2]
                for c in range(NDC):
                    kb.op("pe", lambda e, c=c, t=t, pv=pv, fmb=fmb:
                          e.matmul(pv.h[0:128, 0:256],
                                   fmb[c].h[0:128, t * 128:(t + 1) * 128],
                                   wvs.h[0:128, c * 256:(c + 1) * 256],
                                   start=(c == 0), stop=(c == NDC - 1)),
                          reads=[wvs, fmb[c]], writes=[pv],
                          inc=(c == NDC - 1))
                kb.op("dve", lambda e, pv=pv, g=g: e.tensor_copy(
                    V_all[g].ap, pv.h[0:128, 0:256]),
                    reads=[pv], writes=[V_all[g]])
        kb.phase_barrier()

    SS = float(1.0 / np.sqrt(np.sqrt(64.0)))
    with ExitStack() as ph3:
        E3 = ph3.enter_context
        psS = [Tile(E3(_psum_r(nc, kb.uname("psS"), [128, 512], F32)))
               for _ in range(2)]
        psN = Tile(E3(_psum_r(nc, kb.uname("psN"), [128, 512], F32)))
        psB = Tile(E3(_psum_r(nc, kb.uname("psB"), [128, 512], F32)))
        psO = [Tile(E3(_psum_r(nc, kb.uname("psO"), [128, 512], F32)))
               for _ in range(2)]
        KS = kb.sb("ksx", (65, S), BF16, ctx=ph3)
        VS = kb.sb("vsx", (128, NTF * 65), BF16, ctx=ph3)
        QS = [kb.sb(f"qs{i}", (65, HALF), BF16, ctx=ph3) for i in range(2)]
        sq_ = kb.sb("sq_", (64, 512), BF16, ctx=ph3)
        rstd = kb.sb("rstd", (1, S), ctx=ph3)
        rt1 = kb.sb("rt1", (32, S), BF16, ctx=ph3)
        rt2 = kb.sb("rt2", (32, S), BF16, ctx=ph3)
        x1t = kb.sb("x1t", (32, S), BF16, ctx=ph3)
        x2t = kb.sb("x2t", (32, S), BF16, ctx=ph3)
        rstdb = kb.sb("rstdb", (1, 512), BF16, ctx=ph3)
        linv = kb.sb("linv", (1, 128), ctx=ph3)
        linb = kb.sb("linb", (1, 128), BF16, ctx=ph3)
        osb = kb.sb("osb", (64, 128), BF16, ctx=ph3)
        ptw_g = kb.sb("ptw", (128, 5 * 128), BF16, ctx=ph3)

        def rms_rope(tile, n, cos, sin, scale_extra):
            for c2 in range(n // 512):
                kb.op("dve", lambda e, c2=c2: e.tensor_tensor(
                    sq_.ap, tile.h[0:64, c2 * 512:(c2 + 1) * 512],
                    tile.h[0:64, c2 * 512:(c2 + 1) * 512], op=ALU.mult),
                    reads=[tile], writes=[sq_])
                kb.op("pe", lambda e: e.matmul(
                    psN.h[0:1, 0:512], ones64.ap, sq_.ap,
                    start=True, stop=True),
                    reads=[ones64, sq_], writes=[psN])
                kb.op("dve", lambda e, c2=c2: e.tensor_scalar(
                    rstd.h[0:1, c2 * 512:(c2 + 1) * 512],
                    psN.h[0:1, 0:512], 1.0 / 64.0, RMS_EPS,
                    op0=ALU.mult, op1=ALU.add),
                    reads=[psN], writes=[rstd])
            kb.op("act", lambda e, n=n: e.activation(
                rstd.h[0:1, 0:n], rstd.h[0:1, 0:n], AF.Ln),
                reads=[rstd], writes=[rstd])
            kb.op("act", lambda e, n=n: e.activation(
                rstd.h[0:1, 0:n], rstd.h[0:1, 0:n], AF.Exp, scale=-0.5),
                reads=[rstd], writes=[rstd])
            kb.op("dve", lambda e, n=n, scale_extra=scale_extra:
                  e.tensor_scalar(rstd.h[0:1, 0:n], rstd.h[0:1, 0:n],
                                  scale_extra, None, op0=ALU.mult),
                  reads=[rstd], writes=[rstd])
            lb = rstdb
            for c2 in range(n // 512):
                kb.op("dve", lambda e, c2=c2: e.tensor_copy(
                    lb.ap, rstd.h[0:1, c2 * 512:(c2 + 1) * 512]),
                    reads=[rstd], writes=[lb])
                kb.op("pe", lambda e: e.matmul(
                    psB.h[0:64, 0:512], ones1x64.ap, lb.ap,
                    start=True, stop=True),
                    reads=[ones1x64, lb], writes=[psB])
                kb.op("dve", lambda e, c2=c2: e.tensor_tensor(
                    tile.h[0:64, c2 * 512:(c2 + 1) * 512],
                    tile.h[0:64, c2 * 512:(c2 + 1) * 512],
                    psB.h[0:64, 0:512], op=ALU.mult),
                    reads=[psB, tile], writes=[tile])
            kb.op("dve", lambda e, n=n: e.tensor_copy(
                x1t.h[0:32, 0:n], tile.h[0:32, 0:n]),
                reads=[tile], writes=[x1t])
            kb.op("dve", lambda e, n=n: e.tensor_copy(
                x2t.h[0:32, 0:n], tile.h[32:64, 0:n]),
                reads=[tile], writes=[x2t])
            kb.op("dve", lambda e, n=n, cos=cos: e.tensor_tensor(
                rt1.h[0:32, 0:n], x1t.h[0:32, 0:n], cos.h[0:32, 0:n],
                op=ALU.mult), reads=[x1t, cos], writes=[rt1])
            kb.op("dve", lambda e, n=n, sin=sin: e.tensor_tensor(
                rt2.h[0:32, 0:n], x2t.h[0:32, 0:n], sin.h[0:32, 0:n],
                op=ALU.mult), reads=[x2t, sin], writes=[rt2])
            kb.op("dve", lambda e, n=n: e.tensor_tensor(
                rt1.h[0:32, 0:n], rt1.h[0:32, 0:n], rt2.h[0:32, 0:n],
                op=ALU.add), reads=[rt1, rt2], writes=[rt1])
            kb.op("dve", lambda e, n=n, sin=sin: e.tensor_tensor(
                rt2.h[0:32, 0:n], x1t.h[0:32, 0:n], sin.h[0:32, 0:n],
                op=ALU.mult), reads=[x1t, sin], writes=[rt2])
            kb.op("dve", lambda e, n=n, cos=cos: e.tensor_tensor(
                x2t.h[0:32, 0:n], x2t.h[0:32, 0:n], cos.h[0:32, 0:n],
                op=ALU.mult), reads=[x2t, cos], writes=[x2t])
            kb.op("dve", lambda e, n=n: e.tensor_tensor(
                x2t.h[0:32, 0:n], x2t.h[0:32, 0:n], rt2.h[0:32, 0:n],
                op=ALU.subtract), reads=[x2t, rt2], writes=[x2t])
            kb.op("dve", lambda e, n=n: e.tensor_copy(
                tile.h[0:32, 0:n], rt1.h[0:32, 0:n]),
                reads=[rt1], writes=[tile])
            kb.op("dve", lambda e, n=n: e.tensor_copy(
                tile.h[32:64, 0:n], x2t.h[0:32, 0:n]),
                reads=[x2t], writes=[tile])

        for kv in range(4):
            kb.op("dve", lambda e, kv=kv: e.tensor_copy(
                KS.h[0:64, 0:S],
                K_all[kv // 2].h[(kv % 2) * 64:(kv % 2) * 64 + 64, 0:S]),
                reads=[K_all[kv // 2]], writes=[KS])
            kb.op("dve", lambda e: e.memset(KS.h[64:65, 0:S], 1.0),
                  writes=[KS])
            rms_rope(KS, S, cosk, sink, SS)
            kb.op("dve", lambda e: e.memset(VS.ap, 1.0), writes=[VS])
            for g in range(NTF):
                kb.op("dve", lambda e, g=g, kv=kv: e.tensor_copy(
                    VS.h[0:128, g * 65:g * 65 + 64],
                    V_all[g].h[0:128, kv * 64:kv * 64 + 64]),
                    reads=[V_all[g]], writes=[VS])
            for hq in range(4):
                h = kv * 4 + hq
                qa = QS[hq % 2]
                c, r0 = h // 2, (h % 2) * 64
                kb.op("dve", lambda e, qa=qa, c=c, r0=r0: e.tensor_copy(
                    qa.h[0:64, 0:HALF], QF[c].h[r0:r0 + 64, 0:HALF]),
                    reads=[QF[c]], writes=[qa])
                kb.op("dve", lambda e, qa=qa: e.memset(
                    qa.h[64:65, 0:HALF], 0.0), writes=[qa])
                rms_rope(qa, HALF, cosq, sinq, SS)
                for lt in range(NTOW):
                    slots = [s for s in range(2 * lt - 2, 2 * lt + 2)
                             if 0 <= s < NTF]
                    cls = {s: s - 2 * lt for s in slots}
                    po = psO[lt % 2]
                    nsl = len(slots) + 1
                    for i, s in enumerate(slots):
                        kb.op("pe", lambda e, s=s, lt=lt, i=i, qa=qa:
                              e.matmul(
                                  psS[0].h[0:128, i * 128:(i + 1) * 128],
                                  KS.h[0:65, s * 128:(s + 1) * 128],
                                  qa.h[0:65, lt * 128:(lt + 1) * 128],
                                  start=True, stop=True),
                              reads=[KS, qa], writes=[psS[0]])
                        mt = mks[{-2: "mCm2", -1: "mCm1", 0: "mC0",
                                  1: "mCp1"}[cls[s]]]
                        kb.op("dve", lambda e, i=i, mt=mt: e.tensor_tensor(
                            psS[0].h[0:128, i * 128:(i + 1) * 128],
                            psS[0].h[0:128, i * 128:(i + 1) * 128],
                            mt.ap, op=ALU.add),
                            reads=[psS[0], mt], writes=[psS[0]])
                    kb.op("pe", lambda e, lt=lt, qa=qa: e.matmul(
                        psS[1].h[0:128, 0:128], KS.h[0:65, 0:128],
                        qa.h[0:65, lt * 128:(lt + 1) * 128],
                        start=True, stop=True),
                        reads=[KS, qa], writes=[psS[1]])
                    mg = mks["mG0" if lt == 0 else ("mG1" if lt == 1
                                                    else "mG")]
                    kb.op("dve", lambda e, mg=mg: e.tensor_tensor(
                        psS[1].h[0:128, 0:128], psS[1].h[0:128, 0:128],
                        mg.ap, op=ALU.add),
                        reads=[psS[1], mg], writes=[psS[1]])
                    ptw = ptw_g
                    kb.op("act", lambda e, nw=len(slots), ptw=ptw:
                          e.activation(ptw.h[0:128, 0:nw * 128],
                                       psS[0].h[0:128, 0:nw * 128],
                                       AF.Exp), reads=[psS[0]],
                          writes=[ptw])
                    kb.op("act", lambda e, nw=len(slots), ptw=ptw:
                          e.activation(
                              ptw.h[0:128, nw * 128:(nw + 1) * 128],
                              psS[1].h[0:128, 0:128], AF.Exp),
                          reads=[psS[1]], writes=[ptw])
                    allsl = slots + [0]
                    for i, s in enumerate(allsl):
                        kb.op("pe", lambda e, s=s, i=i, po=po,
                              nsl=nsl: e.matmul(
                                  po.h[0:65, 0:128],
                                  VS.h[0:128, s * 65:s * 65 + 65],
                                  ptw_g.h[0:128,
                                                i * 128:(i + 1) * 128],
                                  start=(i == 0), stop=(i == nsl - 1)),
                              reads=[VS, ptw_g], writes=[po],
                              inc=(i == nsl - 1))
                    kb.op("dve", lambda e, po=po: e.reciprocal(
                        linv.ap, po.h[64:65, 0:128]),
                        reads=[po], writes=[linv])
                    kb.op("dve", lambda e: e.tensor_copy(linb.ap, linv.ap),
                          reads=[linv], writes=[linb])
                    kb.op("pe", lambda e: e.matmul(
                        psB.h[0:64, 0:128], ones1x64.ap, linb.ap,
                        start=True, stop=True),
                        reads=[ones1x64, linb], writes=[psB])
                    kb.op("act", lambda e, po=po: e.activation(
                        osb.ap, po.h[0:64, 0:128], AF.Identity),
                        reads=[po], writes=[osb])
                    c2, r2 = h // 2, (h % 2) * 64
                    kb.op("dve", lambda e, c2=c2, r2=r2, lt=lt:
                          e.tensor_tensor(
                              OA[c2].h[r2:r2 + 64,
                                       lt * 128:(lt + 1) * 128],
                              osb.ap, psB.h[0:64, 0:128], op=ALU.mult),
                          reads=[osb, psB], writes=[OA[c2]])
        kb.phase_barrier()

    with ExitStack() as ph:
        E = ph.enter_context
        psY = [Tile(E(_psum_r(nc, kb.uname("psY"), [128, 512], F32)))
               for _ in range(4)]
        wpj = kb.sb("wpj", (128, NDC * 1024), BF16, ctx=ph)
        for c in range(NDC):
            kb.dma("gd", wpj.h[0:128, c * 1024:(c + 1) * 1024],
                   woD[c * 128:(c + 1) * 128, :], writes=[wpj])
        xr = [kb.sb(f"xr{i}", (128, D), ctx=ph) for i in range(2)]
        yout = [kb.sb(f"yout{i}", (128, D), ctx=ph) for i in range(2)]
        for j in range(NTOW):
            xt = xr[j % 2]
            kb.dma("sd", xt.ap, xown[j * 128:(j + 1) * 128, :], writes=[xt])
            yo = yout[j % 2]
            for o2 in range(2):
                py = psY[(j * 2 + o2) % 4]
                for c in range(NDC):
                    kb.op("pe", lambda e, c=c, o2=o2, j=j, py=py: e.matmul(
                        py.h[0:128, 0:512],
                        OA[c].h[0:128, j * 128:(j + 1) * 128],
                        wpj.h[0:128, c * 1024 + o2 * 512:
                              c * 1024 + o2 * 512 + 512],
                        start=(c == 0), stop=(c == NDC - 1)),
                        reads=[OA[c], wpj], writes=[py],
                        inc=(c == NDC - 1))
                kb.op("dve", lambda e, py=py, yo=yo, xt=xt, o2=o2:
                      e.tensor_tensor(
                          yo.h[0:128, o2 * 512:o2 * 512 + 512],
                          xt.h[0:128, o2 * 512:o2 * 512 + 512],
                          py.h[0:128, 0:512], op=ALU.add),
                      reads=[py, xt], writes=[yo])
            kb.dma("gd", xoutD[j * 128:(j + 1) * 128, :], yo.ap, reads=[yo])
        kb.phase_barrier()



def _scatter(res, key="xout"):
    out = np.empty((B, S, D), np.float32)
    for core in range(NCORES):
        b, hf = core // 2, core % 2
        out[b][own_rows(hf)] = res[core][key]
    return out


def _run(nc, in_maps):
    return run_bass_kernel_spmd(nc, in_maps, core_ids=list(range(NCORES)))


def kernel(**inp):
    return kernel_fused(**inp)


def kernel_unfused(**inp):
    assert np.asarray(inp["L1_ffn_router_bias"]).shape == ()
    C = CONSTS
    cosT, sinT = rope_tables()
    x = np.ascontiguousarray(np.asarray(inp["L0_attn_a_x"], np.float32))

    progs = {
        "l0a": build_l0a(), "l0ffn": build_l0ffn(), "l0b": build_l0b(),
        "l1a": build_l1a(), "l1ffn": build_l1ffn(), "l1b": build_l1b(),
    }

    wqkvT_l0a = np.ascontiguousarray(np.asarray(inp["L0_attn_a_W_attn"]).T)
    wprojT_l0a = np.ascontiguousarray(np.asarray(inp["L0_attn_a_W_proj"]).T)
    gate_wT = np.ascontiguousarray(np.asarray(inp["L0_ffn_gate_w"]).T)
    rmsr = np.ascontiguousarray(
        np.asarray(inp["L0_ffn_rms_w"]).reshape(8, 128).T)
    b1r = np.ascontiguousarray(
        np.asarray(inp["L0_ffn_b1"]).reshape(8, 32, 128)
        .transpose(2, 0, 1).reshape(128, 256))
    bas_l0b = np.ascontiguousarray(np.concatenate(
        [np.asarray(inp["L0_attn_b_ba"]).reshape(8, 128).T,
         np.asarray(inp["L0_attn_b_bb"]).reshape(8, 128).T,
         np.asarray(inp["L0_attn_b_bc"]).reshape(8, 128).T], axis=1))
    wqkvT_l1b = np.ascontiguousarray(np.concatenate(
        [np.asarray(inp["L1_attn_b_Wq"]), np.asarray(inp["L1_attn_b_Wk"]),
         np.asarray(inp["L1_attn_b_Wv"])], axis=1))
    kp = np.arange(128)[:, None]; qp = np.arange(128)[None, :]
    tri50 = np.where(kp > qp, -50.0, 0.0).astype(np.float32)
    full50 = np.full((128, 128), -50.0, np.float32)
    npq0 = np.zeros((1, 16 * HALF), np.float32)
    alz = np.zeros((128, 512), np.float32)

    def maps_l0a(xc):
        ms = []
        for core in range(NCORES):
            b, hf = core // 2, core % 2
            ms.append({
                "xown": np.ascontiguousarray(xc[b][own_rows(hf)]),
                "xfull": np.ascontiguousarray(xc[b]),
                "wqkvT": wqkvT_l0a, "wprojT": wprojT_l0a,
                "ident": C["ident"],
                "maskA": C["tri_causal"] if hf == 0 else C["zeros128"],
                "maskB": C["full_neg"] if hf == 0 else C["tri_causal"],
                "alibik": C["alibik"], "negposq": C[f"negposq_{hf}"]})
        return ms

    def maps_l0ffn(xc):
        ms = []
        for core in range(NCORES):
            b, hf = core // 2, core % 2
            ms.append({
                "xown": np.ascontiguousarray(xc[b][own_rows(hf)]),
                "gate_wT": gate_wT, "rmsr": rmsr,
                "w1": np.asarray(inp["L0_ffn_W1"]),
                "w2": np.asarray(inp["L0_ffn_W2"]),
                "b1r": b1r, "b2": np.asarray(inp["L0_ffn_b2"]),
                "ident": C["ident"], "identf32": C["identf32"]})
        return ms

    def maps_l0b(xc):
        ms = []
        for core in range(NCORES):
            b = core // 2
            ms.append({
                "xfull": np.ascontiguousarray(xc[b]),
                "waT": np.ascontiguousarray(np.asarray(inp["L0_attn_b_Wa"]).T),
                "wbT": np.ascontiguousarray(np.asarray(inp["L0_attn_b_Wb"]).T),
                "wcT": np.ascontiguousarray(np.asarray(inp["L0_attn_b_Wc"]).T),
                "bas": bas_l0b, "ident": C["ident"]})
        return ms

    def maps_l1a(xc):
        ms = []
        for core in range(NCORES):
            b, hf = core // 2, core % 2
            rows = own_rows(hf)
            m = {"xown": np.ascontiguousarray(xc[b][rows]),
                 "xfull": np.ascontiguousarray(xc[b]),
                 "wq": np.asarray(inp["L1_attn_a_Wq"]),
                 "wk": np.asarray(inp["L1_attn_a_Wk"]),
                 "wv": np.asarray(inp["L1_attn_a_Wv"]),
                 "wo": np.asarray(inp["L1_attn_a_Wo"]),
                 "ident": C["ident"],
                 "cosq": np.ascontiguousarray(cosT[:, rows]),
                 "sinq": np.ascontiguousarray(sinT[:, rows]),
                 "cosk": cosT, "sink": sinT}
            m.update(l1a_masks(hf))
            ms.append(m)
        return ms

    def maps_l1ffn(xc):
        ms = []
        for core in range(NCORES):
            b, hf = core // 2, core % 2
            ms.append({
                "xown": np.ascontiguousarray(xc[b][own_rows(hf)]),
                "router_w": np.asarray(inp["L1_ffn_router_w"]),
                "gate_w": np.asarray(inp["L1_ffn_gate_w"]),
                "up_w": np.asarray(inp["L1_ffn_up_w"]),
                "down_w": np.asarray(inp["L1_ffn_down_w"]),
                "s_gate_w": np.asarray(inp["L1_ffn_s_gate_w"]),
                "s_up_w": np.asarray(inp["L1_ffn_s_up_w"]),
                "s_down_w": np.asarray(inp["L1_ffn_s_down_w"]),
                "ident": C["ident"], "identf32": C["identf32"]})
        return ms

    def maps_l1b(xc):
        ms = []
        for core in range(NCORES):
            b, hf = core // 2, core % 2
            ms.append({
                "xown": np.ascontiguousarray(xc[b][own_rows(hf)]),
                "xfull": np.ascontiguousarray(xc[b]),
                "wqkvT": wqkvT_l1b,
                "wprojT": np.ascontiguousarray(np.asarray(inp["L1_attn_b_Wr"])),
                "ident": C["ident"],
                "maskA": tri50 if hf == 0 else C["zeros128"],
                "maskB": full50 if hf == 0 else tri50,
                "alibik": alz, "negposq": npq0})
        return ms

    x = _scatter(_run(progs["l0a"], maps_l0a(x)).results)
    x = _scatter(_run(progs["l0ffn"], maps_l0ffn(x)).results)
    r = _run(progs["l0b"], maps_l0b(x)).results
    x = np.stack([r[2 * b]["xout"] for b in range(B)])
    x = _scatter(_run(progs["l0ffn"], maps_l0ffn(x)).results)
    x = _scatter(_run(progs["l1a"], maps_l1a(x)).results)
    x = _scatter(_run(progs["l1ffn"], maps_l1ffn(x)).results)
    x = _scatter(_run(progs["l1b"], maps_l1b(x)).results)
    x = _scatter(_run(progs["l1ffn"], maps_l1ffn(x)).results)
    return x



def build_fused(dbg_stages=False):
    nc = bass.Bass("TRN2", target_bir_lowering=False)
    def inp(name, shape, dt=F32):
        return nc.dram_tensor(name, list(shape), dt, kind="ExternalInput").ap()
    xownD = inp("xown", [HALF, D])
    xfull0 = inp("xfull0", [S, D])
    ident = inp("ident", [128, 128], BF16)
    identf32 = inp("identf32", [128, 128])
    a_wqkvT = inp("a_wqkvT", [D, 3 * D])
    a_wprojT = inp("a_wprojT", [D, D])
    a_maskA = inp("a_maskA", [128, 128])
    a_maskB = inp("a_maskB", [128, 128])
    a_alibik = inp("a_alibik", [128, 512])
    a_negposq = inp("a_negposq", [1, 16 * HALF])
    f_gate_wT = inp("f_gate_wT", [D, 8])
    f_rmsr = inp("f_rmsr", [128, 8])
    f_w1 = inp("f_w1", [8, D, 4096])
    f_w2 = inp("f_w2", [8, 4096, D])
    f_b1r = inp("f_b1r", [128, 256])
    f_b2 = inp("f_b2", [8, D])
    iotaD = inp("iota128", [128, 1])
    b_waT = inp("b_waT", [D, D])
    b_wbT = inp("b_wbT", [D, D])
    b_wcT = inp("b_wcT", [D, D])
    b_bas = inp("b_bas", [128, 24])
    c_wq = inp("c_wq", [D, D])
    c_wk = inp("c_wk", [D, 256])
    c_wv = inp("c_wv", [D, 256])
    c_wo = inp("c_wo", [D, D])
    c_cosq = inp("c_cosq", [32, HALF])
    c_sinq = inp("c_sinq", [32, HALF])
    c_cosk = inp("c_cosk", [32, S])
    c_sink = inp("c_sink", [32, S])
    c_mks = {m: inp("c_" + m, [128, 128])
             for m in ("mCm2", "mCm1", "mC0", "mCp1", "mG0", "mG1", "mG")}
    g_router_w = inp("g_router_w", [D, 8])
    g_gate_w = inp("g_gate_w", [8, D, 512])
    g_up_w = inp("g_up_w", [8, D, 512])
    g_down_w = inp("g_down_w", [8, 512, D])
    g_s_gate_w = inp("g_s_gate_w", [D, 2048])
    g_s_up_w = inp("g_s_up_w", [D, 2048])
    g_s_down_w = inp("g_s_down_w", [2048, D])
    h_wqkvT = inp("h_wqkvT", [D, 3 * D])
    h_wprojT = inp("h_wprojT", [D, D])
    h_maskA = inp("h_maskA", [128, 128])
    h_maskB = inp("h_maskB", [128, 128])
    h_alibik = inp("h_alibik", [128, 512])
    h_negposq = inp("h_negposq", [1, 16 * HALF])
    xoutD = nc.dram_tensor("xout", [HALF, D], F32, kind="ExternalOutput").ap()
    dbgs = {}
    if dbg_stages:
        stages = (dbg_stages if isinstance(dbg_stages, (set, tuple, list))
                  else (1, 2, 3, 4, 5, 6, 7))
        for i in stages:
            dbgs[i] = nc.dram_tensor(f"dbg_x{i}", [HALF, D], F32,
                                     kind="ExternalOutput").ap()
    xcur = nc.dram_tensor("xcur", [HALF, D], F32).ap()
    xex_in = Tile(nc.dram_tensor("xex_in", [HALF, D], BF16), "xex_in")
    xex_out = Tile(nc.dram_tensor("xex_out", [S, D], BF16), "xex_out")
    ybf = Tile(nc.dram_tensor("ybf", [S, D], BF16), "ybf")
    yrs = Tile(nc.dram_tensor("yrs", [HALF, D], BF16), "yrs")
    xsd = Tile(nc.dram_tensor("xs_sorted", [XS_ROWS, D], BF16), "xsd")
    ysd = Tile(nc.dram_tensor("ys_sorted", [XS_ROWS, D], BF16), "ysd")
    bar_in = Tile(nc.dram_tensor("bar_in", [1, 8], F32), "bar_in")
    bar_out = Tile(nc.dram_tensor("bar_out", [1, 8], F32), "bar_out")

    with ExitStack() as ctx:
        kb = KB(nc, ctx)

        def scoped(fn):
            with ExitStack() as bctx:
                kb._cur = bctx
                fn(bctx)
                kb.flush()
                kb.phase_barrier()
            kb._cur = None

        def dump(i):
            if i in dbgs:
                kb.phase_barrier()
                kb.dma("gd", dbgs[i], xcur[0:HALF, :])
                kb.phase_barrier()

        ccn = [0]

        def exchange():
            kb.phase_barrier()
            kb.dma("gd", xex_in.h[:], xcur[0:HALF, :], writes=[xex_in])
            qcc = f"cc{ccn[0]}"
            qcb = f"cb{ccn[0]}"
            ccn[0] += 1
            kb.op(qcc, lambda e: e.collective_compute(
                "AllGather", ALU.bypass,
                replica_groups=[[0, 1], [2, 3], [4, 5], [6, 7]],
                ins=[xex_in.h[:]], outs=[xex_out.h[:]]),
                reads=[xex_in], writes=[xex_out])
            kb.op(qcb, lambda e: e.collective_compute(
                "AllReduce", ALU.add,
                replica_groups=[[0, 1], [2, 3], [4, 5], [6, 7]],
                ins=[bar_in.h[:]], outs=[bar_out.h[:]]),
                reads=[bar_in, xex_out], writes=[bar_out])
            kb.phase_barrier()

        xg = xex_out.h[:]
        scoped(lambda b: emit_l0a_body(
            kb, b, xownD, xfull0, a_wqkvT, a_wprojT, ident, a_maskA,
            a_maskB, a_alibik, a_negposq, xcur, None, "l0a", None))
        dump(1)
        scoped(lambda b: emit_l0ffn_sparse_body(
            kb, b, xcur, f_gate_wT, f_rmsr, f_w1, f_w2, f_b1r, f_b2,
            ident, identf32, iotaD, xcur, xsd, ysd))
        dump(2)
        exchange()
        scoped(lambda b: emit_l0b_body(
            kb, b, xg, b_waT, b_wbT, b_wcT, b_bas, ident, xcur,
            xmap="gather",
            fused={"ybf": ybf, "yrs": yrs, "xcur": xcur, "ccq": "cc3",
                   "cbq": "cb3", "bar": (bar_in, bar_out)}))
        dump(3)
        scoped(lambda b: emit_l0ffn_sparse_body(
            kb, b, xcur, f_gate_wT, f_rmsr, f_w1, f_w2, f_b1r, f_b2,
            ident, identf32, iotaD, xcur, xsd, ysd))
        dump(4)
        exchange()
        scoped(lambda b: emit_l1a_body(
            kb, b, xcur, xg, c_wq, c_wk, c_wv, c_wo, ident,
            c_cosq, c_sinq, c_cosk, c_sink, c_mks, xcur, xmap="gather"))
        dump(5)
        scoped(lambda b: emit_l1ffn_body(
            kb, b, xcur, g_router_w, g_gate_w, g_up_w, g_down_w,
            g_s_gate_w, g_s_up_w, g_s_down_w, ident, identf32, xcur))
        dump(6)
        exchange()
        scoped(lambda b: emit_l0a_body(
            kb, b, xcur, xg, h_wqkvT, h_wprojT, ident, h_maskA,
            h_maskB, h_alibik, h_negposq, xcur, None, "l1b", "gather"))
        dump(7)
        scoped(lambda b: emit_l1ffn_body(
            kb, b, xcur, g_router_w, g_gate_w, g_up_w, g_down_w,
            g_s_gate_w, g_s_up_w, g_s_down_w, ident, identf32, xoutD))
        kb.finish("gd")
    return nc


def fused_in_maps(inp):
    C = CONSTS
    cosT, sinT = rope_tables()
    x = np.ascontiguousarray(np.asarray(inp["L0_attn_a_x"], np.float32))
    wqkvT_l0a = np.ascontiguousarray(np.asarray(inp["L0_attn_a_W_attn"]).T)
    wprojT_l0a = np.ascontiguousarray(np.asarray(inp["L0_attn_a_W_proj"]).T)
    gate_wT = np.ascontiguousarray(np.asarray(inp["L0_ffn_gate_w"]).T)
    rmsr = np.ascontiguousarray(
        np.asarray(inp["L0_ffn_rms_w"]).reshape(8, 128).T)
    b1r = np.ascontiguousarray(
        np.asarray(inp["L0_ffn_b1"]).reshape(8, 32, 128)
        .transpose(2, 0, 1).reshape(128, 256))
    bas_l0b = np.ascontiguousarray(np.concatenate(
        [np.asarray(inp["L0_attn_b_ba"]).reshape(8, 128).T,
         np.asarray(inp["L0_attn_b_bb"]).reshape(8, 128).T,
         np.asarray(inp["L0_attn_b_bc"]).reshape(8, 128).T], axis=1))
    wqkvT_l1b = np.ascontiguousarray(np.concatenate(
        [np.asarray(inp["L1_attn_b_Wq"]), np.asarray(inp["L1_attn_b_Wk"]),
         np.asarray(inp["L1_attn_b_Wv"])], axis=1))
    kp = np.arange(128)[:, None]; qp = np.arange(128)[None, :]
    tri50 = np.where(kp > qp, -50.0, 0.0).astype(np.float32)
    full50 = np.full((128, 128), -50.0, np.float32)
    npq0 = np.zeros((1, 16 * HALF), np.float32)
    alz = np.zeros((128, 512), np.float32)
    maps = []
    for core in range(NCORES):
        b, hf = core // 2, core % 2
        rows = own_rows(hf)
        m = {
            "xown": np.ascontiguousarray(x[b][rows]),
            "xfull0": np.ascontiguousarray(x[b]),
            "ident": C["ident"], "identf32": C["identf32"],
            "a_wqkvT": wqkvT_l0a, "a_wprojT": wprojT_l0a,
            "a_maskA": C["tri_causal"] if hf == 0 else C["zeros128"],
            "a_maskB": C["full_neg"] if hf == 0 else C["tri_causal"],
            "a_alibik": C["alibik"], "a_negposq": C[f"negposq_{hf}"],
            "f_gate_wT": gate_wT, "f_rmsr": rmsr,
            "f_w1": np.asarray(inp["L0_ffn_W1"]),
            "f_w2": np.asarray(inp["L0_ffn_W2"]),
            "f_b1r": b1r, "f_b2": np.asarray(inp["L0_ffn_b2"]),
            "iota128": np.arange(128, dtype=np.float32).reshape(128, 1),
            "b_waT": np.ascontiguousarray(np.asarray(inp["L0_attn_b_Wa"]).T),
            "b_wbT": np.ascontiguousarray(np.asarray(inp["L0_attn_b_Wb"]).T),
            "b_wcT": np.ascontiguousarray(np.asarray(inp["L0_attn_b_Wc"]).T),
            "b_bas": bas_l0b,
            "c_wq": np.asarray(inp["L1_attn_a_Wq"]),
            "c_wk": np.asarray(inp["L1_attn_a_Wk"]),
            "c_wv": np.asarray(inp["L1_attn_a_Wv"]),
            "c_wo": np.asarray(inp["L1_attn_a_Wo"]),
            "c_cosq": np.ascontiguousarray(cosT[:, rows]),
            "c_sinq": np.ascontiguousarray(sinT[:, rows]),
            "c_cosk": cosT, "c_sink": sinT,
            "g_router_w": np.asarray(inp["L1_ffn_router_w"]),
            "g_gate_w": np.asarray(inp["L1_ffn_gate_w"]),
            "g_up_w": np.asarray(inp["L1_ffn_up_w"]),
            "g_down_w": np.asarray(inp["L1_ffn_down_w"]),
            "g_s_gate_w": np.asarray(inp["L1_ffn_s_gate_w"]),
            "g_s_up_w": np.asarray(inp["L1_ffn_s_up_w"]),
            "g_s_down_w": np.asarray(inp["L1_ffn_s_down_w"]),
            "h_wqkvT": wqkvT_l1b,
            "h_wprojT": np.ascontiguousarray(
                np.asarray(inp["L1_attn_b_Wr"])),
            "h_maskA": tri50 if hf == 0 else C["zeros128"],
            "h_maskB": full50 if hf == 0 else tri50,
            "h_alibik": alz, "h_negposq": npq0,
        }
        for mk, v in l1a_masks(hf).items():
            m["c_" + mk] = v
        maps.append(m)
    return maps


def kernel_fused(**inp):
    assert np.asarray(inp["L1_ffn_router_bias"]).shape == ()
    nc = build_fused()
    maps = fused_in_maps(inp)
    res = run_bass_kernel_spmd(nc, maps, core_ids=list(range(NCORES)))
    out = np.empty((B, S, D), np.float32)
    for core in range(NCORES):
        b, hf = core // 2, core % 2
        out[b][own_rows(hf)] = res.results[core]["xout"]
    return out



NSLOT_E = 3
XS_ROWS = 4096 + 512 + 256

def emit_l0ffn_sparse_body(kb, ctx, xown, gate_wT, rmsr, w1D, w2D, b1rD,
                           b2D, identD, identf32D, iotaD, xoutD, xsd, ysd):
    nc = kb.nc
    NF = 32
    ident = kb.sb("sb_ident", (128, 128), BF16)
    idf32 = kb.sb("sb_idf32", (128, 128))
    rms = kb.sb("sb_rms", (128, 8))
    b1r = kb.sb("sb_b1r", (128, 256))
    gwr = kb.sb("sb_gwr", (128, 8 * NDC))
    rcps = kb.sb("rcps", (128, NTOW))
    dest32 = kb.sb("dest32", (128, NTOW), mybir.dt.uint32)
    inadr = kb.sb("inadr", (1, 8 * NSLOT_E))
    outadr = kb.sb("outadr", (1, 8 * NSLOT_E))
    inidx = kb.sb("inidx", (128, 8 * NSLOT_E), mybir.dt.uint32)
    outidx = kb.sb("outidx", (128, 8 * NSLOT_E), mybir.dt.uint32)
    kb.dma("sd", ident.ap, identD, writes=[ident])
    kb.dma("sd", idf32.ap, identf32D, writes=[idf32])
    kb.dma("sd", rms.ap, rmsr, writes=[rms])
    kb.dma("sd", b1r.ap, b1rD, writes=[b1r])
    iota = kb.sb("sb_iota", (128, 1))
    kb.dma("sd", iota.ap, iotaD, writes=[iota])

    with ExitStack() as ph:
        E = ph.enter_context
        pst = [Tile(E(_psum_r(nc, kb.uname("pst"), [128, 1024], BF16)))
               for _ in range(2)]
        ps32 = [Tile(E(_psum_r(nc, kb.uname("ps32"), [128, 512], F32)))
                for _ in range(2)]
        psR = Tile(E(_psum_r(nc, kb.uname("psR"), [128, 512], F32)))
        psM = Tile(E(_psum_r(nc, kb.uname("psM"), [128, 512], F32)))
        xo = [kb.sb(f"xo{i}", (128, D), ctx=ph) for i in range(2)]
        xnb = [kb.sb(f"xnb{i}", (128, D), BF16, ctx=ph) for i in range(2)]
        xn32 = [kb.sb(f"xn32{i}", (128, D), ctx=ph) for i in range(2)]
        xf32 = [kb.sb(f"xf32{i}", (128, D), ctx=ph) for i in range(2)]
        lmax = kb.sb("lmax", (128, 1), ctx=ph)
        nlmax = kb.sb("nlmax", (128, 1), ctx=ph)
        expl = kb.sb("expl", (128, 8), ctx=ph)
        lsum = kb.sb("lsum", (128, 1), ctx=ph)
        msk = kb.sb("msk", (128, 8), BF16, ctx=ph)
        mfm = kb.sb("mfm", (8, HALF), BF16, ctx=ph)
        cum = kb.sb("cum", (8, HALF), ctx=ph)
        cnt = kb.sb("cnt", (8, 1), ctx=ph)
        cntT = kb.sb("cntT", (1, 8), ctx=ph)
        cI = kb.sb("cI", (1, 8), mybir.dt.int32, ctx=ph)
        padT = kb.sb("padT", (1, 8), ctx=ph)
        incl = kb.sb("incl", (1, 8), ctx=ph)
        poffT = kb.sb("poffT", (1, 8), ctx=ph)
        poff = kb.sb("poff", (8, 1), ctx=ph)
        ones1x8 = kb.sb("ones1x8", (1, 8), BF16, ctx=ph)
        ones8 = kb.sb("ones8", (8, 1), BF16, ctx=ph)
        dtmp = kb.sb("dtmp", (8, 512), ctx=ph)
        dfm = kb.sb("dfm", (1, HALF), ctx=ph)
        tvalid = kb.sb("tvalid", (1, 8), ctx=ph)
        trow = kb.sb("trow", (1, 8), ctx=ph)
        onesfm = kb.sb("onesfm", (8, HALF), BF16, ctx=ph)
        ones8f = kb.sb("ones8f", (8, 1), ctx=ph)
        lns = make_ln_scratch(kb, "ffs", None, extra=True, ctx=ph)
        kb.op("dve", lambda e: e.memset(ones1x8.ap, 1.0), writes=[ones1x8])
        kb.op("dve", lambda e: e.memset(ones8.ap, 1.0), writes=[ones8])
        kb.op("dve", lambda e: e.memset(onesfm.ap, 1.0), writes=[onesfm])
        kb.op("dve", lambda e: e.memset(ones8f.ap, 1.0), writes=[ones8f])
        for c in range(NDC):
            kb.dma("sd", gwr.h[0:128, 8 * c:8 * c + 8],
                   gate_wT[c * 128:(c + 1) * 128, :], writes=[gwr])
        for c in range(NDC):
            kb.op("dve", lambda e, c=c: e.tensor_scalar(
                gwr.h[0:128, 8 * c:8 * c + 8],
                gwr.h[0:128, 8 * c:8 * c + 8],
                rms.h[0:128, c:c + 1], None, op0=ALU.mult),
                reads=[gwr, rms], writes=[gwr])
        for j in range(NTOW):
            xt = xo[j % 2]
            kb.dma("sd", xt.ap, xown[j * 128:(j + 1) * 128, :], writes=[xt])
            xb_ = xnb[j % 2]
            x3 = xn32[j % 2]
            emit_ln_tile(kb, xt, None, None, lns,
                         extra_rms=(xb_.ap, xb_, x3.ap, x3), skip_plain=True)
            xf = xf32[j % 2]
            for c in range(NDC):
                kb.op("pe", lambda e, x3=x3, c=c: e.transpose(
                    ps32[c % 2].h[0:128, 0:128],
                    x3.h[0:128, c * 128:(c + 1) * 128], idf32.ap),
                    reads=[x3, idf32], writes=[ps32[c % 2]])
                kb.op("dve", lambda e, xf=xf, c=c: e.tensor_copy(
                    xf.h[0:128, c * 128:(c + 1) * 128],
                    ps32[c % 2].h[0:128, 0:128]),
                    reads=[ps32[c % 2]], writes=[xf])
            for c in range(NDC):
                kb.op("pe", lambda e, xf=xf, c=c: e.matmul(
                    psR.h[0:128, 0:8],
                    xf.h[0:128, c * 128:(c + 1) * 128],
                    gwr.h[0:128, 8 * c:8 * c + 8],
                    start=(c == 0), stop=(c == NDC - 1)),
                    reads=[xf, gwr], writes=[psR], inc=(c == NDC - 1))
            kb.op("dve", lambda e: e.tensor_reduce(
                lmax.ap, psR.h[0:128, 0:8], axis=AX.X, op=ALU.max),
                reads=[psR], writes=[lmax])
            kb.op("dve", lambda e: e.tensor_scalar(
                nlmax.ap, lmax.ap, -1.0, None, op0=ALU.mult),
                reads=[lmax], writes=[nlmax])
            kb.op("act", lambda e: e.activation(
                expl.ap, psR.h[0:128, 0:8], AF.Exp, bias=nlmax.ap,
                accum_out=lsum.ap), reads=[psR, nlmax],
                writes=[expl, lsum])
            kb.op("dve", lambda e, j=j: e.reciprocal(
                rcps.h[0:128, j:j + 1], lsum.ap), reads=[lsum],
                writes=[rcps])
            kb.op("dve", lambda e: e.tensor_scalar(
                msk.ap, psR.h[0:128, 0:8], lmax.ap, None, op0=ALU.is_ge),
                reads=[psR, lmax], writes=[msk])
            kb.op("pe", lambda e, j=j: e.transpose(
                pst[0].h[0:8, 0:128], msk.ap, ident.ap),
                reads=[msk, ident], writes=[pst[0]])
            kb.op("dve", lambda e, j=j: e.tensor_copy(
                mfm.h[0:8, j * 128:(j + 1) * 128], pst[0].h[0:8, 0:128]),
                reads=[pst[0]], writes=[mfm])
        kb.op("dve", lambda e: e.tensor_tensor_scan(
            cum.ap, onesfm.ap, mfm.ap, 0.0, op0=ALU.mult, op1=ALU.add),
            reads=[onesfm, mfm], writes=[cum])
        kb.op("dve", lambda e: e.tensor_copy(
            cnt.ap, cum.h[0:8, HALF - 1:HALF]), reads=[cum], writes=[cnt])
        kb.op("pe", lambda e: e.transpose(
            psM.h[0:1, 0:8], cnt.ap, idf32.h[0:8, 0:8]),
            reads=[cnt, idf32], writes=[psM])
        kb.op("dve", lambda e: e.tensor_copy(cntT.ap, psM.h[0:1, 0:8]),
              reads=[psM], writes=[cntT])
        kb.op("dve", lambda e: e.tensor_scalar(
            cI.ap, cntT.ap, 255.0, None, op0=ALU.add),
            reads=[cntT], writes=[cI])
        kb.op("dve", lambda e: e.tensor_scalar(
            cI.ap, cI.ap, 8, None, op0=ALU.arith_shift_right),
            reads=[cI], writes=[cI])
        kb.op("dve", lambda e: e.tensor_scalar(
            cI.ap, cI.ap, 8, None, op0=ALU.arith_shift_left),
            reads=[cI], writes=[cI])
        kb.op("dve", lambda e: e.tensor_copy(padT.ap, cI.ap),
              reads=[cI], writes=[padT])
        kb.op("dve", lambda e: e.tensor_tensor_scan(
            incl.ap, ones1x8.ap, padT.ap, 0.0, op0=ALU.mult, op1=ALU.add),
            reads=[ones1x8, padT], writes=[incl])
        kb.op("dve", lambda e: e.tensor_tensor(
            poffT.ap, incl.ap, padT.ap, op=ALU.subtract),
            reads=[incl, padT], writes=[poffT])
        kb.op("pe", lambda e: e.transpose(
            psM.h[0:8, 0:1], poffT.ap, idf32.h[0:1, 0:1]),
            reads=[poffT, idf32], writes=[psM])
        kb.op("dve", lambda e: e.tensor_copy(poff.ap, psM.h[0:8, 0:1]),
              reads=[psM], writes=[poff])
        for g2 in range(4):
            kb.op("dve", lambda e, g2=g2: e.tensor_scalar(
                dtmp.ap, cum.h[0:8, g2 * 512:(g2 + 1) * 512],
                poff.ap, -1.0, op0=ALU.add, op1=ALU.add),
                reads=[cum, poff], writes=[dtmp])
            kb.op("dve", lambda e, g2=g2: e.tensor_tensor(
                dtmp.ap, dtmp.ap, mfm.h[0:8, g2 * 512:(g2 + 1) * 512],
                op=ALU.mult), reads=[dtmp, mfm], writes=[dtmp])
            kb.op("pe", lambda e, g2=g2: e.matmul(
                psM.h[0:1, 0:512], ones8f.ap, dtmp.ap,
                start=True, stop=True), reads=[ones8f, dtmp],
                writes=[psM])
            kb.op("dve", lambda e, g2=g2: e.tensor_copy(
                dfm.h[0:1, g2 * 512:(g2 + 1) * 512], psM.h[0:1, 0:512]),
                reads=[psM], writes=[dfm])
        for j in range(NTOW):
            kb.op("pe", lambda e, j=j: e.transpose(
                psM.h[0:128, 0:1], dfm.h[0:1, j * 128:(j + 1) * 128],
                idf32.h[0:1, 0:1]), reads=[dfm, idf32], writes=[psM])
            kb.op("dve", lambda e, j=j: e.tensor_copy(
                dest32.h[0:128, j:j + 1], psM.h[0:128, 0:1]),
                reads=[psM], writes=[dest32])
        for k in range(NSLOT_E):
            kb.op("dve", lambda e, k=k: e.tensor_scalar(
                trow.ap, poffT.ap, float(256 * k), None,
                op0=ALU.add), reads=[poffT], writes=[trow])
            kb.op("dve", lambda e, k=k: e.tensor_scalar(
                tvalid.ap, padT.ap, float(256 * k), None, op0=ALU.is_gt),
                reads=[padT], writes=[tvalid])
            kb.op("dve", lambda e, k=k: e.tensor_copy(
                inadr.h[0:1, 8 * k:8 * k + 8], trow.ap),
                reads=[trow], writes=[inadr])
            kb.op("dve", lambda e: e.tensor_tensor(
                trow.ap, trow.ap, tvalid.ap, op=ALU.mult),
                reads=[trow, tvalid], writes=[trow])
            kb.op("dve", lambda e: e.tensor_scalar(
                tvalid.ap, tvalid.ap, float(-(XS_ROWS - 256)),
                float(XS_ROWS - 256),
                op0=ALU.mult, op1=ALU.add), reads=[tvalid],
                writes=[tvalid])
            kb.op("dve", lambda e: e.tensor_tensor(
                trow.ap, trow.ap, tvalid.ap, op=ALU.add),
                reads=[trow, tvalid], writes=[trow])
            kb.op("dve", lambda e, k=k: e.tensor_copy(
                outadr.h[0:1, 8 * k:8 * k + 8], trow.ap),
                reads=[trow], writes=[outadr])
        ones1f = kb.sb("ones1f", (1, 128), ctx=ph)
        idxf = kb.sb("idxf", (128, 8 * NSLOT_E), ctx=ph)
        kb.op("dve", lambda e: e.memset(ones1f.ap, 1.0), writes=[ones1f])
        for (srcT, dstI) in ((inadr, inidx), (outadr, outidx)):
            kb.op("pe", lambda e, srcT=srcT: e.matmul(
                psM.h[0:128, 0:24], ones1f.ap, srcT.ap,
                start=True, stop=True),
                reads=[ones1f, srcT], writes=[psM])
            kb.op("dve", lambda e: e.tensor_scalar(
                idxf.ap, psM.h[0:128, 0:24], iota.ap, None, op0=ALU.add),
                reads=[psM, iota], writes=[idxf])
            kb.op("dve", lambda e, dstI=dstI: e.tensor_copy(
                dstI.ap, idxf.ap), reads=[idxf], writes=[dstI])
        kb.phase_barrier()

    with ExitStack() as ph:
        xb2 = [kb.sb(f"xb2{i}", (128, D), BF16, ctx=ph) for i in range(2)]
        xo2 = [kb.sb(f"xo2{i}", (128, D), ctx=ph) for i in range(2)]
        lns = make_ln_scratch(kb, "ffs2", None, extra=True, ctx=ph)
        for j in range(NTOW):
            xt = xo2[j % 2]
            kb.dma("sd", xt.ap, xown[j * 128:(j + 1) * 128, :], writes=[xt])
            xb_ = xb2[j % 2]
            emit_ln_tile(kb, xt, None, None, lns,
                         extra_rms=(xb_.ap, xb_, None, None),
                         skip_plain=True)
            kb.op("gd", lambda e, j=j, xb_=xb_: e.indirect_dma_start(
                out=xsd.h[0:XS_ROWS, :],
                out_offset=bass.IndirectOffsetOnAxis(
                    ap=dest32.h[0:128, j:j + 1], axis=0),
                in_=xb_.ap, in_offset=None),
                reads=[xb_, dest32], writes=[xsd])
        kb.phase_barrier()

    with ExitStack() as ph:
        E = ph.enter_context
        psU = [Tile(E(_psum_r(nc, kb.uname("psU"), [128, 512], F32)))
               for _ in range(2)]
        psD = [Tile(E(_psum_r(nc, kb.uname("psD"), [128, 512], F32)))
               for _ in range(4)]
        pst = [Tile(E(_psum_r(nc, kb.uname("pst"), [128, 1024], BF16)))
               for _ in range(2)]
        w1s = [kb.sb(f"w1s{c}", (128, 4096), BF16, ctx=ph)
               for c in range(NDC)]
        w2s = kb.sb("w2s", (128, NF * 1024), BF16, ctx=ph)
        hfm = kb.sb("hfm", (128, NF * 256), BF16, ctx=ph)
        b2s = kb.sb("b2s", (1, D), BF16, ctx=ph)
        ones1 = kb.sb("ones1r", (1, 128), BF16, ctx=ph)
        xst = [kb.sb(f"xst{i}", (128, D), BF16, ctx=ph) for i in range(2)]
        xsf = [kb.sb(f"xsf{c}", (128, 256), BF16, ctx=ph)
               for c in range(NDC)]
        ysb = [kb.sb(f"ysb{i}", (128, D), BF16, ctx=ph) for i in range(2)]
        kb.op("dve", lambda e: e.memset(ones1.ap, 1.0), writes=[ones1])
        for ex in range(8):
            for c in range(NDC):
                kb.dma("gd", w1s[c].ap, w1D[ex, c * 128:(c + 1) * 128, :],
                       writes=[w1s[c]])
            for fc in range(NF):
                kb.dma("gd", w2s.h[0:128, fc * 1024:(fc + 1) * 1024],
                       w2D[ex, fc * 128:(fc + 1) * 128, :], writes=[w2s])
            kb.dma("gd", b2s.ap, b2D[ex:ex + 1, :], writes=[b2s])
            for k in range(NSLOT_E):
                slot = 8 * k + ex
                for half in range(2):
                    xt = xst[half]
                    kb.op("gd", lambda e, slot=slot, half=half, xt=xt:
                          e.indirect_dma_start(
                              out=xt.ap, out_offset=None,
                              in_=xsd.h[0:XS_ROWS, :],
                              in_offset=bass.IndirectOffsetOnAxis(
                                  ap=inidx.h[0:128, slot:slot + 1], axis=0),
                              element_offset=half * 128 * D),
                          reads=[inidx, xsd], writes=[xt])
                for half in range(2):
                    for c in range(NDC):
                        p = pst[(half * NDC + c) % 2]
                        kb.op("pe", lambda e, half=half, c=c, p=p:
                              e.transpose(p.h[0:128, 0:128],
                                          xst[half].h[0:128,
                                                      c * 128:(c + 1) * 128],
                                          ident.ap),
                              reads=[xst[half], ident], writes=[p])
                        kb.op("dve", lambda e, half=half, c=c, p=p:
                              e.tensor_scalar(
                                  xsf[c].h[0:128,
                                           half * 128:half * 128 + 128],
                                  p.h[0:128, 0:128],
                                  rms.h[0:128, c:c + 1], None,
                                  op0=ALU.mult),
                              reads=[p, rms], writes=[xsf[c]])
                for fc in range(NF):
                    pu = psU[fc % 2]
                    for c in range(NDC):
                        kb.op("pe", lambda e, c=c, fc=fc, pu=pu: e.matmul(
                            pu.h[0:128, 0:256],
                            w1s[c].h[0:128, fc * 128:(fc + 1) * 128],
                            xsf[c].h[0:128, 0:256],
                            start=(c == 0), stop=(c == NDC - 1)),
                            reads=[w1s[c], xsf[c]], writes=[pu],
                            inc=(c == NDC - 1))
                    kb.op("act", lambda e, pu=pu, fc=fc, ex=ex:
                          e.activation(
                              hfm.h[0:128, fc * 256:(fc + 1) * 256],
                              pu.h[0:128, 0:256], AF.Silu,
                              bias=b1r.h[0:128, ex * 32 + fc:
                                         ex * 32 + fc + 1]),
                          reads=[pu, b1r], writes=[hfm])
                for half in range(2):
                    for dh in range(2):
                        pd = psD[half * 2 + dh]
                        kb.op("pe", lambda e, pd=pd, dh=dh: e.matmul(
                            pd.h[0:128, 0:512], ones1.ap,
                            b2s.h[0:1, dh * 512:dh * 512 + 512],
                            start=True, stop=False),
                            reads=[ones1, b2s], writes=[pd], inc=False)
                        for fc in range(NF):
                            kb.op("pe", lambda e, pd=pd, fc=fc, half=half,
                                  dh=dh: e.matmul(
                                      pd.h[0:128, 0:512],
                                      hfm.h[0:128, fc * 256 + half * 128:
                                            fc * 256 + half * 128 + 128],
                                      w2s.h[0:128, fc * 1024 + dh * 512:
                                            fc * 1024 + dh * 512 + 512],
                                      start=False, stop=(fc == NF - 1)),
                                  reads=[hfm, w2s], writes=[pd],
                                  inc=(fc == NF - 1))
                    yb = ysb[half]
                    for dh in range(2):
                        pd = psD[half * 2 + dh]
                        kb.op("dve", lambda e, pd=pd, yb=yb, dh=dh:
                              e.tensor_copy(
                                  yb.h[0:128, dh * 512:dh * 512 + 512],
                                  pd.h[0:128, 0:512]),
                              reads=[pd], writes=[yb])
                    kb.op("gd", lambda e, slot=slot, half=half, yb=yb:
                          e.indirect_dma_start(
                              out=ysd.h[0:XS_ROWS, :],
                              out_offset=bass.IndirectOffsetOnAxis(
                                  ap=outidx.h[0:128, slot:slot + 1],
                                  axis=0),
                              in_=yb.ap, in_offset=None,
                              element_offset=half * 128 * D),
                          reads=[outidx, yb, ysd], writes=[ysd])
        kb.phase_barrier()

    with ExitStack() as ph:
        xr = [kb.sb(f"xr{i}", (128, D), ctx=ph) for i in range(2)]
        yg = [kb.sb(f"yg{i}", (128, D), BF16, ctx=ph) for i in range(2)]
        yout = [kb.sb(f"yo{i}", (128, D), ctx=ph) for i in range(2)]
        for j in range(NTOW):
            xt = xr[j % 2]
            yt = yg[j % 2]
            kb.dma("sd", xt.ap, xown[j * 128:(j + 1) * 128, :], writes=[xt])
            kb.op("gd", lambda e, j=j, yt=yt: e.indirect_dma_start(
                out=yt.ap, out_offset=None,
                in_=ysd.h[0:XS_ROWS, :],
                in_offset=bass.IndirectOffsetOnAxis(
                    ap=dest32.h[0:128, j:j + 1], axis=0)),
                reads=[ysd, dest32], writes=[yt])
            yo = yout[j % 2]
            kb.op("dve", lambda e, xt=xt, yt=yt, yo=yo, j=j:
                  e.scalar_tensor_tensor(
                      yo.ap, yt.ap, rcps.h[0:128, j:j + 1], xt.ap,
                      op0=ALU.mult, op1=ALU.add),
                  reads=[xt, yt, rcps], writes=[yo])
            kb.dma("gd", xoutD[j * 128:(j + 1) * 128, :], yo.ap, reads=[yo])
        kb.phase_barrier()


def _reg_dma_load(eng, r, dst, src_h, adr_h, slot, extra):
    eng.reg_load(r, adr_h[0:1, slot:slot + 1])
    if extra:
        eng.reg_add(r, r, extra)
    return eng.dma_start(
        out=dst.ap,
        in_=bass.AP(src_h, r, [[1024, 128], [1024, 1], [1, 1024]]))


def _reg_dma_store(eng, r, src, dst_h, adr_h, slot, extra):
    eng.reg_load(r, adr_h[0:1, slot:slot + 1])
    if extra:
        eng.reg_add(r, r, extra)
    return eng.dma_start(
        out=bass.AP(dst_h, r, [[1024, 128], [1024, 1], [1, 1024]]),
        in_=src.ap)


def build_l0ffn_sparse():
    nc = bass.Bass("TRN2", target_bir_lowering=False)
    xown = nc.dram_tensor("xown", [HALF, D], F32, kind="ExternalInput").ap()
    gate_wT = nc.dram_tensor("gate_wT", [D, 8], F32, kind="ExternalInput").ap()
    rmsr = nc.dram_tensor("rmsr", [128, 8], F32, kind="ExternalInput").ap()
    w1D = nc.dram_tensor("w1", [8, D, 4096], F32, kind="ExternalInput").ap()
    w2D = nc.dram_tensor("w2", [8, 4096, D], F32, kind="ExternalInput").ap()
    b1rD = nc.dram_tensor("b1r", [128, 256], F32, kind="ExternalInput").ap()
    b2D = nc.dram_tensor("b2", [8, D], F32, kind="ExternalInput").ap()
    identD = nc.dram_tensor("ident", [128, 128], BF16, kind="ExternalInput").ap()
    identf32D = nc.dram_tensor("identf32", [128, 128], F32,
                               kind="ExternalInput").ap()
    iotaD = nc.dram_tensor("iota128", [128, 1], F32,
                           kind="ExternalInput").ap()
    xoutD = nc.dram_tensor("xout", [HALF, D], F32, kind="ExternalOutput").ap()
    xsd = Tile(nc.dram_tensor("xs_sorted", [XS_ROWS, D], BF16), "xsd")
    ysd = Tile(nc.dram_tensor("ys_sorted", [XS_ROWS, D], BF16), "ysd")
    with ExitStack() as ctx:
        kb = KB(nc, ctx)
        emit_l0ffn_sparse_body(kb, ctx, xown, gate_wT, rmsr, w1D, w2D, b1rD,
                               b2D, identD, identf32D, iotaD, xoutD, xsd,
                               ysd)
        kb.finish("gd")
    return nc


# version 11 (final)
# speedup vs baseline: 1.7276x; 1.0506x over previous
import numpy as np
import ml_dtypes
from contextlib import ExitStack

import concourse.bass as bass
import concourse.mybir as mybir
from concourse.bass_utils import run_bass_kernel_spmd

F32 = mybir.dt.float32
BF16 = mybir.dt.bfloat16
AF = mybir.ActivationFunctionType
ALU = mybir.AluOpType
AX = mybir.AxisListType
bf16 = ml_dtypes.bfloat16

def _psum_r(nc, name, shape, dtype):
    return nc.psum_tensor(name, shape, dtype, unsafe_allow_realloc=True)


B, S, D = 4, 4096, 1024
HALF = S // 2
NTOW = HALF // 128
NTF = S // 128
NDC = D // 128
NCORES = 8
NEG = -1e9


class Tile:
    __slots__ = ("h", "last_w", "readers", "name", "subs")
    def __init__(self, h, name=""):
        self.h = h
        self.name = name
        self.last_w = None
        self.readers = []
        self.subs = None
    def sub(self, key):
        if self.subs is None:
            self.subs = {}
        t = self.subs.get(key)
        if t is None:
            t = Tile(self.h, f"{self.name}.{key}")
            self.subs[key] = t
        return t
    def __getitem__(self, sl):
        return self.h[sl]
    @property
    def ap(self):
        return self.h[:]

class Kq:
    __slots__ = ("name", "attr", "step", "sem", "n", "is_dma")
    def __init__(self, name, attr, step):
        self.name, self.attr, self.step = name, attr, step
        self.sem = None
        self.n = 0
        self.is_dma = step == 16

class KB:

    def __init__(self, nc, ctx):
        self.nc = nc
        self.ctx = ctx
        self.block = ctx.enter_context(nc.Block())
        self.q = {}
        for name, attr, step in [
            ("pe", "tensor", 1), ("act", "scalar", 1), ("dve", "vector", 1),
            ("pool", "gpsimd", 1), ("sd", "sync", 16), ("gd", "gpsimd", 16),
            ("gw0", "gpsimd", 16), ("gw1", "gpsimd", 16),
            ("cc0", "gpsimd", 1), ("cc1", "gpsimd", 1),
            ("cc2", "gpsimd", 1), ("cc3", "gpsimd", 1),
            ("cb0", "gpsimd", 1), ("cb1", "gpsimd", 1),
            ("cb2", "gpsimd", 1), ("cb3", "gpsimd", 1),
        ]:
            kq = Kq(name, attr, step)
            kq.sem = ctx.enter_context(nc.semaphore("sem_" + name))
            self.q[name] = kq
        self.satisfied = {n: {} for n in self.q}
        self.maxwait = {n: 0 for n in self.q}
        self._names = {}
        self.pending = {}

    _cur = None

    def uname(self, base):
        i = self._names.get(base, 0)
        self._names[base] = i + 1
        return f"{base}_{i}" if i else base

    def sb(self, name, shape, dtype=F32, ctx=None):
        h = (ctx or self._cur or self.ctx).enter_context(
            self.nc.sbuf_tensor(self.uname(name), list(shape), dtype,
                                unsafe_allow_realloc=True))
        return Tile(h, name)

    def psum(self, name, shape, dtype=F32):
        h = self.ctx.enter_context(
            self.nc.psum_tensor(self.uname(name), list(shape), dtype,
                                unsafe_allow_realloc=True))
        return Tile(h, name)

    def _wait(self, eng, cons_q, tag):
        if tag is None:
            return
        pq, pn = tag
        if pn is None:
            raise RuntimeError("wait on unresolved pending tag")
        if pq.name == cons_q.name and pq.name == "pe":
            return
        sat = self.satisfied[cons_q.name]
        if sat.get(pq.name, 0) >= pn:
            return
        eng.wait_ge(pq.sem, pn * pq.step)
        sat[pq.name] = pn
        self.maxwait[pq.name] = max(self.maxwait[pq.name], pn)

    @staticmethod
    def _freeze(fn):
        import types
        if fn.__closure__:
            cells = tuple(types.CellType(c.cell_contents)
                          for c in fn.__closure__)
            return types.FunctionType(fn.__code__, fn.__globals__,
                                      fn.__name__, fn.__defaults__, cells)
        return fn

    def op(self, qname, emit, reads=(), writes=(), inc=True):
        emit = self._freeze(emit)
        q = self.q[qname]
        tags = []
        if q.is_dma and q.n > 0:
            tags.append((q, q.n))
        for t in reads:
            tags.append(t.last_w)
        for t in writes:
            tags.append(t.last_w)
            tags.extend(t.readers)
        holder = {}

        def body(eng):
            for tag in tags:
                self._wait(eng, q, tag)
            instr = emit(eng)
            if inc:
                instr.then_inc(q.sem, q.step)
            holder["i"] = instr

        self.pending.setdefault(q.attr, []).append(body)
        if inc:
            q.n += 1
            tag = (q, q.n)
        else:
            tag = (q, q.n + 1)
        for t in reads:
            t.readers.append(tag)
        for t in writes:
            t.last_w = tag
            t.readers = []
        return tag

    def flush(self):
        for attr, fns in self.pending.items():
            if not fns:
                continue
            def body(eng, fns=tuple(fns)):
                for fn in fns:
                    fn(eng)
            getattr(self.block, attr)(body)
            fns.clear()

    def dma(self, qname, out, in_, reads=(), writes=()):
        return self.op(qname, lambda e: e.dma_start(
            out=out, in_=in_), reads=reads, writes=writes)

    def raw(self, qname, fn):
        self.flush()
        q = self.q[qname]
        getattr(self.block, q.attr)(fn)

    def barrier_wait(self, qname, tags):
        q = self.q[qname]
        def body(eng):
            for tag in tags:
                self._wait(eng, q, tag)
        getattr(self.block, q.attr)(body)

    def phase_barrier(self):
        self.flush()
        counts = {n: q.n for n, q in self.q.items()}
        for cname, cq in self.q.items():
            def body(eng, cname=cname):
                for pname, pn in counts.items():
                    if pname == cname or pn == 0:
                        continue
                    self._wait(eng, self.q[cname], (self.q[pname], pn))
            getattr(self.block, cq.attr)(body)

    def finish(self, final_store_queue="gd"):
        self.flush()
        q = self.q[final_store_queue]
        n = q.n
        def body(eng):
            eng.wait_ge(q.sem, n * q.step)
        getattr(self.block, q.attr)(body)
        for name, kq in self.q.items():
            assert self.maxwait[name] <= kq.n, (
                f"wait beyond final count on {name}: "
                f"{self.maxwait[name]} > {kq.n}")



def alibi_slopes(n_heads=16):
    import math
    start = 2.0 ** (-2.0 ** (-(math.log2(n_heads) - 3)))
    return np.array([start * start ** i for i in range(n_heads)], np.float32)

def own_rows(hf):
    return (np.arange(S).reshape(NTF, 128)[hf::2]).reshape(-1)

def make_consts():
    c = {}
    c["ident"] = np.eye(128, dtype=bf16)
    c["identf32"] = np.eye(128, dtype=np.float32)
    kp = np.arange(128)[:, None]
    qp = np.arange(128)[None, :]
    c["tri_causal"] = np.where(kp > qp, NEG, 0.0).astype(np.float32)
    c["full_neg"] = np.full((128, 128), NEG, np.float32)
    c["zeros128"] = np.zeros((128, 128), np.float32)
    sl = alibi_slopes()
    ab = np.zeros((128, 16 * 32), np.float32)
    for h in range(16):
        for s in range(32):
            ab[:, h * 32 + s] = sl[h] * (128 * s + np.arange(128))
    c["alibik"] = ab
    for hf in range(2):
        rows = own_rows(hf).astype(np.float32)
        npq = np.zeros((1, 16 * HALF), np.float32)
        for h in range(16):
            npq[0, h * HALF:(h + 1) * HALF] = -sl[h] * rows
        c[f"negposq_{hf}"] = npq
    return c

CONSTS = make_consts()
SQRT_SCALE = float(1.0 / np.sqrt(np.sqrt(64.0)))



def emit_ln_tile(kb, xt, xn_out_ap, xn_out_tile, scratch, extra_rms=None,
                 skip_plain=False):
    s1, s2, m, v, r = (scratch[k] for k in ("s1", "s2", "m", "v", "r"))
    dummy = scratch["dummy"]
    kb.op("act", lambda e: e.activation(dummy.ap, xt.ap, AF.Identity,
                                        accum_out=s1.ap),
          reads=[xt], writes=[dummy, s1])
    kb.op("act", lambda e: e.activation(dummy.ap, xt.ap, AF.Square,
                                        accum_out=s2.ap),
          reads=[xt], writes=[dummy, s2])
    kb.op("dve", lambda e: e.tensor_scalar(m.ap, s1.ap, 1.0 / D, None,
                                           op0=ALU.mult),
          reads=[s1], writes=[m])
    kb.op("dve", lambda e: e.tensor_scalar(v.ap, s2.ap, 1.0 / D, None,
                                           op0=ALU.mult),
          reads=[s2], writes=[v])
    kb.op("dve", lambda e: e.scalar_tensor_tensor(v.ap, m.ap, m.ap, v.ap,
                                                  op0=ALU.mult,
                                                  op1=ALU.subtract),
          reads=[m, v], writes=[v])
    kb.op("dve", lambda e: e.tensor_scalar(v.ap, v.ap, -1.0, 1e-5,
                                           op0=ALU.mult, op1=ALU.add),
          reads=[v], writes=[v])
    kb.op("act", lambda e: e.activation(r.ap, v.ap, AF.Ln),
          reads=[v], writes=[r])
    kb.op("act", lambda e: e.activation(r.ap, r.ap, AF.Exp, scale=-0.5),
          reads=[r], writes=[r])
    if not skip_plain:
        kb.op("dve", lambda e: e.tensor_scalar(xn_out_ap, xt.ap, m.ap, r.ap,
                                               op0=ALU.subtract,
                                               op1=ALU.mult),
              reads=[xt, m, r], writes=[xn_out_tile])
    if extra_rms is not None:
        rms_ap, rms_tile, rms32_ap, rms32_tile = extra_rms
        ms2, r2 = scratch["ms2"], scratch["r2"]
        kb.op("dve", lambda e: e.tensor_scalar(ms2.ap, v.ap, -1e-5, None,
                                               op0=ALU.add),
              reads=[v], writes=[ms2])
        kb.op("dve", lambda e: e.reciprocal(r2.ap, v.ap),
              reads=[v], writes=[r2])
        kb.op("dve", lambda e: e.tensor_scalar(ms2.ap, ms2.ap, r2.ap, 1e-6,
                                               op0=ALU.mult, op1=ALU.add),
              reads=[ms2, r2], writes=[ms2])
        kb.op("act", lambda e: e.activation(ms2.ap, ms2.ap, AF.Ln),
              reads=[ms2], writes=[ms2])
        kb.op("act", lambda e: e.activation(ms2.ap, ms2.ap, AF.Exp,
                                            scale=-0.5),
              reads=[ms2], writes=[ms2])
        kb.op("dve", lambda e: e.tensor_scalar(r2.ap, r.ap, ms2.ap, None,
                                               op0=ALU.mult),
              reads=[r, ms2], writes=[r2])
        kb.op("dve", lambda e: e.tensor_scalar(rms_ap, xt.ap, m.ap, r2.ap,
                                               op0=ALU.subtract,
                                               op1=ALU.mult),
              reads=[xt, m, r2], writes=[rms_tile])
        if rms32_ap is not None:
            kb.op("dve", lambda e: e.tensor_scalar(rms32_ap, xt.ap, m.ap,
                                                   r2.ap, op0=ALU.subtract,
                                                   op1=ALU.mult),
                  reads=[xt, m, r2], writes=[rms32_tile])


def make_ln_scratch(kb, tag, dummy=None, extra=False, ctx=None):
    sc = {k: kb.sb(f"lns_{tag}_{k}", (128, 1), ctx=ctx) for k in
          ("s1", "s2", "m", "v", "r")}
    sc["dummy"] = dummy if dummy is not None else kb.sb(
        f"lns_{tag}_dummy", (128, 1024), ctx=ctx)
    if extra:
        sc["ms2"] = kb.sb(f"lns_{tag}_ms2", (128, 1), ctx=ctx)
        sc["r2"] = kb.sb(f"lns_{tag}_r2", (128, 1), ctx=ctx)
    return sc


def emit_transpose_tile(kb, src_ap, src_tile, dst_ap, dst_tile, ident, pst):
    kb.op("pe", lambda e: e.transpose(pst.h[0:128, 0:128], src_ap, ident.ap),
          reads=[src_tile, ident], writes=[pst])
    kb.op("dve", lambda e: e.tensor_copy(dst_ap, pst.h[0:128, 0:128]),
          reads=[pst], writes=[dst_tile])



def build_l0a(debug=False):
    nc = bass.Bass("TRN2", target_bir_lowering=False)
    xown = nc.dram_tensor("xown", [HALF, D], F32, kind="ExternalInput").ap()
    xfull = nc.dram_tensor("xfull", [S, D], F32, kind="ExternalInput").ap()
    wqkvT = nc.dram_tensor("wqkvT", [D, 3 * D], F32, kind="ExternalInput").ap()
    wprojT = nc.dram_tensor("wprojT", [D, D], F32, kind="ExternalInput").ap()
    identD = nc.dram_tensor("ident", [128, 128], BF16, kind="ExternalInput").ap()
    maskAD = nc.dram_tensor("maskA", [128, 128], F32, kind="ExternalInput").ap()
    maskBD = nc.dram_tensor("maskB", [128, 128], F32, kind="ExternalInput").ap()
    alibikD = nc.dram_tensor("alibik", [128, 512], F32, kind="ExternalInput").ap()
    negposqD = nc.dram_tensor("negposq", [1, 16 * HALF], F32,
                              kind="ExternalInput").ap()
    xoutD = nc.dram_tensor("xout", [HALF, D], F32, kind="ExternalOutput").ap()
    dbg = {}
    if debug:
        dbg["pt0"] = nc.dram_tensor("dbg_pt0", [128, 1536], F32,
                                    kind="ExternalOutput").ap()
        dbg["pt1"] = nc.dram_tensor("dbg_pt1", [128, 1536], F32,
                                    kind="ExternalOutput").ap()
        dbg["po0"] = nc.dram_tensor("dbg_po0", [128, 512], F32,
                                    kind="ExternalOutput").ap()
        dbg["oa0"] = nc.dram_tensor("dbg_oa0", [128, HALF], F32,
                                    kind="ExternalOutput").ap()
        dbg["qa0"] = nc.dram_tensor("dbg_qa0", [65, HALF], F32,
                                    kind="ExternalOutput").ap()
        dbg["ks0"] = nc.dram_tensor("dbg_ks0", [65, S], F32,
                                    kind="ExternalOutput").ap()
        dbg["kmaxn0"] = nc.dram_tensor("dbg_kmaxn0", [1, 1], F32,
                                       kind="ExternalOutput").ap()
        dbg["kall0"] = nc.dram_tensor("dbg_kall0", [128, S], F32,
                                      kind="ExternalOutput").ap()
        dbg["vall0"] = nc.dram_tensor("dbg_vall0", [128, 512], F32,
                                      kind="ExternalOutput").ap()
        dbg["xnow0"] = nc.dram_tensor("dbg_xnow0", [128, HALF], F32,
                                      kind="ExternalOutput").ap()

    with ExitStack() as ctx:
        kb = KB(nc, ctx)
        emit_l0a_body(kb, ctx, xown, xfull, wqkvT, wprojT, identD, maskAD,
                      maskBD, alibikD, negposqD, xoutD, dbg)
        kb.finish("gd")
    return nc


def emit_l0a_body(kb, ctx, xown, xfull, wqkvT, wprojT, identD, maskAD,
                  maskBD, alibikD, negposqD, xoutD, dbg=None, mode="l0a",
                  xmap=None):
    nc = kb.nc

    def ldxf(xt, g):
        if xmap is None:
            kb.dma("sd", xt.ap, xfull[g * 128:(g + 1) * 128, :],
                   writes=[xt])
        else:
            off = (g % 2) * HALF + (g // 2) * 128
            kb.dma("gd", xt.ap, xfull[off:off + 128, :], writes=[xt])
    ident = kb.sb("sb_ident", (128, 128), BF16)
    maskA = kb.sb("sb_maskA", (128, 128))
    maskB = kb.sb("sb_maskB", (128, 128))
    alibik = kb.sb("sb_alibik", (128, 512))
    ones64 = kb.sb("ones64", (64, 1), BF16)
    ones1x64 = kb.sb("ones1x64", (1, 64), BF16)
    m50 = kb.sb("m50", (128, 1))
    zz = kb.sb("zzb", (128, 1))
    dummy = kb.sb("lns_dummy", (128, 1024))
    kb.dma("sd", ident.ap, identD, writes=[ident])
    kb.dma("sd", maskA.ap, maskAD, writes=[maskA])
    kb.dma("sd", maskB.ap, maskBD, writes=[maskB])
    kb.dma("sd", alibik.ap, alibikD, writes=[alibik])
    kb.op("dve", lambda e: e.memset(ones64.ap, 1.0), writes=[ones64])
    kb.op("dve", lambda e: e.memset(ones1x64.ap, 1.0), writes=[ones1x64])
    kb.op("dve", lambda e: e.memset(m50.ap, -50.0), writes=[m50])
    kb.op("dve", lambda e: e.memset(zz.ap, 0.0), writes=[zz])

    OA = [kb.sb(f"oa{c}", (128, HALF), BF16) for c in range(NDC)]
    QF = [kb.sb(f"qf{c}", (128, HALF), BF16) for c in range(NDC)]

    with ExitStack() as ph:
        E = ph.enter_context
        pst = [Tile(E(_psum_r(nc, kb.uname("pst"), [128, 1024], BF16)))
               for _ in range(2)]
        psQ = [Tile(E(_psum_r(nc, kb.uname("psQ"), [128, 512], F32)))
               for _ in range(2)]
        xo = [kb.sb(f"xo{i}", (128, D), ctx=ph) for i in range(2)]
        xnt = [kb.sb(f"xnt{i}", (128, D), BF16, ctx=ph) for i in range(2)]
        xnow_fm = [kb.sb(f"xnowfm{c}", (128, HALF), BF16, ctx=ph)
                   for c in range(NDC)]
        wqs = kb.sb("wqs", (128, NDC * 1024), BF16, ctx=ph)
        lns = make_ln_scratch(kb, "own", dummy, ctx=ph)
        for c in range(NDC):
            kb.dma("gd", wqs.h[0:128, c * 1024:(c + 1) * 1024],
                   wqkvT[c * 128:(c + 1) * 128, 0:D], writes=[wqs])
        for j in range(NTOW):
            xt = xo[j % 2]
            kb.dma("sd", xt.ap, xown[j * 128:(j + 1) * 128, :], writes=[xt])
            xn = xnt[j % 2]
            emit_ln_tile(kb, xt, xn.ap, xn, lns)
            for c in range(NDC):
                emit_transpose_tile(
                    kb, xn.h[0:128, c * 128:(c + 1) * 128], xn,
                    xnow_fm[c].h[0:128, j * 128:(j + 1) * 128], xnow_fm[c],
                    ident, pst[(j * NDC + c) % 2])
        if dbg:
            kb.dma("gd", dbg["xnow0"], xnow_fm[0].ap, reads=[xnow_fm[0]])
        for tq in range(4):
            for dq in range(NDC):
                pq = psQ[dq % 2]
                for c in range(NDC):
                    kb.op("pe", lambda e, c=c, dq=dq, tq=tq, pq=pq: e.matmul(
                        pq.h[0:128, 0:512],
                        wqs.h[0:128, c * 1024 + dq * 128:
                              c * 1024 + dq * 128 + 128],
                        xnow_fm[c].h[0:128, tq * 512:(tq + 1) * 512],
                        start=(c == 0), stop=(c == NDC - 1)),
                        reads=[wqs, xnow_fm[c]], writes=[pq],
                        inc=(c == NDC - 1))
                kb.op("act", lambda e, pq=pq, dq=dq, tq=tq: e.activation(
                    QF[dq].h[0:128, tq * 512:(tq + 1) * 512],
                    pq.h[0:128, 0:512], AF.Identity, scale=SQRT_SCALE),
                    reads=[pq], writes=[QF[dq]])
        kb.phase_barrier()

    for hg in range(2):
        with ExitStack() as ph:
            E = ph.enter_context
            K_all = [kb.sb(f"kall{i}", (128, S), BF16, ctx=ph)
                     for i in range(4)]
            V_all = [kb.sb(f"vall{g}", (128, 512), BF16, ctx=ph)
                     for g in range(NTF)]
            with ExitStack() as ph2:
                E2 = ph2.enter_context
                pst = [Tile(E2(_psum_r(nc, kb.uname("pst"), [128, 1024],
                                              BF16))) for _ in range(2)]
                psK = [Tile(E2(_psum_r(nc, kb.uname("psK"), [128, 512],
                                              F32))) for _ in range(2)]
                psV = [Tile(E2(_psum_r(nc, kb.uname("psV"), [128, 512],
                                              F32))) for _ in range(2)]
                xf = [kb.sb(f"xf{i}", (128, D), ctx=ph2) for i in range(2)]
                xnt = [kb.sb(f"xnt{i}", (128, D), BF16, ctx=ph2)
                       for i in range(2)]
                xnf = [[kb.sb(f"xnf_{c}", (128, 512), BF16, ctx=ph2)
                        for c in range(NDC)]]
                wk = kb.sb("wk", (128, NDC * 512), BF16, ctx=ph2)
                wv = kb.sb("wv", (128, NDC * 512), BF16, ctx=ph2)
                lns = make_ln_scratch(kb, "full", dummy, ctx=ph2)
                for c in range(NDC):
                    kb.dma("gd", wk.h[0:128, c * 512:(c + 1) * 512],
                           wqkvT[c * 128:(c + 1) * 128,
                                 D + hg * 512:D + hg * 512 + 512],
                           writes=[wk])
                    kb.dma("gd", wv.h[0:128, c * 512:(c + 1) * 512],
                           wqkvT[c * 128:(c + 1) * 128,
                                 2 * D + hg * 512:2 * D + hg * 512 + 512],
                           writes=[wv])
                for kg in range(8):
                    fmb = xnf[0]
                    for t in range(4):
                        g = kg * 4 + t
                        xt = xf[g % 2]
                        ldxf(xt, g)
                        xn = xnt[g % 2]
                        emit_ln_tile(kb, xt, xn.ap, xn, lns)
                        for c in range(NDC):
                            emit_transpose_tile(
                                kb, xn.h[0:128, c * 128:(c + 1) * 128], xn,
                                fmb[c].h[0:128, t * 128:(t + 1) * 128],
                                fmb[c], ident, pst[(g * NDC + c) % 2])
                    for oc in range(4):
                        pk = psK[oc % 2]
                        for c in range(NDC):
                            kb.op("pe", lambda e, c=c, oc=oc, pk=pk, fmb=fmb:
                                  e.matmul(pk.h[0:128, 0:512],
                                           wk.h[0:128, c * 512 + oc * 128:
                                                c * 512 + oc * 128 + 128],
                                           fmb[c].h[0:128, 0:512],
                                           start=(c == 0),
                                           stop=(c == NDC - 1)),
                                  reads=[wk, fmb[c]], writes=[pk],
                                  inc=(c == NDC - 1))
                        kb.op("act", lambda e, pk=pk, oc=oc, kg=kg:
                              e.activation(
                                  K_all[oc].h[0:128, kg * 512:(kg + 1) * 512],
                                  pk.h[0:128, 0:512],
                                  AF.Identity, scale=SQRT_SCALE),
                              reads=[pk], writes=[K_all[oc]])
                    for t in range(4):
                        g = kg * 4 + t
                        pv = psV[t % 2]
                        for c in range(NDC):
                            kb.op("pe", lambda e, c=c, t=t, pv=pv, fmb=fmb:
                                  e.matmul(pv.h[0:128, 0:512],
                                           fmb[c].h[0:128,
                                                    t * 128:(t + 1) * 128],
                                           wv.h[0:128, c * 512:(c + 1) * 512],
                                           start=(c == 0),
                                           stop=(c == NDC - 1)),
                                  reads=[wv, fmb[c]], writes=[pv],
                                  inc=(c == NDC - 1))
                        kb.op("dve", lambda e, pv=pv, g=g: e.tensor_copy(
                            V_all[g].ap, pv.h[0:128, 0:512]),
                            reads=[pv], writes=[V_all[g]])
                if dbg and hg == 0:
                    kb.dma("gd", dbg["kall0"], K_all[0].ap, reads=[K_all[0]])
                    kb.dma("gd", dbg["vall0"], V_all[0].ap, reads=[V_all[0]])
                kb.phase_barrier()

            with ExitStack() as ph3:
                E3 = ph3.enter_context
                psS = [Tile(E3(_psum_r(nc, kb.uname("psS"),
                                              [128, 3 * 512], F32)))
                       for _ in range(2)]
                psO = [Tile(E3(_psum_r(nc, kb.uname("psO"), [128, 512],
                                              F32))) for _ in range(2)]
                QS = [kb.sb(f"qs{i}", (65, HALF), BF16, ctx=ph3)
                      for i in range(2)]
                KS = [kb.sb(f"ksx{i}", (65, S), BF16, ctx=ph3)
                      for i in range(2)]
                VS = [kb.sb(f"vsx{i}", (128, NTF * 65), BF16, ctx=ph3)
                      for i in range(2)]
                kmaxn = kb.sb("kmaxn", (1, 1), ctx=ph3)
                ksq = kb.sb("ksq", (64, S), BF16, ctx=ph3)
                qsq = kb.sb("qsq", (64, 512), BF16, ctx=ph3)
                mrow = kb.sb("mrow", (1, 512), ctx=ph3)
                PT = [kb.sb(f"pt{i}", (128, 3 * 512), BF16, ctx=ph3)
                      for i in range(2)]
                linv = kb.sb("linv", (1, 512), ctx=ph3)
                linb = kb.sb("linb", (1, 512), BF16, ctx=ph3)
                osb = kb.sb("osb", (64, 512), BF16, ctx=ph3)
                kred = kb.sb("kred", (1, 1), ctx=ph3)
                npq = [kb.sb(f"npq{i}", (1, HALF), ctx=ph3)
                       for i in range(2)]
                for hh in range(8):
                    h = hg * 8 + hh
                    ks = KS[hh % 2]
                    vs = VS[hh % 2]
                    qa = QS[hh % 2]
                    np_h = npq[hh % 2]
                    kb.dma("sd", np_h.ap,
                           negposqD[0:1, h * HALF:(h + 1) * HALF],
                           writes=[np_h])
                    kb.op("dve", lambda e, ks=ks, hh=hh: e.tensor_copy(
                        ks.h[0:64, 0:S],
                        K_all[hh // 2].h[(hh % 2) * 64:(hh % 2) * 64 + 64,
                                         0:S]),
                        reads=[K_all[hh // 2]], writes=[ks])
                    kb.op("dve", lambda e, ks=ks: e.memset(
                        ks.h[64:65, 0:S], 1.0), writes=[ks])
                    kb.op("dve", lambda e, vs=vs: e.memset(vs.ap, 1.0),
                          writes=[vs])
                    for g in range(NTF):
                        kb.op("dve", lambda e, vs=vs, g=g, hh=hh:
                              e.tensor_copy(
                                  vs.h[0:128, g * 65:g * 65 + 64],
                                  V_all[g].h[0:128, hh * 64:hh * 64 + 64]),
                              reads=[V_all[g]], writes=[vs])
                    c, r0 = h // 2, (h % 2) * 64
                    kb.op("dve", lambda e, qa=qa, c=c, r0=r0:
                          e.tensor_copy(
                              qa.h[0:64, 0:HALF],
                              QF[c].h[r0:r0 + 64, 0:HALF]),
                          reads=[QF[c]], writes=[qa])
                    kb.op("dve", lambda e, ks=ks: e.tensor_tensor(
                        ksq.ap, ks.h[0:64, 0:S], ks.h[0:64, 0:S],
                        op=ALU.mult), reads=[ks], writes=[ksq])
                    for kc in range(8):
                        psn = psS[0].sub(0)
                        kb.op("pe", lambda e, kc=kc, psn=psn: e.matmul(
                            psn.h[0:1, 0:512], ones64.ap,
                            ksq.h[0:64, kc * 512:(kc + 1) * 512],
                            start=True, stop=True),
                            reads=[ones64, ksq], writes=[psn])
                        if kc == 0:
                            kb.op("dve", lambda e, psn=psn: e.tensor_reduce(
                                kmaxn.ap, psn.h[0:1, 0:512], axis=AX.X,
                                op=ALU.max), reads=[psn], writes=[kmaxn])
                        else:
                            kb.op("dve", lambda e, psn=psn: e.tensor_reduce(
                                kred.ap, psn.h[0:1, 0:512], axis=AX.X,
                                op=ALU.max), reads=[psn], writes=[kred])
                            kb.op("dve", lambda e: e.tensor_tensor(
                                kmaxn.ap, kmaxn.ap, kred.ap, op=ALU.max),
                                reads=[kred, kmaxn], writes=[kmaxn])
                    kb.op("dve", lambda e: e.tensor_scalar(
                        kmaxn.ap, kmaxn.ap, -0.5, None, op0=ALU.mult),
                        reads=[kmaxn], writes=[kmaxn])
                    for qc in range(4):
                        kb.op("dve", lambda e, qa=qa, qc=qc: e.tensor_tensor(
                            qsq.ap, qa.h[0:64, qc * 512:(qc + 1) * 512],
                            qa.h[0:64, qc * 512:(qc + 1) * 512],
                            op=ALU.mult), reads=[qa], writes=[qsq])
                        psq = psS[0].sub(0)
                        kb.op("pe", lambda e, psq=psq: e.matmul(
                            psq.h[0:1, 0:512], ones64.ap, qsq.ap,
                            start=True, stop=True),
                            reads=[ones64, qsq], writes=[psq])
                        kb.op("dve", lambda e, psq=psq, h=h, qc=qc:
                              e.scalar_tensor_tensor(
                                  mrow.ap, psq.h[0:1, 0:512], 0.5,
                                  np_h.h[0:1, qc * 512:qc * 512 + 512],
                                  op0=ALU.mult, op1=ALU.subtract),
                              reads=[psq, np_h], writes=[mrow])
                        kb.op("dve", lambda e, qa=qa, qc=qc: e.tensor_scalar(
                            qa.h[64:65, qc * 512:(qc + 1) * 512], mrow.ap,
                            -1.0, kmaxn.ap, op0=ALU.mult, op1=ALU.add),
                            reads=[mrow, kmaxn], writes=[qa])
                        nslot = 8 * qc + 8 if mode == "l0a" else NTF
                        for s in range(nslot):
                            sbt = psS[(s // 3) % 2]
                            soff = (s % 3) * 512
                            sb3 = sbt.sub(soff)
                            kb.op("pe", lambda e, s=s, qc=qc, sb3=sb3,
                                  soff=soff, ks=ks, qa=qa: e.matmul(
                                      sb3.h[0:128, soff:soff + 512],
                                      ks.h[0:65, s * 128:(s + 1) * 128],
                                      qa.h[0:65, qc * 512:(qc + 1) * 512],
                                      start=True, stop=True),
                                  reads=[ks, qa], writes=[sb3])
                            ds = s - 8 * qc
                            if 0 <= ds <= 7:
                                r = ds // 2
                                mt = maskA if ds % 2 == 0 else maskB
                                kb.op("dve", lambda e, sb3=sb3, soff=soff,
                                      r=r, mt=mt: e.tensor_tensor(
                                          sb3.h[0:128, soff + r * 128:
                                                soff + r * 128 + 128],
                                          sb3.h[0:128, soff + r * 128:
                                                soff + r * 128 + 128],
                                          mt.ap, op=ALU.add),
                                      reads=[sb3, mt], writes=[sb3])
                            ptt = PT[(s // 3) % 2]
                            ptoff = (s % 3) * 512
                            pt = ptt.sub(ptoff)
                            if mode == "l0a":
                                deadr = max(0, ds // 2) if ds >= 0 else 0
                                if deadr > 0:
                                    kb.op("dve", lambda e, pt=pt,
                                          ptoff=ptoff, deadr=deadr:
                                          e.memset(
                                              pt.h[0:128, ptoff:
                                                   ptoff + deadr * 128],
                                              0.0), writes=[pt])
                                kb.op("act", lambda e, sb3=sb3, soff=soff,
                                      pt=pt, ptoff=ptoff, deadr=deadr, h=h,
                                      s=s: e.activation(
                                          pt.h[0:128, ptoff + deadr * 128:
                                               ptoff + 512],
                                          sb3.h[0:128, soff + deadr * 128:
                                                soff + 512],
                                          AF.Exp,
                                          bias=alibik.h[0:128, h * 32 + s:
                                                        h * 32 + s + 1]),
                                      reads=[sb3, alibik], writes=[pt])
                            else:
                                if ds < 0:
                                    parts = [(0, 512, 0.0)]
                                elif ds <= 7:
                                    rb = ds // 2
                                    parts = []
                                    if rb > 0:
                                        parts.append((0, rb * 128, -50.0))
                                    parts.append((rb * 128, 512, 0.0))
                                else:
                                    parts = [(0, 512, -50.0)]
                                for (lo, hi, bia) in parts:
                                    bt = m50 if bia == -50.0 else zz
                                    kb.op("act", lambda e, sb3=sb3,
                                          soff=soff, pt=pt, ptoff=ptoff,
                                          lo=lo, hi=hi, bt=bt:
                                          e.activation(
                                              pt.h[0:128, ptoff + lo:
                                                   ptoff + hi],
                                              sb3.h[0:128, soff + lo:
                                                    soff + hi],
                                              AF.Exp, bias=bt.ap),
                                          reads=[sb3, bt], writes=[pt])
                            po = psO[(qc * 8 + hh) % 2]
                            kb.op("pe", lambda e, s=s, pt=pt, ptoff=ptoff,
                                  po=po, vs=vs, nslot=nslot: e.matmul(
                                      po.h[0:65, 0:512],
                                      vs.h[0:128, s * 65:s * 65 + 65],
                                      pt.h[0:128, ptoff:ptoff + 512],
                                      start=(s == 0), stop=(s == nslot - 1)),
                                  reads=[vs, pt], writes=[po],
                                  inc=(s == nslot - 1))
                        po = psO[(qc * 8 + hh) % 2]
                        kb.op("dve", lambda e, po=po: e.reciprocal(
                            linv.ap, po.h[64:65, 0:512]),
                            reads=[po], writes=[linv])
                        kb.op("dve", lambda e: e.tensor_copy(
                            linb.ap, linv.ap), reads=[linv], writes=[linb])
                        psb = psS[0].sub(1024)
                        kb.op("pe", lambda e, psb=psb: e.matmul(
                            psb.h[0:64, 1024:1536], ones1x64.ap, linb.ap,
                            start=True, stop=True),
                            reads=[ones1x64, linb], writes=[psb])
                        kb.op("act", lambda e, po=po: e.activation(
                            osb.ap, po.h[0:64, 0:512], AF.Identity),
                            reads=[po], writes=[osb])
                        c2, r2 = h // 2, (h % 2) * 64
                        kb.op("dve", lambda e, psb=psb, c2=c2, r2=r2, qc=qc:
                              e.tensor_tensor(
                                  OA[c2].h[r2:r2 + 64,
                                           qc * 512:(qc + 1) * 512],
                                  osb.ap, psb.h[0:64, 1024:1536],
                                  op=ALU.mult),
                              reads=[osb, psb], writes=[OA[c2]])
                        if dbg and h == 0 and qc == 3:
                            kb.dma("gd", dbg["qa0"], qa.ap, reads=[qa])
                            kb.dma("gd", dbg["ks0"], ks.ap, reads=[ks])
                            kb.dma("gd", dbg["kmaxn0"], kmaxn.ap,
                                   reads=[kmaxn])
                        if dbg and h == 0 and qc == 0:
                            podump = kb.sb("podump", (128, 512), ctx=ph3)
                            kb.op("dve", lambda e, po=po: e.tensor_copy(
                                podump.ap, po.h[0:128, 0:512]),
                                reads=[po], writes=[podump])
                            kb.dma("gd", dbg["po0"], podump.ap,
                                   reads=[podump])
                            kb.dma("gd", dbg["pt0"], PT[0].ap,
                                   reads=[PT[0]])
                            kb.dma("gd", dbg["pt1"], PT[1].ap,
                                   reads=[PT[1]])
                kb.phase_barrier()
            kb.phase_barrier()

    if dbg:
        kb.dma("gd", dbg["oa0"], OA[0].ap, reads=[OA[0]])
    with ExitStack() as ph:
        E = ph.enter_context
        psY = [Tile(E(_psum_r(nc, kb.uname("psY"), [128, 512], F32)))
               for _ in range(4)]
        wpj = kb.sb("wpj", (128, NDC * 1024), BF16, ctx=ph)
        for c in range(NDC):
            kb.dma("gd", wpj.h[0:128, c * 1024:(c + 1) * 1024],
                   wprojT[c * 128:(c + 1) * 128, :], writes=[wpj])
        xr = [kb.sb(f"xr{i}", (128, D), ctx=ph) for i in range(2)]
        yout = [kb.sb(f"yout{i}", (128, D), ctx=ph) for i in range(2)]
        for j in range(NTOW):
            xt = xr[j % 2]
            kb.dma("sd", xt.ap, xown[j * 128:(j + 1) * 128, :], writes=[xt])
            yo = yout[j % 2]
            for o2 in range(2):
                py = psY[(j * 2 + o2) % 4]
                for c in range(NDC):
                    kb.op("pe", lambda e, c=c, o2=o2, j=j, py=py: e.matmul(
                        py.h[0:128, 0:512],
                        OA[c].h[0:128, j * 128:(j + 1) * 128],
                        wpj.h[0:128, c * 1024 + o2 * 512:
                              c * 1024 + o2 * 512 + 512],
                        start=(c == 0), stop=(c == NDC - 1)),
                        reads=[OA[c], wpj], writes=[py],
                        inc=(c == NDC - 1))
                kb.op("dve", lambda e, py=py, yo=yo, xt=xt, o2=o2:
                      e.tensor_tensor(
                          yo.h[0:128, o2 * 512:o2 * 512 + 512],
                          xt.h[0:128, o2 * 512:o2 * 512 + 512],
                          py.h[0:128, 0:512], op=ALU.add),
                      reads=[py, xt], writes=[yo])
            kb.dma("gd", xoutD[j * 128:(j + 1) * 128, :], yo.ap,
                   reads=[yo])
        kb.phase_barrier()


def build_l1b(debug=False):
    nc = bass.Bass("TRN2", target_bir_lowering=False)
    xown = nc.dram_tensor("xown", [HALF, D], F32, kind="ExternalInput").ap()
    xfull = nc.dram_tensor("xfull", [S, D], F32, kind="ExternalInput").ap()
    wqkvT = nc.dram_tensor("wqkvT", [D, 3 * D], F32, kind="ExternalInput").ap()
    wprojT = nc.dram_tensor("wprojT", [D, D], F32, kind="ExternalInput").ap()
    identD = nc.dram_tensor("ident", [128, 128], BF16, kind="ExternalInput").ap()
    maskAD = nc.dram_tensor("maskA", [128, 128], F32, kind="ExternalInput").ap()
    maskBD = nc.dram_tensor("maskB", [128, 128], F32, kind="ExternalInput").ap()
    alibikD = nc.dram_tensor("alibik", [128, 512], F32, kind="ExternalInput").ap()
    negposqD = nc.dram_tensor("negposq", [1, 16 * HALF], F32,
                              kind="ExternalInput").ap()
    xoutD = nc.dram_tensor("xout", [HALF, D], F32, kind="ExternalOutput").ap()
    with ExitStack() as ctx:
        kb = KB(nc, ctx)
        emit_l0a_body(kb, ctx, xown, xfull, wqkvT, wprojT, identD, maskAD,
                      maskBD, alibikD, negposqD, xoutD, None, mode="l1b")
        kb.finish("gd")
    return nc



def build_l0ffn():
    nc = bass.Bass("TRN2", target_bir_lowering=False)
    xown = nc.dram_tensor("xown", [HALF, D], F32, kind="ExternalInput").ap()
    gate_wT = nc.dram_tensor("gate_wT", [D, 8], F32, kind="ExternalInput").ap()
    rmsr = nc.dram_tensor("rmsr", [128, 8], F32, kind="ExternalInput").ap()
    w1D = nc.dram_tensor("w1", [8, D, 4096], F32, kind="ExternalInput").ap()
    w2D = nc.dram_tensor("w2", [8, 4096, D], F32, kind="ExternalInput").ap()
    b1rD = nc.dram_tensor("b1r", [128, 256], F32, kind="ExternalInput").ap()
    b2D = nc.dram_tensor("b2", [8, D], F32, kind="ExternalInput").ap()
    identD = nc.dram_tensor("ident", [128, 128], BF16, kind="ExternalInput").ap()
    identf32D = nc.dram_tensor("identf32", [128, 128], F32,
                               kind="ExternalInput").ap()
    xoutD = nc.dram_tensor("xout", [HALF, D], F32, kind="ExternalOutput").ap()
    with ExitStack() as ctx:
        kb = KB(nc, ctx)
        emit_l0ffn_body(kb, ctx, xown, gate_wT, rmsr, w1D, w2D, b1rD, b2D,
                        identD, identf32D, xoutD)
        kb.finish("gd")
    return nc


def emit_l0ffn_body(kb, ctx, xown, gate_wT, rmsr, w1D, w2D, b1rD, b2D,
                    identD, identf32D, xoutD):
    nc = kb.nc
    NF = 4096 // 128
    ident = kb.sb("sb_ident", (128, 128), BF16)
    rms = kb.sb("sb_rms", (128, 8))
    b1r = kb.sb("sb_b1r", (128, 256))
    gwr = kb.sb("sb_gwr", (128, 8 * NDC))
    wts = kb.sb("wts", (128, 8 * NTOW))
    xnw = [kb.sb(f"xnw{c}", (128, HALF), BF16) for c in range(NDC)]
    outb = kb.sb("outb", (128, NTOW * D), BF16)
    idf32 = kb.sb("sb_idf32", (128, 128))
    kb.dma("sd", ident.ap, identD, writes=[ident])
    kb.dma("sd", idf32.ap, identf32D, writes=[idf32])
    kb.dma("sd", rms.ap, rmsr, writes=[rms])
    kb.dma("sd", b1r.ap, b1rD, writes=[b1r])

    with ExitStack() as ph:
        E = ph.enter_context
        pst = [Tile(E(_psum_r(nc, kb.uname("pst"), [128, 1024], BF16)))
               for _ in range(2)]
        ps32 = [Tile(E(_psum_r(nc, kb.uname("ps32"), [128, 512], F32)))
                for _ in range(2)]
        psR = Tile(E(_psum_r(nc, kb.uname("psR"), [128, 512], F32)))
        xo = [kb.sb(f"xo{i}", (128, D), ctx=ph) for i in range(2)]
        xnb = [kb.sb(f"xnb{i}", (128, D), BF16, ctx=ph) for i in range(2)]
        xn32 = [kb.sb(f"xn32{i}", (128, D), ctx=ph) for i in range(2)]
        xf32 = [kb.sb(f"xf32{i}", (128, D), ctx=ph) for i in range(2)]
        lmax = kb.sb("lmax", (128, 1), ctx=ph)
        nlmax = kb.sb("nlmax", (128, 1), ctx=ph)
        expl = kb.sb("expl", (128, 8), ctx=ph)
        lsum = kb.sb("lsum", (128, 1), ctx=ph)
        rcp = kb.sb("rcp", (128, 1), ctx=ph)
        msk = kb.sb("msk", (128, 8), ctx=ph)
        lns = make_ln_scratch(kb, "ffn", None, extra=True, ctx=ph)
        for c in range(NDC):
            kb.dma("sd", gwr.h[0:128, 8 * c:8 * c + 8],
                   gate_wT[c * 128:(c + 1) * 128, :], writes=[gwr])
        for c in range(NDC):
            kb.op("dve", lambda e, c=c: e.tensor_scalar(
                gwr.h[0:128, 8 * c:8 * c + 8],
                gwr.h[0:128, 8 * c:8 * c + 8],
                rms.h[0:128, c:c + 1], None, op0=ALU.mult),
                reads=[gwr, rms], writes=[gwr])
        for j in range(NTOW):
            xt = xo[j % 2]
            kb.dma("sd", xt.ap, xown[j * 128:(j + 1) * 128, :], writes=[xt])
            xb_ = xnb[j % 2]
            x3 = xn32[j % 2]
            emit_ln_tile(kb, xt, None, None, lns,
                         extra_rms=(xb_.ap, xb_, x3.ap, x3),
                         skip_plain=True)
            xf = xf32[j % 2]
            for c in range(NDC):
                kb.op("pe", lambda e, xb_=xb_, c=c, j=j: e.transpose(
                    pst[(j * NDC + c) % 2].h[0:128, 0:128],
                    xb_.h[0:128, c * 128:(c + 1) * 128], ident.ap),
                    reads=[xb_, ident], writes=[pst[(j * NDC + c) % 2]])
                kb.op("dve", lambda e, c=c, j=j: e.tensor_scalar(
                    xnw[c].h[0:128, j * 128:(j + 1) * 128],
                    pst[(j * NDC + c) % 2].h[0:128, 0:128],
                    rms.h[0:128, c:c + 1], None, op0=ALU.mult),
                    reads=[pst[(j * NDC + c) % 2], rms], writes=[xnw[c]])
                kb.op("pe", lambda e, x3=x3, c=c, j=j: e.transpose(
                    ps32[c % 2].h[0:128, 0:128],
                    x3.h[0:128, c * 128:(c + 1) * 128], idf32.ap),
                    reads=[x3, idf32], writes=[ps32[c % 2]])
                kb.op("dve", lambda e, xf=xf, c=c: e.tensor_copy(
                    xf.h[0:128, c * 128:(c + 1) * 128],
                    ps32[c % 2].h[0:128, 0:128]),
                    reads=[ps32[c % 2]], writes=[xf])
            for c in range(NDC):
                kb.op("pe", lambda e, xf=xf, c=c: e.matmul(
                    psR.h[0:128, 0:8],
                    xf.h[0:128, c * 128:(c + 1) * 128],
                    gwr.h[0:128, 8 * c:8 * c + 8],
                    start=(c == 0), stop=(c == NDC - 1)),
                    reads=[xf, gwr], writes=[psR], inc=(c == NDC - 1))
            kb.op("dve", lambda e: e.tensor_reduce(
                lmax.ap, psR.h[0:128, 0:8], axis=AX.X, op=ALU.max),
                reads=[psR], writes=[lmax])
            kb.op("dve", lambda e: e.tensor_scalar(
                nlmax.ap, lmax.ap, -1.0, None, op0=ALU.mult),
                reads=[lmax], writes=[nlmax])
            kb.op("act", lambda e: e.activation(
                expl.ap, psR.h[0:128, 0:8], AF.Exp, bias=nlmax.ap,
                accum_out=lsum.ap), reads=[psR, nlmax],
                writes=[expl, lsum])
            kb.op("dve", lambda e: e.reciprocal(rcp.ap, lsum.ap),
                  reads=[lsum], writes=[rcp])
            kb.op("dve", lambda e: e.tensor_scalar(
                msk.ap, psR.h[0:128, 0:8], lmax.ap, None, op0=ALU.is_ge),
                reads=[psR, lmax], writes=[msk])
            kb.op("dve", lambda e, j=j: e.tensor_scalar(
                wts.h[0:128, 8 * j:8 * j + 8], msk.ap, rcp.ap, None,
                op0=ALU.mult), reads=[msk, rcp], writes=[wts])
        kb.phase_barrier()

    with ExitStack() as ph:
        E = ph.enter_context
        psU = [Tile(E(_psum_r(nc, kb.uname("psU"), [128, 512], F32)))
               for _ in range(2)]
        psD = [Tile(E(_psum_r(nc, kb.uname("psD"), [128, 512], F32)))
               for _ in range(4)]
        w1s = [kb.sb(f"w1s{c}", (128, 4096), BF16, ctx=ph)
               for c in range(NDC)]
        w2s = [kb.sb(f"w2s{i}", (128, 512), BF16, ctx=ph) for i in range(2)]
        hfm = kb.sb("hfm", (128, NF * 512), BF16, ctx=ph)
        b2s = kb.sb("b2s", (1, D), BF16, ctx=ph)
        ones1 = kb.sb("ones1r", (1, 128), BF16, ctx=ph)
        kb.op("dve", lambda e: e.memset(ones1.ap, 1.0), writes=[ones1])
        kb.op("dve", lambda e: e.memset(outb.ap, 0.0), writes=[outb])
        for ex in range(8):
            for c in range(NDC):
                kb.dma("gd", w1s[c].ap, w1D[ex, c * 128:(c + 1) * 128, :],
                       writes=[w1s[c]])
            kb.dma("gd", b2s.ap, b2D[ex:ex + 1, :], writes=[b2s])
            for tq in range(4):
                for fc in range(NF):
                    pu = psU[fc % 2]
                    for c in range(NDC):
                        kb.op("pe", lambda e, c=c, fc=fc, tq=tq, pu=pu:
                              e.matmul(pu.h[0:128, 0:512],
                                       w1s[c].h[0:128, fc * 128:
                                                (fc + 1) * 128],
                                       xnw[c].h[0:128, tq * 512:
                                                (tq + 1) * 512],
                                       start=(c == 0), stop=(c == NDC - 1)),
                              reads=[w1s[c], xnw[c]], writes=[pu],
                              inc=(c == NDC - 1))
                    kb.op("act", lambda e, pu=pu, fc=fc, ex=ex: e.activation(
                        hfm.h[0:128, fc * 512:(fc + 1) * 512],
                        pu.h[0:128, 0:512], AF.Silu,
                        bias=b1r.h[0:128, ex * 32 + fc:ex * 32 + fc + 1]),
                        reads=[pu, b1r], writes=[hfm])
                for dh in range(2):
                    for t4 in range(4):
                        pd = psD[t4]
                        kb.op("pe", lambda e, pd=pd, dh=dh: e.matmul(
                            pd.h[0:128, 0:512], ones1.ap,
                            b2s.h[0:1, dh * 512:dh * 512 + 512],
                            start=True, stop=False),
                            reads=[ones1, b2s], writes=[pd], inc=False)
                    for fc in range(NF):
                        w2t = w2s[fc % 2]
                        kb.dma("gd", w2t.ap,
                               w2D[ex, fc * 128:(fc + 1) * 128,
                                   dh * 512:dh * 512 + 512],
                               writes=[w2t])
                        for t4 in range(4):
                            pd = psD[t4]
                            kb.op("pe", lambda e, pd=pd, w2t=w2t, fc=fc,
                                  t4=t4: e.matmul(
                                      pd.h[0:128, 0:512],
                                      hfm.h[0:128, fc * 512 + t4 * 128:
                                            fc * 512 + t4 * 128 + 128],
                                      w2t.h[0:128, 0:512],
                                      start=False, stop=(fc == NF - 1)),
                                  reads=[hfm, w2t], writes=[pd],
                                  inc=(t4 == 3))
                    for t4 in range(4):
                        j = tq * 4 + t4
                        pd = psD[t4]
                        kb.op("dve", lambda e, pd=pd, j=j, dh=dh, ex=ex:
                              e.scalar_tensor_tensor(
                                  outb.h[0:128, j * 1024 + dh * 512:
                                         j * 1024 + dh * 512 + 512],
                                  pd.h[0:128, 0:512],
                                  wts.h[0:128, 8 * j + ex:8 * j + ex + 1],
                                  outb.h[0:128, j * 1024 + dh * 512:
                                         j * 1024 + dh * 512 + 512],
                                  op0=ALU.mult, op1=ALU.add),
                              reads=[pd, wts, outb], writes=[outb])
        kb.phase_barrier()

    with ExitStack() as ph:
        xr = [kb.sb(f"xr{i}", (128, D), ctx=ph) for i in range(2)]
        yout = [kb.sb(f"yo{i}", (128, D), ctx=ph) for i in range(2)]
        for j in range(NTOW):
            xt = xr[j % 2]
            kb.dma("sd", xt.ap, xown[j * 128:(j + 1) * 128, :], writes=[xt])
            yo = yout[j % 2]
            kb.op("dve", lambda e, xt=xt, yo=yo, j=j: e.tensor_tensor(
                yo.ap, xt.ap, outb.h[0:128, j * 1024:(j + 1) * 1024],
                op=ALU.add), reads=[xt, outb], writes=[yo])
            kb.dma("gd", xoutD[j * 128:(j + 1) * 128, :], yo.ap, reads=[yo])
        kb.phase_barrier()



def build_l0b():
    nc = bass.Bass("TRN2", target_bir_lowering=False)
    xfull = nc.dram_tensor("xfull", [S, D], F32, kind="ExternalInput").ap()
    waT = nc.dram_tensor("waT", [D, D], F32, kind="ExternalInput").ap()
    wbT = nc.dram_tensor("wbT", [D, D], F32, kind="ExternalInput").ap()
    wcT = nc.dram_tensor("wcT", [D, D], F32, kind="ExternalInput").ap()
    basD = nc.dram_tensor("bas", [128, 8 * 3], F32, kind="ExternalInput").ap()
    identD = nc.dram_tensor("ident", [128, 128], BF16, kind="ExternalInput").ap()
    xoutD = nc.dram_tensor("xout", [S, D], F32, kind="ExternalOutput").ap()
    with ExitStack() as ctx:
        kb = KB(nc, ctx)
        emit_l0b_body(kb, ctx, xfull, waT, wbT, wcT, basD, identD, xoutD)
        kb.finish("gd")
    return nc


def emit_l0b_body(kb, ctx, xfull, waT, wbT, wcT, basD, identD, xoutD,
                  xmap=None, fused=None):
    nc = kb.nc

    def ldxf(xt, g):
        if xmap is None:
            kb.dma("sd", xt.ap, xfull[g * 128:(g + 1) * 128, :],
                   writes=[xt])
        else:
            off = (g % 2) * HALF + (g // 2) * 128
            kb.dma("gd", xt.ap, xfull[off:off + 128, :], writes=[xt])
    ident = kb.sb("sb_ident", (128, 128), BF16)
    bas = kb.sb("sb_bas", (128, 24))
    kb.dma("sd", ident.ap, identD, writes=[ident])
    kb.dma("sd", bas.ap, basD, writes=[bas])
    xnf = [kb.sb(f"xnf{c}", (128, S), BF16) for c in range(NDC)]
    ytm = kb.sb("ytm", (128, NTF * D), BF16)

    with ExitStack() as ph:
        E = ph.enter_context
        pst = [Tile(E(_psum_r(nc, kb.uname("pst"), [128, 1024], BF16)))
               for _ in range(2)]
        xo = [kb.sb(f"xo{i}", (128, D), ctx=ph) for i in range(2)]
        xnt = [kb.sb(f"xnt{i}", (128, D), BF16, ctx=ph) for i in range(2)]
        lns = make_ln_scratch(kb, "l0b", None, ctx=ph)
        for g in range(NTF):
            xt = xo[g % 2]
            ldxf(xt, g)
            xn = xnt[g % 2]
            emit_ln_tile(kb, xt, xn.ap, xn, lns)
            for c in range(NDC):
                emit_transpose_tile(
                    kb, xn.h[0:128, c * 128:(c + 1) * 128], xn,
                    xnf[c].h[0:128, g * 128:(g + 1) * 128], xnf[c],
                    ident, pst[(g * NDC + c) % 2])
        kb.phase_barrier()

    with ExitStack() as ph:
        E = ph.enter_context
        psP = [Tile(E(_psum_r(nc, kb.uname("psP"), [128, 512], F32)))
               for _ in range(3)]
        pst = [Tile(E(_psum_r(nc, kb.uname("pst"), [128, 1024], BF16)))
               for _ in range(2)]
        was = kb.sb("was", (128, NDC * 128), BF16, ctx=ph)
        wbs = kb.sb("wbs", (128, NDC * 128), BF16, ctx=ph)
        wcs = kb.sb("wcs", (128, NDC * 128), BF16, ctx=ph)
        at = kb.sb("at", (128, S), BF16, ctx=ph)
        dt_ = kb.sb("dt", (128, S), BF16, ctx=ph)
        ct_ = kb.sb("ct", (128, S), BF16, ctx=ph)
        st_ = kb.sb("st", (128, S), BF16, ctx=ph)
        yc = kb.sb("yc", (128, S), BF16, ctx=ph)
        for co in range(NDC):
            for c in range(NDC):
                kb.dma("gd", was.h[0:128, c * 128:(c + 1) * 128],
                       waT[c * 128:(c + 1) * 128, co * 128:(co + 1) * 128],
                       writes=[was])
                kb.dma("gd", wbs.h[0:128, c * 128:(c + 1) * 128],
                       wbT[c * 128:(c + 1) * 128, co * 128:(co + 1) * 128],
                       writes=[wbs])
                kb.dma("gd", wcs.h[0:128, c * 128:(c + 1) * 128],
                       wcT[c * 128:(c + 1) * 128, co * 128:(co + 1) * 128],
                       writes=[wcs])
            for kg in range(8):
                for i, (ws, dst, fn, bcol) in enumerate([
                        (was, at, AF.Sigmoid, co),
                        (wbs, dt_, AF.Tanh, 8 + co),
                        (wcs, ct_, AF.Sigmoid, 16 + co)]):
                    pp = psP[i]
                    for c in range(NDC):
                        kb.op("pe", lambda e, ws=ws, c=c, kg=kg, pp=pp:
                              e.matmul(pp.h[0:128, 0:512],
                                       ws.h[0:128, c * 128:(c + 1) * 128],
                                       xnf[c].h[0:128, kg * 512:
                                                (kg + 1) * 512],
                                       start=(c == 0), stop=(c == NDC - 1)),
                              reads=[ws, xnf[c]], writes=[pp],
                              inc=(c == NDC - 1))
                    kb.op("act", lambda e, pp=pp, dst=dst, fn=fn, bcol=bcol,
                          kg=kg: e.activation(
                              dst.h[0:128, kg * 512:(kg + 1) * 512],
                              pp.h[0:128, 0:512], fn,
                              bias=bas.h[0:128, bcol:bcol + 1]),
                          reads=[pp, bas], writes=[dst])
                kb.op("dve", lambda e, kg=kg: e.tensor_scalar(
                    yc.h[0:128, kg * 512:(kg + 1) * 512],
                    at.h[0:128, kg * 512:(kg + 1) * 512], -1.0, 1.0,
                    op0=ALU.mult, op1=ALU.add),
                    reads=[at], writes=[yc])
                kb.op("dve", lambda e, kg=kg: e.tensor_tensor(
                    dt_.h[0:128, kg * 512:(kg + 1) * 512],
                    yc.h[0:128, kg * 512:(kg + 1) * 512],
                    dt_.h[0:128, kg * 512:(kg + 1) * 512], op=ALU.mult),
                    reads=[yc, dt_], writes=[dt_])
            kb.op("dve", lambda e: e.tensor_tensor_scan(
                st_.ap, at.ap, dt_.ap, 0.0, op0=ALU.mult, op1=ALU.add),
                reads=[at, dt_], writes=[st_])
            kb.op("dve", lambda e: e.tensor_tensor(
                yc.ap, ct_.ap, st_.ap, op=ALU.mult),
                reads=[ct_, st_], writes=[yc])
            for g in range(NTF):
                p = pst[g % 2]
                kb.op("pe", lambda e, g=g, p=p: e.transpose(
                    p.h[0:128, 0:128], yc.h[0:128, g * 128:(g + 1) * 128],
                    ident.ap), reads=[yc, ident], writes=[p])
                kb.op("dve", lambda e, g=g, p=p, co=co: e.tensor_copy(
                    ytm.h[0:128, g * 1024 + co * 128:
                          g * 1024 + co * 128 + 128],
                    p.h[0:128, 0:128]), reads=[p], writes=[ytm])
        kb.phase_barrier()

    if fused is None:
        with ExitStack() as ph:
            xr = [kb.sb(f"xr{i}", (128, D), ctx=ph) for i in range(2)]
            yout = [kb.sb(f"yo{i}", (128, D), ctx=ph) for i in range(2)]
            for g in range(NTF):
                xt = xr[g % 2]
                kb.dma("sd", xt.ap, xfull[g * 128:(g + 1) * 128, :],
                       writes=[xt])
                yo = yout[g % 2]
                kb.op("dve", lambda e, xt=xt, yo=yo, g=g: e.tensor_tensor(
                    yo.ap, xt.ap, ytm.h[0:128, g * 1024:(g + 1) * 1024],
                    op=ALU.add), reads=[xt, ytm], writes=[yo])
                kb.dma("gd", xoutD[g * 128:(g + 1) * 128, :], yo.ap,
                       reads=[yo])
            kb.phase_barrier()
    else:
        ybf, yrs, xcur = fused["ybf"], fused["yrs"], fused["xcur"]
        for g in range(NTF):
            roff = (g % 2) * HALF + (g // 2) * 128
            kb.dma("gd", ybf.h[roff:roff + 128, :],
                   ytm.h[0:128, g * 1024:(g + 1) * 1024],
                   reads=[ytm], writes=[ybf])
        kb.phase_barrier()
        kb.op(fused["ccq"], lambda e: e.collective_compute(
            "ReduceScatter", ALU.add,
            replica_groups=[[0, 1], [2, 3], [4, 5], [6, 7]],
            ins=[ybf.h[:]], outs=[yrs.h[:]]),
            reads=[ybf], writes=[yrs])
        if "cbq" in fused:
            bi, bo = fused["bar"]
            kb.op(fused["cbq"], lambda e: e.collective_compute(
                "AllReduce", ALU.add,
                replica_groups=[[0, 1], [2, 3], [4, 5], [6, 7]],
                ins=[bi.h[:]], outs=[bo.h[:]]),
                reads=[bi, yrs], writes=[bo])
        kb.phase_barrier()
        with ExitStack() as ph:
            xr = [kb.sb(f"xr{i}", (128, D), ctx=ph) for i in range(2)]
            yr = [kb.sb(f"yr{i}", (128, D), BF16, ctx=ph) for i in range(2)]
            yout = [kb.sb(f"yo{i}", (128, D), ctx=ph) for i in range(2)]
            for j in range(NTOW):
                xt = xr[j % 2]
                yt = yr[j % 2]
                kb.dma("sd", xt.ap, xcur[j * 128:(j + 1) * 128, :],
                       writes=[xt])
                kb.dma("gd", yt.ap, yrs.h[j * 128:(j + 1) * 128, :],
                       reads=[yrs], writes=[yt])
                yo = yout[j % 2]
                kb.op("dve", lambda e, xt=xt, yt=yt, yo=yo:
                      e.scalar_tensor_tensor(
                          yo.ap, yt.ap, 0.5, xt.ap,
                          op0=ALU.mult, op1=ALU.add),
                      reads=[xt, yt], writes=[yo])
                kb.dma("gd", xoutD[j * 128:(j + 1) * 128, :], yo.ap,
                       reads=[yo])
            kb.phase_barrier()



def build_l1ffn():
    nc = bass.Bass("TRN2", target_bir_lowering=False)
    xown = nc.dram_tensor("xown", [HALF, D], F32, kind="ExternalInput").ap()
    rwD = nc.dram_tensor("router_w", [D, 8], F32, kind="ExternalInput").ap()
    gwD = nc.dram_tensor("gate_w", [8, D, 512], F32, kind="ExternalInput").ap()
    uwD = nc.dram_tensor("up_w", [8, D, 512], F32, kind="ExternalInput").ap()
    dwD = nc.dram_tensor("down_w", [8, 512, D], F32, kind="ExternalInput").ap()
    sgD = nc.dram_tensor("s_gate_w", [D, 2048], F32, kind="ExternalInput").ap()
    suD = nc.dram_tensor("s_up_w", [D, 2048], F32, kind="ExternalInput").ap()
    sdD = nc.dram_tensor("s_down_w", [2048, D], F32, kind="ExternalInput").ap()
    identD = nc.dram_tensor("ident", [128, 128], BF16, kind="ExternalInput").ap()
    identf32D = nc.dram_tensor("identf32", [128, 128], F32,
                               kind="ExternalInput").ap()
    xoutD = nc.dram_tensor("xout", [HALF, D], F32, kind="ExternalOutput").ap()
    with ExitStack() as ctx:
        kb = KB(nc, ctx)
        emit_l1ffn_body(kb, ctx, xown, rwD, gwD, uwD, dwD, sgD, suD, sdD,
                        identD, identf32D, xoutD)
        kb.finish("gd")
    return nc


def emit_l1ffn_body(kb, ctx, xown, rwD, gwD, uwD, dwD, sgD, suD, sdD,
                    identD, identf32D, xoutD):
    nc = kb.nc
    ident = kb.sb("sb_ident", (128, 128), BF16)
    idf32 = kb.sb("sb_idf32", (128, 128))
    rwr = kb.sb("sb_rwr", (128, 8 * NDC))
    wts = kb.sb("wts", (128, 8 * NTOW))
    xnw = [kb.sb(f"xnw{c}", (128, HALF), BF16) for c in range(NDC)]
    outb = kb.sb("outb", (128, NTOW * D), BF16)
    kb.dma("sd", ident.ap, identD, writes=[ident])
    kb.dma("sd", idf32.ap, identf32D, writes=[idf32])

    with ExitStack() as ph:
        E = ph.enter_context
        pst = [Tile(E(_psum_r(nc, kb.uname("pst"), [128, 1024], BF16)))
               for _ in range(2)]
        ps32 = [Tile(E(_psum_r(nc, kb.uname("ps32"), [128, 512], F32)))
                for _ in range(2)]
        psR = Tile(E(_psum_r(nc, kb.uname("psR"), [128, 512], F32)))
        xo = [kb.sb(f"xo{i}", (128, D), ctx=ph) for i in range(2)]
        xnb = [kb.sb(f"xnb{i}", (128, D), BF16, ctx=ph) for i in range(2)]
        xn32 = [kb.sb(f"xn32{i}", (128, D), ctx=ph) for i in range(2)]
        xf32 = [kb.sb(f"xf32{i}", (128, D), ctx=ph) for i in range(2)]
        top8 = kb.sb("top8", (128, 8), ctx=ph)
        nlmax = kb.sb("nlmax", (128, 1), ctx=ph)
        expl = kb.sb("expl", (128, 8), ctx=ph)
        lsum = kb.sb("lsum", (128, 1), ctx=ph)
        msk = kb.sb("msk", (128, 8), ctx=ph)
        msel = kb.sb("msel", (128, 8), ctx=ph)
        s4 = kb.sb("s4", (128, 1), ctx=ph)
        den = kb.sb("den", (128, 1), ctx=ph)
        rcp = kb.sb("rcp", (128, 1), ctx=ph)
        lns = make_ln_scratch(kb, "l1f", None, ctx=ph)
        for c in range(NDC):
            kb.dma("sd", rwr.h[0:128, 8 * c:8 * c + 8],
                   rwD[c * 128:(c + 1) * 128, :], writes=[rwr])
        for j in range(NTOW):
            xt = xo[j % 2]
            kb.dma("sd", xt.ap, xown[j * 128:(j + 1) * 128, :], writes=[xt])
            xb_ = xnb[j % 2]
            x3 = xn32[j % 2]
            emit_ln_tile(kb, xt, xb_.ap, xb_, lns)
            kb.op("dve", lambda e, xt=xt, x3=x3: e.tensor_scalar(
                x3.ap, xt.ap, lns["m"].ap, lns["r"].ap,
                op0=ALU.subtract, op1=ALU.mult),
                reads=[xt, lns["m"], lns["r"]], writes=[x3])
            xf = xf32[j % 2]
            for c in range(NDC):
                kb.op("pe", lambda e, xb_=xb_, c=c, j=j: e.transpose(
                    pst[(j * NDC + c) % 2].h[0:128, 0:128],
                    xb_.h[0:128, c * 128:(c + 1) * 128], ident.ap),
                    reads=[xb_, ident], writes=[pst[(j * NDC + c) % 2]])
                kb.op("dve", lambda e, c=c, j=j: e.tensor_copy(
                    xnw[c].h[0:128, j * 128:(j + 1) * 128],
                    pst[(j * NDC + c) % 2].h[0:128, 0:128]),
                    reads=[pst[(j * NDC + c) % 2]], writes=[xnw[c]])
                kb.op("pe", lambda e, x3=x3, c=c: e.transpose(
                    ps32[c % 2].h[0:128, 0:128],
                    x3.h[0:128, c * 128:(c + 1) * 128], idf32.ap),
                    reads=[x3, idf32], writes=[ps32[c % 2]])
                kb.op("dve", lambda e, xf=xf, c=c: e.tensor_copy(
                    xf.h[0:128, c * 128:(c + 1) * 128],
                    ps32[c % 2].h[0:128, 0:128]),
                    reads=[ps32[c % 2]], writes=[xf])
            for c in range(NDC):
                kb.op("pe", lambda e, xf=xf, c=c: e.matmul(
                    psR.h[0:128, 0:8],
                    xf.h[0:128, c * 128:(c + 1) * 128],
                    rwr.h[0:128, 8 * c:8 * c + 8],
                    start=(c == 0), stop=(c == NDC - 1)),
                    reads=[xf, rwr], writes=[psR], inc=(c == NDC - 1))
            kb.op("dve", lambda e: e.max(top8.ap, psR.h[0:128, 0:8]),
                  reads=[psR], writes=[top8])
            kb.op("dve", lambda e: e.tensor_scalar(
                nlmax.ap, top8.h[0:128, 0:1], -1.0, None, op0=ALU.mult),
                reads=[top8], writes=[nlmax])
            kb.op("act", lambda e: e.activation(
                expl.ap, psR.h[0:128, 0:8], AF.Exp, bias=nlmax.ap,
                accum_out=lsum.ap), reads=[psR, nlmax],
                writes=[expl, lsum])
            kb.op("dve", lambda e: e.tensor_scalar(
                msk.ap, psR.h[0:128, 0:8], top8.h[0:128, 3:4], None,
                op0=ALU.is_ge), reads=[psR, top8], writes=[msk])
            kb.op("dve", lambda e: e.tensor_tensor(
                msel.ap, msk.ap, expl.ap, op=ALU.mult),
                reads=[msk, expl], writes=[msel])
            kb.op("dve", lambda e: e.tensor_reduce(
                s4.ap, msel.ap, axis=AX.X, op=ALU.add),
                reads=[msel], writes=[s4])
            kb.op("dve", lambda e: e.scalar_tensor_tensor(
                den.ap, lsum.ap, 1e-20, s4.ap, op0=ALU.mult, op1=ALU.add),
                reads=[lsum, s4], writes=[den])
            kb.op("dve", lambda e: e.reciprocal(rcp.ap, den.ap),
                  reads=[den], writes=[rcp])
            kb.op("dve", lambda e, j=j: e.tensor_scalar(
                wts.h[0:128, 8 * j:8 * j + 8], msel.ap, rcp.ap, None,
                op0=ALU.mult), reads=[msel, rcp], writes=[wts])
        kb.phase_barrier()

    with ExitStack() as ph:
        E = ph.enter_context
        psG = [Tile(E(_psum_r(nc, kb.uname("psG"), [128, 512], F32)))
               for _ in range(2)]
        psU = [Tile(E(_psum_r(nc, kb.uname("psU"), [128, 512], F32)))
               for _ in range(2)]
        psD = [Tile(E(_psum_r(nc, kb.uname("psD"), [128, 512], F32)))
               for _ in range(4)]
        gws = [kb.sb(f"gws{c}", (128, 512), BF16, ctx=ph)
               for c in range(NDC)]
        uws = [kb.sb(f"uws{c}", (128, 512), BF16, ctx=ph)
               for c in range(NDC)]
        dws = [kb.sb(f"dws{i}", (128, D), BF16, ctx=ph) for i in range(4)]
        gsb = kb.sb("gsb", (128, 512), BF16, ctx=ph)
        act4 = kb.sb("act4", (128, 4 * 512), BF16, ctx=ph)
        kb.op("dve", lambda e: e.memset(outb.ap, 0.0), writes=[outb])
        for ei in range(12):
            routed = ei < 8
            ic0 = 0 if routed else (ei - 8) * 4
            for c in range(NDC):
                src_g = (gwD[ei, c * 128:(c + 1) * 128, :] if routed else
                         sgD[c * 128:(c + 1) * 128,
                             (ei - 8) * 512:(ei - 7) * 512])
                src_u = (uwD[ei, c * 128:(c + 1) * 128, :] if routed else
                         suD[c * 128:(c + 1) * 128,
                             (ei - 8) * 512:(ei - 7) * 512])
                kb.dma("gd", gws[c].ap, src_g, writes=[gws[c]])
                kb.dma("gd", uws[c].ap, src_u, writes=[uws[c]])
            for i in range(4):
                src_d = (dwD[ei, i * 128:(i + 1) * 128, :] if routed else
                         sdD[(ei - 8) * 512 + i * 128:
                             (ei - 8) * 512 + (i + 1) * 128, :])
                kb.dma("gd", dws[i].ap, src_d, writes=[dws[i]])
            for tq in range(4):
                for ic in range(4):
                    pg = psG[ic % 2]
                    pu = psU[ic % 2]
                    for c in range(NDC):
                        kb.op("pe", lambda e, c=c, ic=ic, tq=tq, pg=pg:
                              e.matmul(pg.h[0:128, 0:512],
                                       gws[c].h[0:128, ic * 128:
                                                (ic + 1) * 128],
                                       xnw[c].h[0:128, tq * 512:
                                                (tq + 1) * 512],
                                       start=(c == 0), stop=(c == NDC - 1)),
                              reads=[gws[c], xnw[c]], writes=[pg],
                              inc=(c == NDC - 1))
                    for c in range(NDC):
                        kb.op("pe", lambda e, c=c, ic=ic, tq=tq, pu=pu:
                              e.matmul(pu.h[0:128, 0:512],
                                       uws[c].h[0:128, ic * 128:
                                                (ic + 1) * 128],
                                       xnw[c].h[0:128, tq * 512:
                                                (tq + 1) * 512],
                                       start=(c == 0), stop=(c == NDC - 1)),
                              reads=[uws[c], xnw[c]], writes=[pu],
                              inc=(c == NDC - 1))
                    kb.op("act", lambda e, pg=pg: e.activation(
                        gsb.ap, pg.h[0:128, 0:512], AF.Silu),
                        reads=[pg], writes=[gsb])
                    kb.op("dve", lambda e, pu=pu, ic=ic: e.tensor_tensor(
                        act4.h[0:128, ic * 512:(ic + 1) * 512],
                        gsb.ap, pu.h[0:128, 0:512], op=ALU.mult),
                        reads=[gsb, pu], writes=[act4])
                for dh in range(2):
                    for t4 in range(4):
                        pd = psD[t4]
                        for ic in range(4):
                            kb.op("pe", lambda e, pd=pd, ic=ic, t4=t4,
                                  dh=dh: e.matmul(
                                      pd.h[0:128, 0:512],
                                      act4.h[0:128, ic * 512 + t4 * 128:
                                             ic * 512 + t4 * 128 + 128],
                                      dws[ic].h[0:128, dh * 512:
                                                dh * 512 + 512],
                                      start=(ic == 0), stop=(ic == 3)),
                                  reads=[act4, dws[ic]], writes=[pd],
                                  inc=(ic == 3))
                        j = tq * 4 + t4
                        if routed:
                            kb.op("dve", lambda e, pd=pd, j=j, dh=dh, ei=ei:
                                  e.scalar_tensor_tensor(
                                      outb.h[0:128, j * 1024 + dh * 512:
                                             j * 1024 + dh * 512 + 512],
                                      pd.h[0:128, 0:512],
                                      wts.h[0:128, 8 * j + ei:
                                            8 * j + ei + 1],
                                      outb.h[0:128, j * 1024 + dh * 512:
                                             j * 1024 + dh * 512 + 512],
                                      op0=ALU.mult, op1=ALU.add),
                                  reads=[pd, wts, outb], writes=[outb])
                        else:
                            kb.op("dve", lambda e, pd=pd, j=j, dh=dh:
                                  e.tensor_tensor(
                                      outb.h[0:128, j * 1024 + dh * 512:
                                             j * 1024 + dh * 512 + 512],
                                      pd.h[0:128, 0:512],
                                      outb.h[0:128, j * 1024 + dh * 512:
                                             j * 1024 + dh * 512 + 512],
                                      op=ALU.add),
                                  reads=[pd, outb], writes=[outb])
        kb.phase_barrier()

    with ExitStack() as ph:
        xr = [kb.sb(f"xr{i}", (128, D), ctx=ph) for i in range(2)]
        yout = [kb.sb(f"yo{i}", (128, D), ctx=ph) for i in range(2)]
        for j in range(NTOW):
            xt = xr[j % 2]
            kb.dma("sd", xt.ap, xown[j * 128:(j + 1) * 128, :], writes=[xt])
            yo = yout[j % 2]
            kb.op("dve", lambda e, xt=xt, yo=yo, j=j: e.tensor_tensor(
                yo.ap, xt.ap, outb.h[0:128, j * 1024:(j + 1) * 1024],
                op=ALU.add), reads=[xt, outb], writes=[yo])
            kb.dma("gd", xoutD[j * 128:(j + 1) * 128, :], yo.ap, reads=[yo])
        kb.phase_barrier()



def l1a_masks(hf):
    kp = np.arange(128)[:, None]
    qp = np.arange(128)[None, :]
    tri = np.where(kp > qp, NEG, 0.0).astype(np.float32)
    edge = np.where(kp <= qp, NEG, 0.0).astype(np.float32)
    full = np.full((128, 128), NEG, np.float32)
    zeros = np.zeros((128, 128), np.float32)
    g64 = np.broadcast_to(np.where(kp >= 64, NEG, 0.0),
                          (128, 128)).astype(np.float32)
    g64tri = np.where((kp >= 64) | (kp > qp), NEG, 0.0).astype(np.float32)
    if hf == 0:
        return dict(mCm2=edge, mCm1=zeros, mC0=tri, mCp1=full,
                    mG0=full, mG1=g64tri, mG=g64)
    return dict(mCm2=full, mCm1=edge, mC0=zeros, mCp1=tri,
                mG0=full, mG1=g64, mG=g64)


def rope_tables():
    inv = 1.0 / (10000.0 ** (np.arange(0, 64, 2, dtype=np.float32) / 64.0))
    t = np.arange(S, dtype=np.float32)
    fr = t[:, None] * inv[None, :]
    return np.cos(fr).T.astype(np.float32), np.sin(fr).T.astype(np.float32)


def build_l1a():
    nc = bass.Bass("TRN2", target_bir_lowering=False)
    xown = nc.dram_tensor("xown", [HALF, D], F32, kind="ExternalInput").ap()
    xfull = nc.dram_tensor("xfull", [S, D], F32, kind="ExternalInput").ap()
    wqD = nc.dram_tensor("wq", [D, D], F32, kind="ExternalInput").ap()
    wkD = nc.dram_tensor("wk", [D, 256], F32, kind="ExternalInput").ap()
    wvD = nc.dram_tensor("wv", [D, 256], F32, kind="ExternalInput").ap()
    woD = nc.dram_tensor("wo", [D, D], F32, kind="ExternalInput").ap()
    identD = nc.dram_tensor("ident", [128, 128], BF16, kind="ExternalInput").ap()
    cqD = nc.dram_tensor("cosq", [32, HALF], F32, kind="ExternalInput").ap()
    sqD = nc.dram_tensor("sinq", [32, HALF], F32, kind="ExternalInput").ap()
    ckD = nc.dram_tensor("cosk", [32, S], F32, kind="ExternalInput").ap()
    skD = nc.dram_tensor("sink", [32, S], F32, kind="ExternalInput").ap()
    mks = {m: nc.dram_tensor(m, [128, 128], F32, kind="ExternalInput").ap()
           for m in ("mCm2", "mCm1", "mC0", "mCp1", "mG0", "mG1", "mG")}
    xoutD = nc.dram_tensor("xout", [HALF, D], F32, kind="ExternalOutput").ap()
    with ExitStack() as ctx:
        kb = KB(nc, ctx)
        emit_l1a_body(kb, ctx, xown, xfull, wqD, wkD, wvD, woD, identD,
                      cqD, sqD, ckD, skD, mks, xoutD)
        kb.finish("gd")
    return nc


def emit_l1a_body(kb, ctx, xown, xfull, wqD, wkD, wvD, woD, identD,
                  cqD, sqD, ckD, skD, mksD, xoutD, xmap=None):
    nc = kb.nc

    def ldxf(xt, g):
        if xmap is None:
            kb.dma("sd", xt.ap, xfull[g * 128:(g + 1) * 128, :],
                   writes=[xt])
        else:
            off = (g % 2) * HALF + (g // 2) * 128
            kb.dma("gd", xt.ap, xfull[off:off + 128, :], writes=[xt])
    RMS_EPS = float(np.finfo(np.float32).eps)
    ident = kb.sb("sb_ident", (128, 128), BF16)
    cosq = kb.sb("sb_cosq", (32, HALF), BF16)
    sinq = kb.sb("sb_sinq", (32, HALF), BF16)
    cosk = kb.sb("sb_cosk", (32, S), BF16)
    sink = kb.sb("sb_sink", (32, S), BF16)
    mks = {m: kb.sb("sb_" + m, (128, 128)) for m in mksD}
    ones64 = kb.sb("ones64", (64, 1), BF16)
    ones1x64 = kb.sb("ones1x64", (1, 64), BF16)
    kb.dma("sd", ident.ap, identD, writes=[ident])
    kb.dma("gd", cosq.ap, cqD, writes=[cosq])
    kb.dma("gd", sinq.ap, sqD, writes=[sinq])
    kb.dma("gd", cosk.ap, ckD, writes=[cosk])
    kb.dma("gd", sink.ap, skD, writes=[sink])
    for m in mks:
        kb.dma("sd", mks[m].ap, mksD[m], writes=[mks[m]])
    kb.op("dve", lambda e: e.memset(ones64.ap, 1.0), writes=[ones64])
    kb.op("dve", lambda e: e.memset(ones1x64.ap, 1.0), writes=[ones1x64])
    OA = [kb.sb(f"oa{c}", (128, HALF), BF16) for c in range(NDC)]
    QF = [kb.sb(f"qf{c}", (128, HALF), BF16) for c in range(NDC)]
    K_all = [kb.sb(f"kall{i}", (128, S), BF16) for i in range(2)]
    V_all = [kb.sb(f"vall{g}", (128, 256), BF16) for g in range(NTF)]

    with ExitStack() as ph:
        E = ph.enter_context
        pst = [Tile(E(_psum_r(nc, kb.uname("pst"), [128, 1024], BF16)))
               for _ in range(2)]
        psQ = [Tile(E(_psum_r(nc, kb.uname("psQ"), [128, 512], F32)))
               for _ in range(2)]
        xo = [kb.sb(f"xo{i}", (128, D), ctx=ph) for i in range(2)]
        xnt = [kb.sb(f"xnt{i}", (128, D), BF16, ctx=ph) for i in range(2)]
        xnow_fm = [kb.sb(f"xnowfm{c}", (128, HALF), BF16, ctx=ph)
                   for c in range(NDC)]
        wqsl = [kb.sb(f"wqsl{i}", (128, 1024), BF16, ctx=ph)
                for i in range(2)]
        lns = make_ln_scratch(kb, "l1a", None, ctx=ph)
        for j in range(NTOW):
            xt = xo[j % 2]
            kb.dma("sd", xt.ap, xown[j * 128:(j + 1) * 128, :], writes=[xt])
            xn = xnt[j % 2]
            emit_ln_tile(kb, xt, xn.ap, xn, lns)
            for c in range(NDC):
                emit_transpose_tile(
                    kb, xn.h[0:128, c * 128:(c + 1) * 128], xn,
                    xnow_fm[c].h[0:128, j * 128:(j + 1) * 128], xnow_fm[c],
                    ident, pst[(j * NDC + c) % 2])
        for dq in range(NDC):
            ws = wqsl[dq % 2]
            for c in range(NDC):
                kb.dma("gd", ws.h[0:128, c * 128:(c + 1) * 128],
                       wqD[c * 128:(c + 1) * 128, dq * 128:(dq + 1) * 128],
                       writes=[ws])
            for tq in range(4):
                pq = psQ[tq % 2]
                for c in range(NDC):
                    kb.op("pe", lambda e, c=c, tq=tq, pq=pq, ws=ws:
                          e.matmul(
                              pq.h[0:128, 0:512],
                              ws.h[0:128, c * 128:(c + 1) * 128],
                              xnow_fm[c].h[0:128, tq * 512:(tq + 1) * 512],
                              start=(c == 0), stop=(c == NDC - 1)),
                          reads=[ws, xnow_fm[c]], writes=[pq],
                          inc=(c == NDC - 1))
                kb.op("act", lambda e, pq=pq, dq=dq, tq=tq: e.activation(
                    QF[dq].h[0:128, tq * 512:(tq + 1) * 512],
                    pq.h[0:128, 0:512], AF.Identity),
                    reads=[pq], writes=[QF[dq]])
        kb.phase_barrier()

    with ExitStack() as ph:
        E = ph.enter_context
        pst = [Tile(E(_psum_r(nc, kb.uname("pst"), [128, 1024], BF16)))
               for _ in range(2)]
        psK = [Tile(E(_psum_r(nc, kb.uname("psK"), [128, 512], F32)))
               for _ in range(2)]
        psV = [Tile(E(_psum_r(nc, kb.uname("psV"), [128, 512], F32)))
               for _ in range(2)]
        xf = [kb.sb(f"xf{i}", (128, D), ctx=ph) for i in range(2)]
        xnt = [kb.sb(f"xnt{i}", (128, D), BF16, ctx=ph) for i in range(2)]
        xnf = [[kb.sb(f"xnf_{c}", (128, 512), BF16, ctx=ph)
                for c in range(NDC)]]
        wks = kb.sb("wks", (128, NDC * 256), BF16, ctx=ph)
        wvs = kb.sb("wvs", (128, NDC * 256), BF16, ctx=ph)
        lns = make_ln_scratch(kb, "l1af", None, ctx=ph)
        for c in range(NDC):
            kb.dma("gd", wks.h[0:128, c * 256:(c + 1) * 256],
                   wkD[c * 128:(c + 1) * 128, :], writes=[wks])
            kb.dma("gd", wvs.h[0:128, c * 256:(c + 1) * 256],
                   wvD[c * 128:(c + 1) * 128, :], writes=[wvs])
        for kg in range(8):
            fmb = xnf[0]
            for t in range(4):
                g = kg * 4 + t
                xt = xf[g % 2]
                ldxf(xt, g)
                xn = xnt[g % 2]
                emit_ln_tile(kb, xt, xn.ap, xn, lns)
                for c in range(NDC):
                    emit_transpose_tile(
                        kb, xn.h[0:128, c * 128:(c + 1) * 128], xn,
                        fmb[c].h[0:128, t * 128:(t + 1) * 128], fmb[c],
                        ident, pst[(g * NDC + c) % 2])
            for oc in range(2):
                pk = psK[oc % 2]
                for c in range(NDC):
                    kb.op("pe", lambda e, c=c, oc=oc, pk=pk, fmb=fmb:
                          e.matmul(pk.h[0:128, 0:512],
                                   wks.h[0:128, c * 256 + oc * 128:
                                         c * 256 + oc * 128 + 128],
                                   fmb[c].h[0:128, 0:512],
                                   start=(c == 0), stop=(c == NDC - 1)),
                          reads=[wks, fmb[c]], writes=[pk],
                          inc=(c == NDC - 1))
                kb.op("act", lambda e, pk=pk, oc=oc, kg=kg: e.activation(
                    K_all[oc].h[0:128, kg * 512:(kg + 1) * 512],
                    pk.h[0:128, 0:512], AF.Identity),
                    reads=[pk], writes=[K_all[oc]])
            for t in range(4):
                g = kg * 4 + t
                pv = psV[t % 2]
                for c in range(NDC):
                    kb.op("pe", lambda e, c=c, t=t, pv=pv, fmb=fmb:
                          e.matmul(pv.h[0:128, 0:256],
                                   fmb[c].h[0:128, t * 128:(t + 1) * 128],
                                   wvs.h[0:128, c * 256:(c + 1) * 256],
                                   start=(c == 0), stop=(c == NDC - 1)),
                          reads=[wvs, fmb[c]], writes=[pv],
                          inc=(c == NDC - 1))
                kb.op("dve", lambda e, pv=pv, g=g: e.tensor_copy(
                    V_all[g].ap, pv.h[0:128, 0:256]),
                    reads=[pv], writes=[V_all[g]])
        kb.phase_barrier()

    SS = float(1.0 / np.sqrt(np.sqrt(64.0)))
    with ExitStack() as ph3:
        E3 = ph3.enter_context
        psS = [Tile(E3(_psum_r(nc, kb.uname("psS"), [128, 512], F32)))
               for _ in range(2)]
        psN = Tile(E3(_psum_r(nc, kb.uname("psN"), [128, 512], F32)))
        psB = Tile(E3(_psum_r(nc, kb.uname("psB"), [128, 512], F32)))
        psO = [Tile(E3(_psum_r(nc, kb.uname("psO"), [128, 512], F32)))
               for _ in range(2)]
        KS = kb.sb("ksx", (65, S), BF16, ctx=ph3)
        VS = kb.sb("vsx", (128, NTF * 65), BF16, ctx=ph3)
        QS = [kb.sb(f"qs{i}", (65, HALF), BF16, ctx=ph3) for i in range(2)]
        sq_ = kb.sb("sq_", (64, 512), BF16, ctx=ph3)
        rstd = kb.sb("rstd", (1, S), ctx=ph3)
        rt1 = kb.sb("rt1", (32, S), BF16, ctx=ph3)
        rt2 = kb.sb("rt2", (32, S), BF16, ctx=ph3)
        x1t = kb.sb("x1t", (32, S), BF16, ctx=ph3)
        x2t = kb.sb("x2t", (32, S), BF16, ctx=ph3)
        rstdb = kb.sb("rstdb", (1, 512), BF16, ctx=ph3)
        linv = kb.sb("linv", (1, 128), ctx=ph3)
        linb = kb.sb("linb", (1, 128), BF16, ctx=ph3)
        osb = kb.sb("osb", (64, 128), BF16, ctx=ph3)
        ptw_g = kb.sb("ptw", (128, 5 * 128), BF16, ctx=ph3)

        def rms_rope(tile, n, cos, sin, scale_extra):
            for c2 in range(n // 512):
                kb.op("dve", lambda e, c2=c2: e.tensor_tensor(
                    sq_.ap, tile.h[0:64, c2 * 512:(c2 + 1) * 512],
                    tile.h[0:64, c2 * 512:(c2 + 1) * 512], op=ALU.mult),
                    reads=[tile], writes=[sq_])
                kb.op("pe", lambda e: e.matmul(
                    psN.h[0:1, 0:512], ones64.ap, sq_.ap,
                    start=True, stop=True),
                    reads=[ones64, sq_], writes=[psN])
                kb.op("dve", lambda e, c2=c2: e.tensor_scalar(
                    rstd.h[0:1, c2 * 512:(c2 + 1) * 512],
                    psN.h[0:1, 0:512], 1.0 / 64.0, RMS_EPS,
                    op0=ALU.mult, op1=ALU.add),
                    reads=[psN], writes=[rstd])
            kb.op("act", lambda e, n=n: e.activation(
                rstd.h[0:1, 0:n], rstd.h[0:1, 0:n], AF.Ln),
                reads=[rstd], writes=[rstd])
            kb.op("act", lambda e, n=n: e.activation(
                rstd.h[0:1, 0:n], rstd.h[0:1, 0:n], AF.Exp, scale=-0.5),
                reads=[rstd], writes=[rstd])
            kb.op("dve", lambda e, n=n, scale_extra=scale_extra:
                  e.tensor_scalar(rstd.h[0:1, 0:n], rstd.h[0:1, 0:n],
                                  scale_extra, None, op0=ALU.mult),
                  reads=[rstd], writes=[rstd])
            lb = rstdb
            for c2 in range(n // 512):
                kb.op("dve", lambda e, c2=c2: e.tensor_copy(
                    lb.ap, rstd.h[0:1, c2 * 512:(c2 + 1) * 512]),
                    reads=[rstd], writes=[lb])
                kb.op("pe", lambda e: e.matmul(
                    psB.h[0:64, 0:512], ones1x64.ap, lb.ap,
                    start=True, stop=True),
                    reads=[ones1x64, lb], writes=[psB])
                kb.op("dve", lambda e, c2=c2: e.tensor_tensor(
                    tile.h[0:64, c2 * 512:(c2 + 1) * 512],
                    tile.h[0:64, c2 * 512:(c2 + 1) * 512],
                    psB.h[0:64, 0:512], op=ALU.mult),
                    reads=[psB, tile], writes=[tile])
            kb.op("dve", lambda e, n=n: e.tensor_copy(
                x1t.h[0:32, 0:n], tile.h[0:32, 0:n]),
                reads=[tile], writes=[x1t])
            kb.op("dve", lambda e, n=n: e.tensor_copy(
                x2t.h[0:32, 0:n], tile.h[32:64, 0:n]),
                reads=[tile], writes=[x2t])
            kb.op("dve", lambda e, n=n, cos=cos: e.tensor_tensor(
                rt1.h[0:32, 0:n], x1t.h[0:32, 0:n], cos.h[0:32, 0:n],
                op=ALU.mult), reads=[x1t, cos], writes=[rt1])
            kb.op("dve", lambda e, n=n, sin=sin: e.tensor_tensor(
                rt2.h[0:32, 0:n], x2t.h[0:32, 0:n], sin.h[0:32, 0:n],
                op=ALU.mult), reads=[x2t, sin], writes=[rt2])
            kb.op("dve", lambda e, n=n: e.tensor_tensor(
                rt1.h[0:32, 0:n], rt1.h[0:32, 0:n], rt2.h[0:32, 0:n],
                op=ALU.add), reads=[rt1, rt2], writes=[rt1])
            kb.op("dve", lambda e, n=n, sin=sin: e.tensor_tensor(
                rt2.h[0:32, 0:n], x1t.h[0:32, 0:n], sin.h[0:32, 0:n],
                op=ALU.mult), reads=[x1t, sin], writes=[rt2])
            kb.op("dve", lambda e, n=n, cos=cos: e.tensor_tensor(
                x2t.h[0:32, 0:n], x2t.h[0:32, 0:n], cos.h[0:32, 0:n],
                op=ALU.mult), reads=[x2t, cos], writes=[x2t])
            kb.op("dve", lambda e, n=n: e.tensor_tensor(
                x2t.h[0:32, 0:n], x2t.h[0:32, 0:n], rt2.h[0:32, 0:n],
                op=ALU.subtract), reads=[x2t, rt2], writes=[x2t])
            kb.op("dve", lambda e, n=n: e.tensor_copy(
                tile.h[0:32, 0:n], rt1.h[0:32, 0:n]),
                reads=[rt1], writes=[tile])
            kb.op("dve", lambda e, n=n: e.tensor_copy(
                tile.h[32:64, 0:n], x2t.h[0:32, 0:n]),
                reads=[x2t], writes=[tile])

        for kv in range(4):
            kb.op("dve", lambda e, kv=kv: e.tensor_copy(
                KS.h[0:64, 0:S],
                K_all[kv // 2].h[(kv % 2) * 64:(kv % 2) * 64 + 64, 0:S]),
                reads=[K_all[kv // 2]], writes=[KS])
            kb.op("dve", lambda e: e.memset(KS.h[64:65, 0:S], 1.0),
                  writes=[KS])
            rms_rope(KS, S, cosk, sink, SS)
            kb.op("dve", lambda e: e.memset(VS.ap, 1.0), writes=[VS])
            for g in range(NTF):
                kb.op("dve", lambda e, g=g, kv=kv: e.tensor_copy(
                    VS.h[0:128, g * 65:g * 65 + 64],
                    V_all[g].h[0:128, kv * 64:kv * 64 + 64]),
                    reads=[V_all[g]], writes=[VS])
            for hq in range(4):
                h = kv * 4 + hq
                qa = QS[hq % 2]
                c, r0 = h // 2, (h % 2) * 64
                kb.op("dve", lambda e, qa=qa, c=c, r0=r0: e.tensor_copy(
                    qa.h[0:64, 0:HALF], QF[c].h[r0:r0 + 64, 0:HALF]),
                    reads=[QF[c]], writes=[qa])
                kb.op("dve", lambda e, qa=qa: e.memset(
                    qa.h[64:65, 0:HALF], 0.0), writes=[qa])
                rms_rope(qa, HALF, cosq, sinq, SS)
                for lt in range(NTOW):
                    slots = [s for s in range(2 * lt - 2, 2 * lt + 2)
                             if 0 <= s < NTF]
                    cls = {s: s - 2 * lt for s in slots}
                    po = psO[lt % 2]
                    nsl = len(slots) + 1
                    for i, s in enumerate(slots):
                        kb.op("pe", lambda e, s=s, lt=lt, i=i, qa=qa:
                              e.matmul(
                                  psS[0].h[0:128, i * 128:(i + 1) * 128],
                                  KS.h[0:65, s * 128:(s + 1) * 128],
                                  qa.h[0:65, lt * 128:(lt + 1) * 128],
                                  start=True, stop=True),
                              reads=[KS, qa], writes=[psS[0]])
                        mt = mks[{-2: "mCm2", -1: "mCm1", 0: "mC0",
                                  1: "mCp1"}[cls[s]]]
                        kb.op("dve", lambda e, i=i, mt=mt: e.tensor_tensor(
                            psS[0].h[0:128, i * 128:(i + 1) * 128],
                            psS[0].h[0:128, i * 128:(i + 1) * 128],
                            mt.ap, op=ALU.add),
                            reads=[psS[0], mt], writes=[psS[0]])
                    kb.op("pe", lambda e, lt=lt, qa=qa: e.matmul(
                        psS[1].h[0:128, 0:128], KS.h[0:65, 0:128],
                        qa.h[0:65, lt * 128:(lt + 1) * 128],
                        start=True, stop=True),
                        reads=[KS, qa], writes=[psS[1]])
                    mg = mks["mG0" if lt == 0 else ("mG1" if lt == 1
                                                    else "mG")]
                    kb.op("dve", lambda e, mg=mg: e.tensor_tensor(
                        psS[1].h[0:128, 0:128], psS[1].h[0:128, 0:128],
                        mg.ap, op=ALU.add),
                        reads=[psS[1], mg], writes=[psS[1]])
                    ptw = ptw_g
                    kb.op("act", lambda e, nw=len(slots), ptw=ptw:
                          e.activation(ptw.h[0:128, 0:nw * 128],
                                       psS[0].h[0:128, 0:nw * 128],
                                       AF.Exp), reads=[psS[0]],
                          writes=[ptw])
                    kb.op("act", lambda e, nw=len(slots), ptw=ptw:
                          e.activation(
                              ptw.h[0:128, nw * 128:(nw + 1) * 128],
                              psS[1].h[0:128, 0:128], AF.Exp),
                          reads=[psS[1]], writes=[ptw])
                    allsl = slots + [0]
                    for i, s in enumerate(allsl):
                        kb.op("pe", lambda e, s=s, i=i, po=po,
                              nsl=nsl: e.matmul(
                                  po.h[0:65, 0:128],
                                  VS.h[0:128, s * 65:s * 65 + 65],
                                  ptw_g.h[0:128,
                                                i * 128:(i + 1) * 128],
                                  start=(i == 0), stop=(i == nsl - 1)),
                              reads=[VS, ptw_g], writes=[po],
                              inc=(i == nsl - 1))
                    kb.op("dve", lambda e, po=po: e.reciprocal(
                        linv.ap, po.h[64:65, 0:128]),
                        reads=[po], writes=[linv])
                    kb.op("dve", lambda e: e.tensor_copy(linb.ap, linv.ap),
                          reads=[linv], writes=[linb])
                    kb.op("pe", lambda e: e.matmul(
                        psB.h[0:64, 0:128], ones1x64.ap, linb.ap,
                        start=True, stop=True),
                        reads=[ones1x64, linb], writes=[psB])
                    kb.op("act", lambda e, po=po: e.activation(
                        osb.ap, po.h[0:64, 0:128], AF.Identity),
                        reads=[po], writes=[osb])
                    c2, r2 = h // 2, (h % 2) * 64
                    kb.op("dve", lambda e, c2=c2, r2=r2, lt=lt:
                          e.tensor_tensor(
                              OA[c2].h[r2:r2 + 64,
                                       lt * 128:(lt + 1) * 128],
                              osb.ap, psB.h[0:64, 0:128], op=ALU.mult),
                          reads=[osb, psB], writes=[OA[c2]])
        kb.phase_barrier()

    with ExitStack() as ph:
        E = ph.enter_context
        psY = [Tile(E(_psum_r(nc, kb.uname("psY"), [128, 512], F32)))
               for _ in range(4)]
        wpj = kb.sb("wpj", (128, NDC * 1024), BF16, ctx=ph)
        for c in range(NDC):
            kb.dma("gd", wpj.h[0:128, c * 1024:(c + 1) * 1024],
                   woD[c * 128:(c + 1) * 128, :], writes=[wpj])
        xr = [kb.sb(f"xr{i}", (128, D), ctx=ph) for i in range(2)]
        yout = [kb.sb(f"yout{i}", (128, D), ctx=ph) for i in range(2)]
        for j in range(NTOW):
            xt = xr[j % 2]
            kb.dma("sd", xt.ap, xown[j * 128:(j + 1) * 128, :], writes=[xt])
            yo = yout[j % 2]
            for o2 in range(2):
                py = psY[(j * 2 + o2) % 4]
                for c in range(NDC):
                    kb.op("pe", lambda e, c=c, o2=o2, j=j, py=py: e.matmul(
                        py.h[0:128, 0:512],
                        OA[c].h[0:128, j * 128:(j + 1) * 128],
                        wpj.h[0:128, c * 1024 + o2 * 512:
                              c * 1024 + o2 * 512 + 512],
                        start=(c == 0), stop=(c == NDC - 1)),
                        reads=[OA[c], wpj], writes=[py],
                        inc=(c == NDC - 1))
                kb.op("dve", lambda e, py=py, yo=yo, xt=xt, o2=o2:
                      e.tensor_tensor(
                          yo.h[0:128, o2 * 512:o2 * 512 + 512],
                          xt.h[0:128, o2 * 512:o2 * 512 + 512],
                          py.h[0:128, 0:512], op=ALU.add),
                      reads=[py, xt], writes=[yo])
            kb.dma("gd", xoutD[j * 128:(j + 1) * 128, :], yo.ap, reads=[yo])
        kb.phase_barrier()



def _scatter(res, key="xout"):
    out = np.empty((B, S, D), np.float32)
    for core in range(NCORES):
        b, hf = core // 2, core % 2
        out[b][own_rows(hf)] = res[core][key]
    return out


def _run(nc, in_maps):
    return run_bass_kernel_spmd(nc, in_maps, core_ids=list(range(NCORES)))


def kernel(**inp):
    return kernel_fused(**inp)


def kernel_unfused(**inp):
    assert np.asarray(inp["L1_ffn_router_bias"]).shape == ()
    C = CONSTS
    cosT, sinT = rope_tables()
    x = np.ascontiguousarray(np.asarray(inp["L0_attn_a_x"], np.float32))

    progs = {
        "l0a": build_l0a(), "l0ffn": build_l0ffn(), "l0b": build_l0b(),
        "l1a": build_l1a(), "l1ffn": build_l1ffn(), "l1b": build_l1b(),
    }

    wqkvT_l0a = np.ascontiguousarray(np.asarray(inp["L0_attn_a_W_attn"]).T)
    wprojT_l0a = np.ascontiguousarray(np.asarray(inp["L0_attn_a_W_proj"]).T)
    gate_wT = np.ascontiguousarray(np.asarray(inp["L0_ffn_gate_w"]).T)
    rmsr = np.ascontiguousarray(
        np.asarray(inp["L0_ffn_rms_w"]).reshape(8, 128).T)
    b1r = np.ascontiguousarray(
        np.asarray(inp["L0_ffn_b1"]).reshape(8, 32, 128)
        .transpose(2, 0, 1).reshape(128, 256))
    bas_l0b = np.ascontiguousarray(np.concatenate(
        [np.asarray(inp["L0_attn_b_ba"]).reshape(8, 128).T,
         np.asarray(inp["L0_attn_b_bb"]).reshape(8, 128).T,
         np.asarray(inp["L0_attn_b_bc"]).reshape(8, 128).T], axis=1))
    wqkvT_l1b = np.ascontiguousarray(np.concatenate(
        [np.asarray(inp["L1_attn_b_Wq"]), np.asarray(inp["L1_attn_b_Wk"]),
         np.asarray(inp["L1_attn_b_Wv"])], axis=1))
    kp = np.arange(128)[:, None]; qp = np.arange(128)[None, :]
    tri50 = np.where(kp > qp, -50.0, 0.0).astype(np.float32)
    full50 = np.full((128, 128), -50.0, np.float32)
    npq0 = np.zeros((1, 16 * HALF), np.float32)
    alz = np.zeros((128, 512), np.float32)

    def maps_l0a(xc):
        ms = []
        for core in range(NCORES):
            b, hf = core // 2, core % 2
            ms.append({
                "xown": np.ascontiguousarray(xc[b][own_rows(hf)]),
                "xfull": np.ascontiguousarray(xc[b]),
                "wqkvT": wqkvT_l0a, "wprojT": wprojT_l0a,
                "ident": C["ident"],
                "maskA": C["tri_causal"] if hf == 0 else C["zeros128"],
                "maskB": C["full_neg"] if hf == 0 else C["tri_causal"],
                "alibik": C["alibik"], "negposq": C[f"negposq_{hf}"]})
        return ms

    def maps_l0ffn(xc):
        ms = []
        for core in range(NCORES):
            b, hf = core // 2, core % 2
            ms.append({
                "xown": np.ascontiguousarray(xc[b][own_rows(hf)]),
                "gate_wT": gate_wT, "rmsr": rmsr,
                "w1": np.asarray(inp["L0_ffn_W1"]),
                "w2": np.asarray(inp["L0_ffn_W2"]),
                "b1r": b1r, "b2": np.asarray(inp["L0_ffn_b2"]),
                "ident": C["ident"], "identf32": C["identf32"]})
        return ms

    def maps_l0b(xc):
        ms = []
        for core in range(NCORES):
            b = core // 2
            ms.append({
                "xfull": np.ascontiguousarray(xc[b]),
                "waT": np.ascontiguousarray(np.asarray(inp["L0_attn_b_Wa"]).T),
                "wbT": np.ascontiguousarray(np.asarray(inp["L0_attn_b_Wb"]).T),
                "wcT": np.ascontiguousarray(np.asarray(inp["L0_attn_b_Wc"]).T),
                "bas": bas_l0b, "ident": C["ident"]})
        return ms

    def maps_l1a(xc):
        ms = []
        for core in range(NCORES):
            b, hf = core // 2, core % 2
            rows = own_rows(hf)
            m = {"xown": np.ascontiguousarray(xc[b][rows]),
                 "xfull": np.ascontiguousarray(xc[b]),
                 "wq": np.asarray(inp["L1_attn_a_Wq"]),
                 "wk": np.asarray(inp["L1_attn_a_Wk"]),
                 "wv": np.asarray(inp["L1_attn_a_Wv"]),
                 "wo": np.asarray(inp["L1_attn_a_Wo"]),
                 "ident": C["ident"],
                 "cosq": np.ascontiguousarray(cosT[:, rows]),
                 "sinq": np.ascontiguousarray(sinT[:, rows]),
                 "cosk": cosT, "sink": sinT}
            m.update(l1a_masks(hf))
            ms.append(m)
        return ms

    def maps_l1ffn(xc):
        ms = []
        for core in range(NCORES):
            b, hf = core // 2, core % 2
            ms.append({
                "xown": np.ascontiguousarray(xc[b][own_rows(hf)]),
                "router_w": np.asarray(inp["L1_ffn_router_w"]),
                "gate_w": np.asarray(inp["L1_ffn_gate_w"]),
                "up_w": np.asarray(inp["L1_ffn_up_w"]),
                "down_w": np.asarray(inp["L1_ffn_down_w"]),
                "s_gate_w": np.asarray(inp["L1_ffn_s_gate_w"]),
                "s_up_w": np.asarray(inp["L1_ffn_s_up_w"]),
                "s_down_w": np.asarray(inp["L1_ffn_s_down_w"]),
                "ident": C["ident"], "identf32": C["identf32"]})
        return ms

    def maps_l1b(xc):
        ms = []
        for core in range(NCORES):
            b, hf = core // 2, core % 2
            ms.append({
                "xown": np.ascontiguousarray(xc[b][own_rows(hf)]),
                "xfull": np.ascontiguousarray(xc[b]),
                "wqkvT": wqkvT_l1b,
                "wprojT": np.ascontiguousarray(np.asarray(inp["L1_attn_b_Wr"])),
                "ident": C["ident"],
                "maskA": tri50 if hf == 0 else C["zeros128"],
                "maskB": full50 if hf == 0 else tri50,
                "alibik": alz, "negposq": npq0})
        return ms

    x = _scatter(_run(progs["l0a"], maps_l0a(x)).results)
    x = _scatter(_run(progs["l0ffn"], maps_l0ffn(x)).results)
    r = _run(progs["l0b"], maps_l0b(x)).results
    x = np.stack([r[2 * b]["xout"] for b in range(B)])
    x = _scatter(_run(progs["l0ffn"], maps_l0ffn(x)).results)
    x = _scatter(_run(progs["l1a"], maps_l1a(x)).results)
    x = _scatter(_run(progs["l1ffn"], maps_l1ffn(x)).results)
    x = _scatter(_run(progs["l1b"], maps_l1b(x)).results)
    x = _scatter(_run(progs["l1ffn"], maps_l1ffn(x)).results)
    return x



def build_fused(dbg_stages=False):
    nc = bass.Bass("TRN2", target_bir_lowering=False)
    def inp(name, shape, dt=F32):
        return nc.dram_tensor(name, list(shape), dt, kind="ExternalInput").ap()
    xownD = inp("xown", [HALF, D])
    xfull0 = inp("xfull0", [S, D])
    ident = inp("ident", [128, 128], BF16)
    identf32 = inp("identf32", [128, 128])
    a_wqkvT = inp("a_wqkvT", [D, 3 * D])
    a_wprojT = inp("a_wprojT", [D, D])
    a_maskA = inp("a_maskA", [128, 128])
    a_maskB = inp("a_maskB", [128, 128])
    a_alibik = inp("a_alibik", [128, 512])
    a_negposq = inp("a_negposq", [1, 16 * HALF])
    f_gate_wT = inp("f_gate_wT", [D, 8])
    f_rmsr = inp("f_rmsr", [128, 8])
    f_w1 = inp("f_w1", [8, D, 4096])
    f_w2 = inp("f_w2", [8, 4096, D])
    f_b1r = inp("f_b1r", [128, 256])
    f_b2 = inp("f_b2", [8, D])
    iotaD = inp("iota128", [128, 1])
    b_waT = inp("b_waT", [D, D])
    b_wbT = inp("b_wbT", [D, D])
    b_wcT = inp("b_wcT", [D, D])
    b_bas = inp("b_bas", [128, 24])
    c_wq = inp("c_wq", [D, D])
    c_wk = inp("c_wk", [D, 256])
    c_wv = inp("c_wv", [D, 256])
    c_wo = inp("c_wo", [D, D])
    c_cosq = inp("c_cosq", [32, HALF])
    c_sinq = inp("c_sinq", [32, HALF])
    c_cosk = inp("c_cosk", [32, S])
    c_sink = inp("c_sink", [32, S])
    c_mks = {m: inp("c_" + m, [128, 128])
             for m in ("mCm2", "mCm1", "mC0", "mCp1", "mG0", "mG1", "mG")}
    g_router_w = inp("g_router_w", [D, 8])
    g_gate_w = inp("g_gate_w", [8, D, 512])
    g_up_w = inp("g_up_w", [8, D, 512])
    g_down_w = inp("g_down_w", [8, 512, D])
    g_s_gate_w = inp("g_s_gate_w", [D, 2048])
    g_s_up_w = inp("g_s_up_w", [D, 2048])
    g_s_down_w = inp("g_s_down_w", [2048, D])
    h_wqkvT = inp("h_wqkvT", [D, 3 * D])
    h_wprojT = inp("h_wprojT", [D, D])
    h_maskA = inp("h_maskA", [128, 128])
    h_maskB = inp("h_maskB", [128, 128])
    h_alibik = inp("h_alibik", [128, 512])
    h_negposq = inp("h_negposq", [1, 16 * HALF])
    xoutD = nc.dram_tensor("xout", [HALF, D], F32, kind="ExternalOutput").ap()
    dbgs = {}
    if dbg_stages:
        stages = (dbg_stages if isinstance(dbg_stages, (set, tuple, list))
                  else (1, 2, 3, 4, 5, 6, 7))
        for i in stages:
            dbgs[i] = nc.dram_tensor(f"dbg_x{i}", [HALF, D], F32,
                                     kind="ExternalOutput").ap()
    xcur = nc.dram_tensor("xcur", [HALF, D], F32).ap()
    xex_in = Tile(nc.dram_tensor("xex_in", [HALF, D], BF16), "xex_in")
    xex_out = Tile(nc.dram_tensor("xex_out", [S, D], BF16), "xex_out")
    ybf = Tile(nc.dram_tensor("ybf", [S, D], BF16), "ybf")
    yrs = Tile(nc.dram_tensor("yrs", [HALF, D], BF16), "yrs")
    xsd = Tile(nc.dram_tensor("xs_sorted", [XS_ROWS, D], BF16), "xsd")
    ysd = Tile(nc.dram_tensor("ys_sorted", [XS_ROWS, D], BF16), "ysd")
    bar_in = Tile(nc.dram_tensor("bar_in", [1, 8], F32), "bar_in")
    bar_out = Tile(nc.dram_tensor("bar_out", [1, 8], F32), "bar_out")
    w1bf = Tile(nc.dram_tensor("w1bf", [8, D, 4096], BF16), "w1bf")
    w2bf = Tile(nc.dram_tensor("w2bf", [8, 4096, D], BF16), "w2bf")

    with ExitStack() as ctx:
        kb = KB(nc, ctx)
        for ex in range(8):
            kb.dma("gw0", w1bf.h[ex, :, :], f_w1[ex, :, :], writes=[w1bf])
            kb.dma("gw1", w2bf.h[ex, :, :], f_w2[ex, :, :], writes=[w2bf])

        def scoped(fn):
            with ExitStack() as bctx:
                kb._cur = bctx
                fn(bctx)
                kb.flush()
                kb.phase_barrier()
            kb._cur = None

        def dump(i):
            if i in dbgs:
                kb.phase_barrier()
                kb.dma("gd", dbgs[i], xcur[0:HALF, :])
                kb.phase_barrier()

        ccn = [0]

        def exchange():
            kb.phase_barrier()
            kb.dma("gd", xex_in.h[:], xcur[0:HALF, :], writes=[xex_in])
            qcc = f"cc{ccn[0]}"
            qcb = f"cb{ccn[0]}"
            ccn[0] += 1
            kb.op(qcc, lambda e: e.collective_compute(
                "AllGather", ALU.bypass,
                replica_groups=[[0, 1], [2, 3], [4, 5], [6, 7]],
                ins=[xex_in.h[:]], outs=[xex_out.h[:]]),
                reads=[xex_in], writes=[xex_out])
            kb.op(qcb, lambda e: e.collective_compute(
                "AllReduce", ALU.add,
                replica_groups=[[0, 1], [2, 3], [4, 5], [6, 7]],
                ins=[bar_in.h[:]], outs=[bar_out.h[:]]),
                reads=[bar_in, xex_out], writes=[bar_out])
            kb.phase_barrier()

        xg = xex_out.h[:]
        scoped(lambda b: emit_l0a_body(
            kb, b, xownD, xfull0, a_wqkvT, a_wprojT, ident, a_maskA,
            a_maskB, a_alibik, a_negposq, xcur, None, "l0a", None))
        dump(1)
        scoped(lambda b: emit_l0ffn_sparse_body(
            kb, b, xcur, f_gate_wT, f_rmsr, f_w1, f_w2, f_b1r, f_b2,
            ident, identf32, iotaD, xcur, xsd, ysd, wbf=(w1bf, w2bf)))
        dump(2)
        exchange()
        scoped(lambda b: emit_l0b_body(
            kb, b, xg, b_waT, b_wbT, b_wcT, b_bas, ident, xcur,
            xmap="gather",
            fused={"ybf": ybf, "yrs": yrs, "xcur": xcur, "ccq": "cc3",
                   "cbq": "cb3", "bar": (bar_in, bar_out)}))
        dump(3)
        scoped(lambda b: emit_l0ffn_sparse_body(
            kb, b, xcur, f_gate_wT, f_rmsr, f_w1, f_w2, f_b1r, f_b2,
            ident, identf32, iotaD, xcur, xsd, ysd, wbf=(w1bf, w2bf)))
        dump(4)
        exchange()
        scoped(lambda b: emit_l1a_body(
            kb, b, xcur, xg, c_wq, c_wk, c_wv, c_wo, ident,
            c_cosq, c_sinq, c_cosk, c_sink, c_mks, xcur, xmap="gather"))
        dump(5)
        scoped(lambda b: emit_l1ffn_body(
            kb, b, xcur, g_router_w, g_gate_w, g_up_w, g_down_w,
            g_s_gate_w, g_s_up_w, g_s_down_w, ident, identf32, xcur))
        dump(6)
        exchange()
        scoped(lambda b: emit_l0a_body(
            kb, b, xcur, xg, h_wqkvT, h_wprojT, ident, h_maskA,
            h_maskB, h_alibik, h_negposq, xcur, None, "l1b", "gather"))
        dump(7)
        scoped(lambda b: emit_l1ffn_body(
            kb, b, xcur, g_router_w, g_gate_w, g_up_w, g_down_w,
            g_s_gate_w, g_s_up_w, g_s_down_w, ident, identf32, xoutD))
        kb.finish("gd")
    return nc


def fused_in_maps(inp):
    C = CONSTS
    cosT, sinT = rope_tables()
    x = np.ascontiguousarray(np.asarray(inp["L0_attn_a_x"], np.float32))
    wqkvT_l0a = np.ascontiguousarray(np.asarray(inp["L0_attn_a_W_attn"]).T)
    wprojT_l0a = np.ascontiguousarray(np.asarray(inp["L0_attn_a_W_proj"]).T)
    gate_wT = np.ascontiguousarray(np.asarray(inp["L0_ffn_gate_w"]).T)
    rmsr = np.ascontiguousarray(
        np.asarray(inp["L0_ffn_rms_w"]).reshape(8, 128).T)
    b1r = np.ascontiguousarray(
        np.asarray(inp["L0_ffn_b1"]).reshape(8, 32, 128)
        .transpose(2, 0, 1).reshape(128, 256))
    bas_l0b = np.ascontiguousarray(np.concatenate(
        [np.asarray(inp["L0_attn_b_ba"]).reshape(8, 128).T,
         np.asarray(inp["L0_attn_b_bb"]).reshape(8, 128).T,
         np.asarray(inp["L0_attn_b_bc"]).reshape(8, 128).T], axis=1))
    wqkvT_l1b = np.ascontiguousarray(np.concatenate(
        [np.asarray(inp["L1_attn_b_Wq"]), np.asarray(inp["L1_attn_b_Wk"]),
         np.asarray(inp["L1_attn_b_Wv"])], axis=1))
    kp = np.arange(128)[:, None]; qp = np.arange(128)[None, :]
    tri50 = np.where(kp > qp, -50.0, 0.0).astype(np.float32)
    full50 = np.full((128, 128), -50.0, np.float32)
    npq0 = np.zeros((1, 16 * HALF), np.float32)
    alz = np.zeros((128, 512), np.float32)
    maps = []
    for core in range(NCORES):
        b, hf = core // 2, core % 2
        rows = own_rows(hf)
        m = {
            "xown": np.ascontiguousarray(x[b][rows]),
            "xfull0": np.ascontiguousarray(x[b]),
            "ident": C["ident"], "identf32": C["identf32"],
            "a_wqkvT": wqkvT_l0a, "a_wprojT": wprojT_l0a,
            "a_maskA": C["tri_causal"] if hf == 0 else C["zeros128"],
            "a_maskB": C["full_neg"] if hf == 0 else C["tri_causal"],
            "a_alibik": C["alibik"], "a_negposq": C[f"negposq_{hf}"],
            "f_gate_wT": gate_wT, "f_rmsr": rmsr,
            "f_w1": np.asarray(inp["L0_ffn_W1"]),
            "f_w2": np.asarray(inp["L0_ffn_W2"]),
            "f_b1r": b1r, "f_b2": np.asarray(inp["L0_ffn_b2"]),
            "iota128": np.arange(128, dtype=np.float32).reshape(128, 1),
            "b_waT": np.ascontiguousarray(np.asarray(inp["L0_attn_b_Wa"]).T),
            "b_wbT": np.ascontiguousarray(np.asarray(inp["L0_attn_b_Wb"]).T),
            "b_wcT": np.ascontiguousarray(np.asarray(inp["L0_attn_b_Wc"]).T),
            "b_bas": bas_l0b,
            "c_wq": np.asarray(inp["L1_attn_a_Wq"]),
            "c_wk": np.asarray(inp["L1_attn_a_Wk"]),
            "c_wv": np.asarray(inp["L1_attn_a_Wv"]),
            "c_wo": np.asarray(inp["L1_attn_a_Wo"]),
            "c_cosq": np.ascontiguousarray(cosT[:, rows]),
            "c_sinq": np.ascontiguousarray(sinT[:, rows]),
            "c_cosk": cosT, "c_sink": sinT,
            "g_router_w": np.asarray(inp["L1_ffn_router_w"]),
            "g_gate_w": np.asarray(inp["L1_ffn_gate_w"]),
            "g_up_w": np.asarray(inp["L1_ffn_up_w"]),
            "g_down_w": np.asarray(inp["L1_ffn_down_w"]),
            "g_s_gate_w": np.asarray(inp["L1_ffn_s_gate_w"]),
            "g_s_up_w": np.asarray(inp["L1_ffn_s_up_w"]),
            "g_s_down_w": np.asarray(inp["L1_ffn_s_down_w"]),
            "h_wqkvT": wqkvT_l1b,
            "h_wprojT": np.ascontiguousarray(
                np.asarray(inp["L1_attn_b_Wr"])),
            "h_maskA": tri50 if hf == 0 else C["zeros128"],
            "h_maskB": full50 if hf == 0 else tri50,
            "h_alibik": alz, "h_negposq": npq0,
        }
        for mk, v in l1a_masks(hf).items():
            m["c_" + mk] = v
        maps.append(m)
    return maps


def kernel_fused(**inp):
    assert np.asarray(inp["L1_ffn_router_bias"]).shape == ()
    nc = build_fused()
    maps = fused_in_maps(inp)
    res = run_bass_kernel_spmd(nc, maps, core_ids=list(range(NCORES)))
    out = np.empty((B, S, D), np.float32)
    for core in range(NCORES):
        b, hf = core // 2, core % 2
        out[b][own_rows(hf)] = res.results[core]["xout"]
    return out



NSLOT_E = 3
XS_ROWS = 4096 + 512 + 256

def emit_l0ffn_sparse_body(kb, ctx, xown, gate_wT, rmsr, w1D, w2D, b1rD,
                           b2D, identD, identf32D, iotaD, xoutD, xsd, ysd,
                           wbf=None):
    nc = kb.nc
    NF = 32
    ident = kb.sb("sb_ident", (128, 128), BF16)
    idf32 = kb.sb("sb_idf32", (128, 128))
    rms = kb.sb("sb_rms", (128, 8))
    b1r = kb.sb("sb_b1r", (128, 256))
    gwr = kb.sb("sb_gwr", (128, 8 * NDC))
    rcps = kb.sb("rcps", (128, NTOW))
    dest32 = kb.sb("dest32", (128, NTOW), mybir.dt.uint32)
    inadr = kb.sb("inadr", (1, 8 * NSLOT_E))
    outadr = kb.sb("outadr", (1, 8 * NSLOT_E))
    inidx = kb.sb("inidx", (128, 8 * NSLOT_E), mybir.dt.uint32)
    outidx = kb.sb("outidx", (128, 8 * NSLOT_E), mybir.dt.uint32)
    kb.dma("sd", ident.ap, identD, writes=[ident])
    kb.dma("sd", idf32.ap, identf32D, writes=[idf32])
    kb.dma("sd", rms.ap, rmsr, writes=[rms])
    kb.dma("sd", b1r.ap, b1rD, writes=[b1r])
    iota = kb.sb("sb_iota", (128, 1))
    kb.dma("sd", iota.ap, iotaD, writes=[iota])

    EC = ctx.enter_context
    pst = [Tile(EC(_psum_r(nc, kb.uname("pst"), [128, 1024], BF16)))
           for _ in range(2)]
    ps32 = [Tile(EC(_psum_r(nc, kb.uname("ps32"), [128, 512], F32)))
            for _ in range(2)]
    psR = Tile(EC(_psum_r(nc, kb.uname("psR"), [128, 512], F32)))
    psM = Tile(EC(_psum_r(nc, kb.uname("psM"), [128, 512], F32)))
    psD2 = [Tile(EC(_psum_r(nc, kb.uname("psD2"), [128, 512], F32)))
            for _ in range(2)]

    with ExitStack() as ph:
        E = ph.enter_context
        xo = [kb.sb(f"xo{i}", (128, D), ctx=ph) for i in range(2)]
        xnb = [kb.sb(f"xnb{i}", (128, D), BF16, ctx=ph) for i in range(2)]
        xn32 = [kb.sb(f"xn32{i}", (128, D), ctx=ph) for i in range(2)]
        xf32 = [kb.sb(f"xf32{i}", (128, D), ctx=ph) for i in range(2)]
        lmax = kb.sb("lmax", (128, 1), ctx=ph)
        nlmax = kb.sb("nlmax", (128, 1), ctx=ph)
        expl = kb.sb("expl", (128, 8), ctx=ph)
        lsum = kb.sb("lsum", (128, 1), ctx=ph)
        msk = kb.sb("msk", (128, 8), BF16, ctx=ph)
        mfm = kb.sb("mfm", (8, HALF), BF16, ctx=ph)
        cum = kb.sb("cum", (8, HALF), ctx=ph)
        cnt = kb.sb("cnt", (8, 1), ctx=ph)
        cntT = kb.sb("cntT", (1, 8), ctx=ph)
        cI = kb.sb("cI", (1, 8), mybir.dt.int32, ctx=ph)
        padT = kb.sb("padT", (1, 8), ctx=ph)
        incl = kb.sb("incl", (1, 8), ctx=ph)
        poffT = kb.sb("poffT", (1, 8), ctx=ph)
        poff = kb.sb("poff", (8, 1), ctx=ph)
        ones1x8 = kb.sb("ones1x8", (1, 8), BF16, ctx=ph)
        ones8 = kb.sb("ones8", (8, 1), BF16, ctx=ph)
        dtmp = kb.sb("dtmp", (8, 512), ctx=ph)
        dfm = kb.sb("dfm", (1, HALF), ctx=ph)
        tvalid = kb.sb("tvalid", (1, 8), ctx=ph)
        trow = kb.sb("trow", (1, 8), ctx=ph)
        onesfm = kb.sb("onesfm", (8, HALF), BF16, ctx=ph)
        ones8f = kb.sb("ones8f", (8, 1), ctx=ph)
        lns = make_ln_scratch(kb, "ffs", None, extra=True, ctx=ph)
        kb.op("dve", lambda e: e.memset(ones1x8.ap, 1.0), writes=[ones1x8])
        kb.op("dve", lambda e: e.memset(ones8.ap, 1.0), writes=[ones8])
        kb.op("dve", lambda e: e.memset(onesfm.ap, 1.0), writes=[onesfm])
        kb.op("dve", lambda e: e.memset(ones8f.ap, 1.0), writes=[ones8f])
        for c in range(NDC):
            kb.dma("sd", gwr.h[0:128, 8 * c:8 * c + 8],
                   gate_wT[c * 128:(c + 1) * 128, :], writes=[gwr])
        for c in range(NDC):
            kb.op("dve", lambda e, c=c: e.tensor_scalar(
                gwr.h[0:128, 8 * c:8 * c + 8],
                gwr.h[0:128, 8 * c:8 * c + 8],
                rms.h[0:128, c:c + 1], None, op0=ALU.mult),
                reads=[gwr, rms], writes=[gwr])
        for j in range(NTOW):
            xt = xo[j % 2]
            kb.dma("sd", xt.ap, xown[j * 128:(j + 1) * 128, :], writes=[xt])
            xb_ = xnb[j % 2]
            x3 = xn32[j % 2]
            emit_ln_tile(kb, xt, None, None, lns,
                         extra_rms=(xb_.ap, xb_, x3.ap, x3), skip_plain=True)
            xf = xf32[j % 2]
            for c in range(NDC):
                kb.op("pe", lambda e, x3=x3, c=c: e.transpose(
                    ps32[c % 2].h[0:128, 0:128],
                    x3.h[0:128, c * 128:(c + 1) * 128], idf32.ap),
                    reads=[x3, idf32], writes=[ps32[c % 2]])
                kb.op("dve", lambda e, xf=xf, c=c: e.tensor_copy(
                    xf.h[0:128, c * 128:(c + 1) * 128],
                    ps32[c % 2].h[0:128, 0:128]),
                    reads=[ps32[c % 2]], writes=[xf])
            for c in range(NDC):
                kb.op("pe", lambda e, xf=xf, c=c: e.matmul(
                    psR.h[0:128, 0:8],
                    xf.h[0:128, c * 128:(c + 1) * 128],
                    gwr.h[0:128, 8 * c:8 * c + 8],
                    start=(c == 0), stop=(c == NDC - 1)),
                    reads=[xf, gwr], writes=[psR], inc=(c == NDC - 1))
            kb.op("dve", lambda e: e.tensor_reduce(
                lmax.ap, psR.h[0:128, 0:8], axis=AX.X, op=ALU.max),
                reads=[psR], writes=[lmax])
            kb.op("dve", lambda e: e.tensor_scalar(
                nlmax.ap, lmax.ap, -1.0, None, op0=ALU.mult),
                reads=[lmax], writes=[nlmax])
            kb.op("act", lambda e: e.activation(
                expl.ap, psR.h[0:128, 0:8], AF.Exp, bias=nlmax.ap,
                accum_out=lsum.ap), reads=[psR, nlmax],
                writes=[expl, lsum])
            kb.op("dve", lambda e, j=j: e.reciprocal(
                rcps.h[0:128, j:j + 1], lsum.ap), reads=[lsum],
                writes=[rcps])
            kb.op("dve", lambda e: e.tensor_scalar(
                msk.ap, psR.h[0:128, 0:8], lmax.ap, None, op0=ALU.is_ge),
                reads=[psR, lmax], writes=[msk])
            kb.op("pe", lambda e, j=j: e.transpose(
                pst[0].h[0:8, 0:128], msk.ap, ident.ap),
                reads=[msk, ident], writes=[pst[0]])
            kb.op("dve", lambda e, j=j: e.tensor_copy(
                mfm.h[0:8, j * 128:(j + 1) * 128], pst[0].h[0:8, 0:128]),
                reads=[pst[0]], writes=[mfm])
        kb.op("dve", lambda e: e.tensor_tensor_scan(
            cum.ap, onesfm.ap, mfm.ap, 0.0, op0=ALU.mult, op1=ALU.add),
            reads=[onesfm, mfm], writes=[cum])
        kb.op("dve", lambda e: e.tensor_copy(
            cnt.ap, cum.h[0:8, HALF - 1:HALF]), reads=[cum], writes=[cnt])
        kb.op("pe", lambda e: e.transpose(
            psM.h[0:1, 0:8], cnt.ap, idf32.h[0:8, 0:8]),
            reads=[cnt, idf32], writes=[psM])
        kb.op("dve", lambda e: e.tensor_copy(cntT.ap, psM.h[0:1, 0:8]),
              reads=[psM], writes=[cntT])
        kb.op("dve", lambda e: e.tensor_scalar(
            cI.ap, cntT.ap, 255.0, None, op0=ALU.add),
            reads=[cntT], writes=[cI])
        kb.op("dve", lambda e: e.tensor_scalar(
            cI.ap, cI.ap, 8, None, op0=ALU.arith_shift_right),
            reads=[cI], writes=[cI])
        kb.op("dve", lambda e: e.tensor_scalar(
            cI.ap, cI.ap, 8, None, op0=ALU.arith_shift_left),
            reads=[cI], writes=[cI])
        kb.op("dve", lambda e: e.tensor_copy(padT.ap, cI.ap),
              reads=[cI], writes=[padT])
        kb.op("dve", lambda e: e.tensor_tensor_scan(
            incl.ap, ones1x8.ap, padT.ap, 0.0, op0=ALU.mult, op1=ALU.add),
            reads=[ones1x8, padT], writes=[incl])
        kb.op("dve", lambda e: e.tensor_tensor(
            poffT.ap, incl.ap, padT.ap, op=ALU.subtract),
            reads=[incl, padT], writes=[poffT])
        kb.op("pe", lambda e: e.transpose(
            psM.h[0:8, 0:1], poffT.ap, idf32.h[0:1, 0:1]),
            reads=[poffT, idf32], writes=[psM])
        kb.op("dve", lambda e: e.tensor_copy(poff.ap, psM.h[0:8, 0:1]),
              reads=[psM], writes=[poff])
        for g2 in range(4):
            kb.op("dve", lambda e, g2=g2: e.tensor_scalar(
                dtmp.ap, cum.h[0:8, g2 * 512:(g2 + 1) * 512],
                poff.ap, -1.0, op0=ALU.add, op1=ALU.add),
                reads=[cum, poff], writes=[dtmp])
            kb.op("dve", lambda e, g2=g2: e.tensor_tensor(
                dtmp.ap, dtmp.ap, mfm.h[0:8, g2 * 512:(g2 + 1) * 512],
                op=ALU.mult), reads=[dtmp, mfm], writes=[dtmp])
            kb.op("pe", lambda e, g2=g2: e.matmul(
                psM.h[0:1, 0:512], ones8f.ap, dtmp.ap,
                start=True, stop=True), reads=[ones8f, dtmp],
                writes=[psM])
            kb.op("dve", lambda e, g2=g2: e.tensor_copy(
                dfm.h[0:1, g2 * 512:(g2 + 1) * 512], psM.h[0:1, 0:512]),
                reads=[psM], writes=[dfm])
        for j in range(NTOW):
            kb.op("pe", lambda e, j=j: e.transpose(
                psM.h[0:128, 0:1], dfm.h[0:1, j * 128:(j + 1) * 128],
                idf32.h[0:1, 0:1]), reads=[dfm, idf32], writes=[psM])
            kb.op("dve", lambda e, j=j: e.tensor_copy(
                dest32.h[0:128, j:j + 1], psM.h[0:128, 0:1]),
                reads=[psM], writes=[dest32])
        for k in range(NSLOT_E):
            kb.op("dve", lambda e, k=k: e.tensor_scalar(
                trow.ap, poffT.ap, float(256 * k), None,
                op0=ALU.add), reads=[poffT], writes=[trow])
            kb.op("dve", lambda e, k=k: e.tensor_scalar(
                tvalid.ap, padT.ap, float(256 * k), None, op0=ALU.is_gt),
                reads=[padT], writes=[tvalid])
            kb.op("dve", lambda e, k=k: e.tensor_copy(
                inadr.h[0:1, 8 * k:8 * k + 8], trow.ap),
                reads=[trow], writes=[inadr])
            kb.op("dve", lambda e: e.tensor_tensor(
                trow.ap, trow.ap, tvalid.ap, op=ALU.mult),
                reads=[trow, tvalid], writes=[trow])
            kb.op("dve", lambda e: e.tensor_scalar(
                tvalid.ap, tvalid.ap, float(-(XS_ROWS - 256)),
                float(XS_ROWS - 256),
                op0=ALU.mult, op1=ALU.add), reads=[tvalid],
                writes=[tvalid])
            kb.op("dve", lambda e: e.tensor_tensor(
                trow.ap, trow.ap, tvalid.ap, op=ALU.add),
                reads=[trow, tvalid], writes=[trow])
            kb.op("dve", lambda e, k=k: e.tensor_copy(
                outadr.h[0:1, 8 * k:8 * k + 8], trow.ap),
                reads=[trow], writes=[outadr])
        ones1f = kb.sb("ones1f", (1, 128), ctx=ph)
        idxf = kb.sb("idxf", (128, 8 * NSLOT_E), ctx=ph)
        kb.op("dve", lambda e: e.memset(ones1f.ap, 1.0), writes=[ones1f])
        for (srcT, dstI) in ((inadr, inidx), (outadr, outidx)):
            kb.op("pe", lambda e, srcT=srcT: e.matmul(
                psM.h[0:128, 0:24], ones1f.ap, srcT.ap,
                start=True, stop=True),
                reads=[ones1f, srcT], writes=[psM])
            kb.op("dve", lambda e: e.tensor_scalar(
                idxf.ap, psM.h[0:128, 0:24], iota.ap, None, op0=ALU.add),
                reads=[psM, iota], writes=[idxf])
            kb.op("dve", lambda e, dstI=dstI: e.tensor_copy(
                dstI.ap, idxf.ap), reads=[idxf], writes=[dstI])

        xb2 = [kb.sb(f"xb2{i}", (128, D), BF16, ctx=ph) for i in range(2)]
        xo2 = [kb.sb(f"xo2{i}", (128, D), ctx=ph) for i in range(2)]
        lns = make_ln_scratch(kb, "ffs2", None, extra=True, ctx=ph)
        for j in range(NTOW):
            xt = xo2[j % 2]
            kb.dma("sd", xt.ap, xown[j * 128:(j + 1) * 128, :], writes=[xt])
            xb_ = xb2[j % 2]
            emit_ln_tile(kb, xt, None, None, lns,
                         extra_rms=(xb_.ap, xb_, None, None),
                         skip_plain=True)
            kb.op("gd", lambda e, j=j, xb_=xb_: e.indirect_dma_start(
                out=xsd.h[0:XS_ROWS, :],
                out_offset=bass.IndirectOffsetOnAxis(
                    ap=dest32.h[0:128, j:j + 1], axis=0),
                in_=xb_.ap, in_offset=None),
                reads=[xb_, dest32], writes=[xsd])
    kb.phase_barrier()

    with ExitStack() as ph:
        psU = ps32
        psD = [psR, psM] + psD2
        w1s = [kb.sb(f"w1s{c}", (128, 4096), BF16, ctx=ph)
               for c in range(NDC)]
        w2s = kb.sb("w2s", (128, NF * 1024), BF16, ctx=ph)
        hfm = kb.sb("hfm", (128, NF * 256), BF16, ctx=ph)
        b2s = kb.sb("b2s", (1, D), BF16, ctx=ph)
        ones1 = kb.sb("ones1r", (1, 128), BF16, ctx=ph)
        xst = [kb.sb(f"xst{i}", (128, D), BF16, ctx=ph) for i in range(2)]
        xsf = [kb.sb(f"xsf{c}", (128, 256), BF16, ctx=ph)
               for c in range(NDC)]
        ysb = [kb.sb(f"ysb{i}", (128, D), BF16, ctx=ph) for i in range(2)]
        kb.op("dve", lambda e: e.memset(ones1.ap, 1.0), writes=[ones1])
        for ex in range(8):
            for c in range(NDC):
                if wbf is None:
                    kb.dma("gd", w1s[c].ap,
                           w1D[ex, c * 128:(c + 1) * 128, :],
                           writes=[w1s[c]])
                else:
                    kb.dma("sd", w1s[c].ap,
                           wbf[0].h[ex, c * 128:(c + 1) * 128, :],
                           reads=[wbf[0]], writes=[w1s[c]])
            for fc in range(NF):
                if wbf is None:
                    kb.dma("gd", w2s.h[0:128, fc * 1024:(fc + 1) * 1024],
                           w2D[ex, fc * 128:(fc + 1) * 128, :],
                           writes=[w2s])
                else:
                    kb.dma("sd", w2s.h[0:128, fc * 1024:(fc + 1) * 1024],
                           wbf[1].h[ex, fc * 128:(fc + 1) * 128, :],
                           reads=[wbf[1]], writes=[w2s])
            kb.dma("gd", b2s.ap, b2D[ex:ex + 1, :], writes=[b2s])
            for k in range(NSLOT_E):
                slot = 8 * k + ex
                for half in range(2):
                    xt = xst[half]
                    kb.op("gd", lambda e, slot=slot, half=half, xt=xt:
                          e.indirect_dma_start(
                              out=xt.ap, out_offset=None,
                              in_=xsd.h[0:XS_ROWS, :],
                              in_offset=bass.IndirectOffsetOnAxis(
                                  ap=inidx.h[0:128, slot:slot + 1], axis=0),
                              element_offset=half * 128 * D),
                          reads=[inidx, xsd], writes=[xt])
                for half in range(2):
                    for c in range(NDC):
                        p = pst[(half * NDC + c) % 2]
                        kb.op("pe", lambda e, half=half, c=c, p=p:
                              e.transpose(p.h[0:128, 0:128],
                                          xst[half].h[0:128,
                                                      c * 128:(c + 1) * 128],
                                          ident.ap),
                              reads=[xst[half], ident], writes=[p])
                        kb.op("dve", lambda e, half=half, c=c, p=p:
                              e.tensor_scalar(
                                  xsf[c].h[0:128,
                                           half * 128:half * 128 + 128],
                                  p.h[0:128, 0:128],
                                  rms.h[0:128, c:c + 1], None,
                                  op0=ALU.mult),
                              reads=[p, rms], writes=[xsf[c]])
                for fc in range(NF):
                    pu = psU[fc % 2]
                    for c in range(NDC):
                        kb.op("pe", lambda e, c=c, fc=fc, pu=pu: e.matmul(
                            pu.h[0:128, 0:256],
                            w1s[c].h[0:128, fc * 128:(fc + 1) * 128],
                            xsf[c].h[0:128, 0:256],
                            start=(c == 0), stop=(c == NDC - 1)),
                            reads=[w1s[c], xsf[c]], writes=[pu],
                            inc=(c == NDC - 1))
                    kb.op("act", lambda e, pu=pu, fc=fc, ex=ex:
                          e.activation(
                              hfm.h[0:128, fc * 256:(fc + 1) * 256],
                              pu.h[0:128, 0:256], AF.Silu,
                              bias=b1r.h[0:128, ex * 32 + fc:
                                         ex * 32 + fc + 1]),
                          reads=[pu, b1r], writes=[hfm])
                for half in range(2):
                    for dh in range(2):
                        pd = psD[half * 2 + dh]
                        kb.op("pe", lambda e, pd=pd, dh=dh: e.matmul(
                            pd.h[0:128, 0:512], ones1.ap,
                            b2s.h[0:1, dh * 512:dh * 512 + 512],
                            start=True, stop=False),
                            reads=[ones1, b2s], writes=[pd], inc=False)
                        for fc in range(NF):
                            kb.op("pe", lambda e, pd=pd, fc=fc, half=half,
                                  dh=dh: e.matmul(
                                      pd.h[0:128, 0:512],
                                      hfm.h[0:128, fc * 256 + half * 128:
                                            fc * 256 + half * 128 + 128],
                                      w2s.h[0:128, fc * 1024 + dh * 512:
                                            fc * 1024 + dh * 512 + 512],
                                      start=False, stop=(fc == NF - 1)),
                                  reads=[hfm, w2s], writes=[pd],
                                  inc=(fc == NF - 1))
                    yb = ysb[half]
                    for dh in range(2):
                        pd = psD[half * 2 + dh]
                        kb.op("dve", lambda e, pd=pd, yb=yb, dh=dh:
                              e.tensor_copy(
                                  yb.h[0:128, dh * 512:dh * 512 + 512],
                                  pd.h[0:128, 0:512]),
                              reads=[pd], writes=[yb])
                    kb.op("gd", lambda e, slot=slot, half=half, yb=yb:
                          e.indirect_dma_start(
                              out=ysd.h[0:XS_ROWS, :],
                              out_offset=bass.IndirectOffsetOnAxis(
                                  ap=outidx.h[0:128, slot:slot + 1],
                                  axis=0),
                              in_=yb.ap, in_offset=None,
                              element_offset=half * 128 * D),
                          reads=[outidx, yb, ysd], writes=[ysd])

        xr = [kb.sb(f"xr{i}", (128, D), ctx=ph) for i in range(2)]
        yg = [kb.sb(f"yg{i}", (128, D), BF16, ctx=ph) for i in range(2)]
        yout = [kb.sb(f"yo{i}", (128, D), ctx=ph) for i in range(2)]
        for j in range(NTOW):
            xt = xr[j % 2]
            yt = yg[j % 2]
            kb.dma("sd", xt.ap, xown[j * 128:(j + 1) * 128, :], writes=[xt])
            kb.op("gd", lambda e, j=j, yt=yt: e.indirect_dma_start(
                out=yt.ap, out_offset=None,
                in_=ysd.h[0:XS_ROWS, :],
                in_offset=bass.IndirectOffsetOnAxis(
                    ap=dest32.h[0:128, j:j + 1], axis=0)),
                reads=[ysd, dest32], writes=[yt])
            yo = yout[j % 2]
            kb.op("dve", lambda e, xt=xt, yt=yt, yo=yo, j=j:
                  e.scalar_tensor_tensor(
                      yo.ap, yt.ap, rcps.h[0:128, j:j + 1], xt.ap,
                      op0=ALU.mult, op1=ALU.add),
                  reads=[xt, yt, rcps], writes=[yo])
            kb.dma("gd", xoutD[j * 128:(j + 1) * 128, :], yo.ap, reads=[yo])
        kb.phase_barrier()


def _reg_dma_load(eng, r, dst, src_h, adr_h, slot, extra):
    eng.reg_load(r, adr_h[0:1, slot:slot + 1])
    if extra:
        eng.reg_add(r, r, extra)
    return eng.dma_start(
        out=dst.ap,
        in_=bass.AP(src_h, r, [[1024, 128], [1024, 1], [1, 1024]]))


def _reg_dma_store(eng, r, src, dst_h, adr_h, slot, extra):
    eng.reg_load(r, adr_h[0:1, slot:slot + 1])
    if extra:
        eng.reg_add(r, r, extra)
    return eng.dma_start(
        out=bass.AP(dst_h, r, [[1024, 128], [1024, 1], [1, 1024]]),
        in_=src.ap)


def build_l0ffn_sparse():
    nc = bass.Bass("TRN2", target_bir_lowering=False)
    xown = nc.dram_tensor("xown", [HALF, D], F32, kind="ExternalInput").ap()
    gate_wT = nc.dram_tensor("gate_wT", [D, 8], F32, kind="ExternalInput").ap()
    rmsr = nc.dram_tensor("rmsr", [128, 8], F32, kind="ExternalInput").ap()
    w1D = nc.dram_tensor("w1", [8, D, 4096], F32, kind="ExternalInput").ap()
    w2D = nc.dram_tensor("w2", [8, 4096, D], F32, kind="ExternalInput").ap()
    b1rD = nc.dram_tensor("b1r", [128, 256], F32, kind="ExternalInput").ap()
    b2D = nc.dram_tensor("b2", [8, D], F32, kind="ExternalInput").ap()
    identD = nc.dram_tensor("ident", [128, 128], BF16, kind="ExternalInput").ap()
    identf32D = nc.dram_tensor("identf32", [128, 128], F32,
                               kind="ExternalInput").ap()
    iotaD = nc.dram_tensor("iota128", [128, 1], F32,
                           kind="ExternalInput").ap()
    xoutD = nc.dram_tensor("xout", [HALF, D], F32, kind="ExternalOutput").ap()
    xsd = Tile(nc.dram_tensor("xs_sorted", [XS_ROWS, D], BF16), "xsd")
    ysd = Tile(nc.dram_tensor("ys_sorted", [XS_ROWS, D], BF16), "ysd")
    w1bf = Tile(nc.dram_tensor("w1bf", [8, D, 4096], BF16), "w1bf")
    w2bf = Tile(nc.dram_tensor("w2bf", [8, 4096, D], BF16), "w2bf")
    with ExitStack() as ctx:
        kb = KB(nc, ctx)
        for ex in range(8):
            kb.dma("gw0", w1bf.h[ex, :, :], w1D[ex, :, :], writes=[w1bf])
            kb.dma("gw1", w2bf.h[ex, :, :], w2D[ex, :, :], writes=[w2bf])
        emit_l0ffn_sparse_body(kb, ctx, xown, gate_wT, rmsr, w1D, w2D, b1rD,
                               b2D, identD, identf32D, iotaD, xoutD, xsd,
                               ysd, wbf=(w1bf, w2bf))
        kb.finish("gd")
    return nc
